# version 16
# baseline (speedup 1.0000x reference)
import contextlib
import numpy as np
import ml_dtypes

import concourse.bass as bass
import concourse.mybir as mybir
from concourse.bass_utils import run_bass_kernel_spmd

F32 = mybir.dt.float32
BF16 = mybir.dt.bfloat16
AF = mybir.ActivationFunctionType
ALU = mybir.AluOpType
AX = mybir.AxisListType
BF = ml_dtypes.bfloat16

D = 1024
S = 4096
NSEQ = 4
NCORE = 8
TOK = 2048
HD = 64
MYH = 8
NEG_SHIFT = -24.0

ENGS = ("sync", "tensor", "vector", "scalar", "gpsimd")


class Prog:

    def __init__(self, nc, es):
        self.nc = nc
        self.es = es
        self.ops = {e: [] for e in ENGS}
        self.cnt = {}

    def sem(self, name):
        h = self.es.enter_context(self.nc.semaphore(name))
        self.cnt[name] = 0
        return h

    def add(self, eng, fn, waits=(), inc=None, inc_by=1):
        waits = tuple((s, v) for (s, v) in waits if v > 0)
        for s, v in waits:
            assert v <= self.cnt[s.name], f"wait on future value of {s.name}"
        if inc is not None:
            self.cnt[inc.name] += inc_by
        self.ops[eng].append((fn, waits, inc, inc_by))
        return self.cnt[inc.name] if inc is not None else None

    def now(self, semh):
        return self.cnt[semh.name]

    def barrier(self, semh):
        for e in ENGS:
            self.add(e, lambda eng: eng.nop(), inc=semh)
        tgt = self.cnt[semh.name]
        for e in ENGS:
            self.add(e, lambda eng: eng.nop(), waits=((semh, tgt),))
        return tgt

    def emit(self):
        with self.nc.Block() as block:
            def make(eng_name):
                def body(eng):
                    for fn, waits, inc, inc_by in self.ops[eng_name]:
                        for (s, v) in waits:
                            eng.wait_ge(s, v)
                        inst = fn(eng)
                        if inc is not None:
                            if inst is None:
                                inst = eng.nop()
                            inst.then_inc(inc, inc_by)
                return body
            block.sync(make("sync"))
            block.tensor(make("tensor"))
            block.vector(make("vector"))
            block.scalar(make("scalar"))
            block.gpsimd(make("gpsimd"))


class Launch:

    def __init__(self):
        self.nc = bass.Bass("TRN2", target_bir_lowering=False)
        self.es = contextlib.ExitStack()
        self.p = Prog(self.nc, self.es)
        self.scope = self.es
        self.sfx = ""
        self.guards = {}

    def inp(self, name, shape, dt=F32):
        return self.nc.dram_tensor(name, shape, dt, kind="ExternalInput").ap()

    def out(self, name, shape, dt=F32):
        return self.nc.dram_tensor(name, shape, dt, kind="ExternalOutput").ap()

    def scr(self, name, shape, dt=F32):
        return self.nc.dram_tensor(name, shape, dt, kind="Internal").ap()

    def sb(self, name, shape, dt=F32):
        return self.scope.enter_context(
            self.nc.sbuf_tensor(name + self.sfx, shape, dt)).ap()

    def ps(self, name, shape, dt=F32):
        return self.scope.enter_context(
            self.nc.psum_tensor(name + self.sfx, shape, dt)).ap()

    def sem(self, name):
        return self.p.sem(name + self.sfx)

    def guard(self, key):
        return tuple(self.guards.get(key, ()))

    def set_guard(self, key, sem, val):
        self.guards[key] = [(sem, val)]



def emit_ln_tiles(L, tiles, waits=(), eps_ap=None, gamma_beta=None):
    p, sv, ss = L.p, L.sv, L.ss
    n = tiles[0][0].shape[-1]
    inv = 1.0 / n
    sls = [[sc["s"][:, i:i + 1] for i in range(6)] for (_, _, sc) in tiles]
    for (x_ap, _, sc), sl in zip(tiles, sls):
        p.add("scalar", lambda e, x_ap=x_ap, sc=sc, sl=sl: e.activation(
            out=sc["sq"], in_=x_ap, func=AF.Square, accum_out=sl[0]),
            waits=waits, inc=ss)
    w_ss = p.now(ss)
    for (x_ap, _, _), sl in zip(tiles, sls):
        p.add("vector", lambda e, x_ap=x_ap, sl=sl: e.reduce_sum(
            out=sl[1], in_=x_ap, axis=AX.X), waits=waits, inc=sv)
    w1 = p.now(sv)
    for sl in sls:
        p.add("vector", lambda e, sl=sl: e.tensor_scalar_mul(
            out=sl[2], in0=sl[1], scalar1=inv), waits=((sv, w1),), inc=sv)
    w2 = p.now(sv)
    for sl in sls:
        p.add("vector", lambda e, sl=sl: e.tensor_mul(
            out=sl[3], in0=sl[2], in1=sl[2]), waits=((sv, w2),), inc=sv)
    w3 = p.now(sv)
    for sl in sls:
        p.add("vector", lambda e, sl=sl: e.scalar_tensor_tensor(
            out=sl[4], in0=sl[0], scalar=inv, in1=sl[3],
            op0=ALU.mult, op1=ALU.subtract),
            waits=((sv, w3), (ss, w_ss)), inc=sv)
    w4 = p.now(sv)
    for sl in sls:
        p.add("scalar", lambda e, sl=sl: e.activation(
            out=sl[5], in_=sl[4], func=AF.Sqrt, bias=eps_ap),
            waits=((sv, w4),), inc=ss)
    w_s2 = p.now(ss)
    for sl in sls:
        p.add("vector", lambda e, sl=sl: e.reciprocal(out=sl[4], in_=sl[5]),
              waits=((ss, w_s2),), inc=sv)
    w5 = p.now(sv)
    for (x_ap, xn_out_ap, sc), sl in zip(tiles, sls):
        if gamma_beta is None:
            p.add("vector", lambda e, x_ap=x_ap, o=xn_out_ap, sl=sl:
                  e.tensor_scalar(out=o, in0=x_ap, scalar1=sl[2],
                                  scalar2=sl[4], op0=ALU.subtract,
                                  op1=ALU.mult), waits=((sv, w5),), inc=sv)
        else:
            tmp, gb, bb = gamma_beta
            p.add("vector", lambda e, x_ap=x_ap, sl=sl, tmp=tmp:
                  e.tensor_scalar(out=tmp, in0=x_ap, scalar1=sl[2],
                                  scalar2=sl[4], op0=ALU.subtract,
                                  op1=ALU.mult), waits=((sv, w5),), inc=sv)
    if gamma_beta is not None:
        assert len(tiles) == 1
        w6 = p.now(sv)
        tmp, gb, bb = gamma_beta
        xn_out_ap = tiles[0][1]
        p.add("vector", lambda e: e.tensor_mul(out=tmp, in0=tmp, in1=gb),
              waits=((sv, w6),), inc=sv)
        w7 = p.now(sv)
        p.add("vector", lambda e: e.tensor_add(out=xn_out_ap, in0=tmp, in1=bb),
              waits=((sv, w7),), inc=sv)
    return p.now(sv)



def phase_A(L, d):
    nc, p = L.nc, L.p
    x_full = d["x_full"]
    x_own = d["x_own"]
    w1p = d["w1p"]
    b1p = d["b1p"]
    w0t = d["w0t"]
    wpmy = d["wpmy"]
    idn = d["idn"]
    dmask = d["dmask"]
    s0p = d["s0p"]
    g0 = d["g0"]

    sb, ps = L.sb, L.ps
    xbuf = sb("xbuf", [128, 2 * 4 * D], BF16)
    xn_bf = sb("xn_bf", [128, 4 * D], BF16)
    xnT = sb("xnT", [128, 8 * 512], BF16)
    w1sb = sb("w1sb", [128, 8 * 1536], BF16)
    b1sb = sb("b1sb", [128, 12])
    sgnT = sb("sgnT", [128, 8 * D], BF16)
    wpsb = sb("wpsb", [128, 4 * D], BF16)
    idsb = sb("idsb", [128, 128], BF16)
    msksb = sb("msksb", [128, 4 * 1024], BF16)
    negm = msksb
    qC = sb("qC", [128, 4 * 512], BF16)
    kT = sb("kT", [128, 4 * S], BF16)
    vC = sb("vC", [128, 4 * 512], BF16)
    vtok = sb("vtok", [128, 32 * MYH * 65], BF16)
    oTb = sb("oTb", [128, 4 * 512], BF16)
    ptile = sb("ptile", [128, 2 * 1024], BF16)
    stage = sb("stage", [128, 4 * D], BF16)
    gstage = sb("gstage", [128, 4 * D])
    ones64 = sb("ones64", [1, 64])
    o_s = sb("o_s", [64, 512])
    rd = sb("rd", [1, 512])
    lsq = sb("lsq", [128, D])
    lsm = sb("lsm", [128, 32])
    consts = sb("consts", [128, 4])

    pq = [ps("pq0", [128, 1024]), ps("pq1", [128, 1024])]
    pb = [pq[0][:, 0:512], pq[0][:, 512:1024],
          pq[1][:, 0:512], pq[1][:, 512:1024]]
    pob = [ps("po0", [128, 512]), ps("po1", [128, 512])]
    po = [t[0:65, :] for t in pob]
    ptb = ps("pt0", [128, 2048], BF16)
    pt = [ptb[:, 0:512], ptb[:, 1024:1536]]

    dx = L.sem("dx")
    gx = L.sem("gx")
    st = L.sem("st")
    sv = L.sem("sv")
    ss = L.sem("ss")
    so = L.sem("so")
    L.sv, L.ss = sv, ss

    eps5 = consts[:, 0:1]
    neg24 = consts[:, 1:2]

    def ln_sc(i):
        return {"sq": lsq, "s": lsm[:, 6 * i:6 * i + 6]}

    p.add("vector", lambda e: e.memset(eps5, 1e-5), inc=sv)
    p.add("vector", lambda e: e.memset(neg24, NEG_SHIFT), inc=sv)
    p.add("vector", lambda e: e.memset(ones64[:], 1.0), inc=sv)
    p.add("vector", lambda e: e.memset(vtok[:], 1.0), inc=sv)
    vtok_ready = p.now(sv)
    p.add("sync", lambda e: e.dma_start(out=idsb[:], in_=idn),
          inc=dx, inc_by=16)
    w_id = p.now(dx)
    p.add("sync", lambda e: e.dma_start(
        out=msksb[:].rearrange("p (j n) -> p j n", j=4),
        in_=dmask.rearrange("(j p) n -> p j n", p=128)), inc=dx, inc_by=16)
    p.add("sync", lambda e: e.dma_start(out=b1sb[:], in_=b1p),
          inc=dx, inc_by=16)
    w_small = p.now(dx)
    p.add("gpsimd", lambda e: e.dma_start(
        out=w1sb[:].rearrange("p (c n) -> p c n", c=8),
        in_=w1p.rearrange("(c p) n -> p c n", p=128)), inc=gx, inc_by=16)
    p.add("gpsimd", lambda e: e.dma_start(
        out=sgnT[:].rearrange("p (c n) -> p c n", c=8),
        in_=w0t.rearrange("(c p) n -> p c n", p=128)), inc=gx, inc_by=16)
    p.add("gpsimd", lambda e: e.dma_start(
        out=wpsb[:].rearrange("p (c n) -> p c n", c=4),
        in_=wpmy.rearrange("(c p) n -> p c n", p=128)), inc=gx, inc_by=16)
    w_wdma = p.now(gx)

    for jp in range(4):
        p.add("vector", lambda e, jp=jp: e.tensor_scalar(
            out=negm[:, jp * 1024:(jp + 1) * 1024],
            in0=msksb[:, jp * 1024:(jp + 1) * 1024],
            scalar1=400.0, scalar2=-400.0, op0=ALU.mult, op1=ALU.add),
            waits=((dx, w_small),), inc=sv)
    negm_rdy = p.now(sv)
    p.add("scalar", lambda e: e.activation(out=sgnT[:], in_=sgnT[:],
                                           func=AF.Sign),
          waits=((gx, w_wdma),), inc=ss)
    w_sgn = p.now(ss)

    def chunk_ln_transpose(cbuf, dmawaits):
        tiles = [(cbuf[:, t * D:(t + 1) * D],
                  xn_bf[:, t * D:(t + 1) * D], ln_sc(t))
                 for t in range(4)]
        w_apply = emit_ln_tiles(L, tiles, waits=dmawaits, eps_ap=eps5)
        for ci in range(8):
            bk = ci % 2
            for t in range(4):
                def tfn(e, ci=ci, t=t, bk=bk):
                    return e.transpose(
                        out=pt[bk][:, t * 128:(t + 1) * 128],
                        in_=xn_bf[:, t * D + ci * 128:
                                  t * D + ci * 128 + 128],
                        identity=idsb[:])
                p.add("tensor", tfn,
                      waits=(((sv, w_apply),) + L.guard(("pt", bk))
                             if t == 0 else ()),
                      inc=st if t == 3 else None)
            w_t = p.now(st)

            def cfn(e, ci=ci, bk=bk):
                return e.tensor_copy(out=xnT[:, ci * 512:(ci + 1) * 512],
                                     in_=pt[bk][:])
            p.add("vector", cfn,
                  waits=((st, w_t),) +
                  (L.guard("xnT_w") if ci == 0 else ()), inc=sv)
            L.set_guard(("pt", bk), sv, p.now(sv))
        return p.now(sv)

    save_xnt = d.get("save_xnt")
    load_xnt = d.get("load_xnt")
    for c in range(8):
        par = c % 2
        cbuf = xbuf[:, par * 4 * D:(par + 1) * 4 * D]

        if load_xnt is None:
            def xdma(e, c=c, cbuf=cbuf):
                return e.dma_start(
                    out=cbuf.rearrange("p (t n) -> p t n", t=4),
                    in_=x_full[c * 512:(c + 1) * 512, :].rearrange(
                        "(t p) n -> p t n", p=128))
            p.add("gpsimd", xdma, waits=L.guard(("xbuf", par)),
                  inc=gx, inc_by=16)
            w_x = p.now(gx)

            xnT_done = chunk_ln_transpose(cbuf, ((gx, w_x),))
            L.set_guard(("xbuf", par), sv, xnT_done)
            xn_wait = (sv, xnT_done)
            if save_xnt is not None:
                def xtd(e, c=c):
                    return e.dma_start(out=save_xnt[c], in_=xnT[:])
                p.add("sync", xtd, waits=((sv, xnT_done),),
                      inc=so, inc_by=16)
                L.set_guard("xnT_w", so, p.now(so))
        else:
            def xld(e, c=c):
                return e.dma_start(out=xnT[:], in_=load_xnt[c])
            p.add("gpsimd", xld, waits=L.guard("xnT_rd"), inc=gx, inc_by=16)
            xn_wait = (gx, p.now(gx))

        evac_tgt = {}
        for f1 in range(12):
            bk = f1 % 2
            for ci in range(8):
                w = ()
                if ci == 0:
                    w = (xn_wait, (gx, w_wdma)) + \
                        L.guard(("pb", bk))

                def mfn(e, f1=f1, ci=ci, bk=bk):
                    return e.matmul(
                        out=pb[bk][:],
                        lhsT=w1sb[:, ci * 1536 + f1 * 128:
                                  ci * 1536 + (f1 + 1) * 128],
                        rhs=xnT[:, ci * 512:(ci + 1) * 512],
                        start=(ci == 0), stop=(ci == 7))
                p.add("tensor", mfn, waits=w, inc=st if ci == 7 else None)
            w_mm = p.now(st)

            if f1 < 4:
                edst = qC[:, f1 * 512:(f1 + 1) * 512]
            elif f1 < 8:
                edst = kT[:, (f1 - 4) * S + c * 512:
                          (f1 - 4) * S + (c + 1) * 512]
            else:
                edst = vC[:, (f1 - 8) * 512:(f1 - 7) * 512]

            def efn(e, f1=f1, bk=bk, edst=edst):
                return e.activation(
                    out=edst,
                    in_=pb[bk][:], func=AF.Identity,
                    bias=b1sb[:, f1:f1 + 1], scale=1.0)
            p.add("scalar", efn, waits=((st, w_mm), (dx, w_small)), inc=ss)
            evac_tgt[f1] = p.now(ss)
            L.set_guard(("pb", bk), ss, evac_tgt[f1])
        L.set_guard("xnT_rd", st, p.now(st))

        for hp in range(4):
            for tb in range(4):
                j = 4 * c + tb
                bk = tb % 2

                def tfn(e, hp=hp, tb=tb, bk=bk):
                    return e.transpose(
                        out=pt[bk][:, 0:128],
                        in_=vC[:, hp * 512 + tb * 128:
                                hp * 512 + (tb + 1) * 128],
                        identity=idsb[:])
                p.add("tensor", tfn,
                      waits=((ss, evac_tgt[8 + hp]),) +
                      L.guard(("pt", bk)), inc=st)
                w_t = p.now(st)
                base = (j * MYH + 2 * hp) * 65

                def c1(e, base=base, bk=bk):
                    return e.tensor_copy(out=vtok[:, base:base + 64],
                                         in_=pt[bk][:, 0:64])

                def c2(e, base=base, bk=bk):
                    return e.tensor_copy(
                        out=vtok[:, base + 65:base + 65 + 64],
                        in_=pt[bk][:, 64:128])
                p.add("vector", c1, waits=((st, w_t), (sv, vtok_ready)))
                p.add("vector", c2, inc=sv)
                L.set_guard(("pt", bk), sv, p.now(sv))
        vtok_done = p.now(sv)

        for hp in range(4):
            qA = qC[:, hp * 512:(hp + 1) * 512]
            kA = kT[:, hp * S:(hp + 1) * S]
            njc = 4 * c + 4
            exp_done = {}
            pend = []
            def flush_av(av):
                a1f, a2f, avw = av
                p.add("tensor", a1f, waits=avw)
                p.add("tensor", a2f, inc=st)
            for j in range(njc):
                bA, bB = 2 * (j % 2), 2 * (j % 2) + 1
                w = [(ss, evac_tgt[4 + hp])]
                if j >= 2:
                    w.append((ss, exp_done[j - 2]))
                else:
                    w += list(L.guard(("pb", bA)) + L.guard(("pb", bB)))
                    w.append((ss, p.now(ss)))
                    w.append((sv, p.now(sv)))

                diag = j >= 4 * c

                def s1(e, j=j, qA=qA, kA=kA, bA=bA, diag=diag):
                    return e.matmul(
                        out=pb[bA][:], lhsT=kA[0:64, j * 128:(j + 1) * 128],
                        rhs=qA[0:64, :],
                        start=True, stop=not diag)

                def s2(e, j=j, qA=qA, kA=kA, bB=bB, diag=diag):
                    return e.matmul(
                        out=pb[bB][:],
                        lhsT=kA[64:128, j * 128:(j + 1) * 128],
                        rhs=qA[64:128, :],
                        start=True, stop=not diag)
                p.add("tensor", s1, waits=w)
                p.add("tensor", s2, inc=None if diag else st)
                if diag:
                    nm = negm[:, (j - 4 * c) * 1024:(j - 4 * c) * 1024 + 512]

                    def mA(e, bA=bA, nm=nm):
                        return e.matmul(out=pb[bA][:], lhsT=idsb[:],
                                        rhs=nm, start=False, stop=True)

                    def mB(e, bB=bB, nm=nm):
                        return e.matmul(out=pb[bB][:], lhsT=idsb[:],
                                        rhs=nm, start=False, stop=True)
                    p.add("tensor", mA, waits=((sv, negm_rdy), (dx, w_id)))
                    p.add("tensor", mB, inc=st)
                w_s = p.now(st)
                pAB = ptile[:, (j % 2) * 1024:(j % 2) * 1024 + 1024]
                pA = pAB[:, 0:512]
                pB = pAB[:, 512:1024]

                def e12(e, par=j % 2, pAB=pAB):
                    return e.activation(out=pAB, in_=pq[par][:], func=AF.Exp,
                                        bias=neg24, scale=0.125)
                p.add("scalar", e12, waits=((st, w_s),), inc=ss)
                exp_done[j] = p.now(ss)
                avw = [(ss, exp_done[j]), (sv, vtok_done)]
                vb = (j * MYH + 2 * hp) * 65
                if j == 0:
                    avw += list(L.guard(("po", 0)) + L.guard(("po", 1)))

                def a1(e, pA=pA, vb=vb, j=j, njc=njc):
                    return e.matmul(out=po[0][:],
                                    lhsT=vtok[:, vb:vb + 65], rhs=pA,
                                    start=(j == 0), stop=(j == njc - 1))

                def a2(e, pB=pB, vb=vb, j=j, njc=njc):
                    return e.matmul(out=po[1][:],
                                    lhsT=vtok[:, vb + 65:vb + 130], rhs=pB,
                                    start=(j == 0), stop=(j == njc - 1))
                pend.append((a1, a2, tuple(avw)))
                if len(pend) > 1:
                    flush_av(pend.pop(0))
            while pend:
                flush_av(pend.pop(0))
            av_done = p.now(st)

            for hh in range(2):
                pox = po[hh]

                def r1(e, pox=pox):
                    return e.reciprocal(out=rd[:], in_=pox[64:65, :])
                p.add("vector", r1,
                      waits=((st, av_done),) + L.guard("rd"), inc=sv)
                w_r = p.now(sv)

                def bc(e):
                    return e.matmul(out=pq[0][0:64, 0:512], lhsT=ones64[:],
                                    rhs=rd[:], start=True, stop=True)
                p.add("tensor", bc,
                      waits=((sv, w_r),) + L.guard(("pb", 0)), inc=st)
                w_b = p.now(st)
                L.set_guard("rd", st, w_b)

                def oc(e, pox=pox):
                    return e.copy(out=o_s[:], in_=pox[0:64, :])
                p.add("scalar", oc,
                      waits=((st, av_done),) + L.guard("o_s"), inc=ss)
                w_oc = p.now(ss)
                dst = oTb[64 * hh:64 * hh + 64,
                          hp * 512:(hp + 1) * 512]

                def dv(e, dst=dst):
                    return e.tensor_mul(out=dst, in0=o_s[:],
                                        in1=pq[0][0:64, 0:512])
                p.add("vector", dv, waits=((st, w_b), (ss, w_oc)), inc=sv)
                L.set_guard("o_s", sv, p.now(sv))
                L.set_guard(("pb", 0), sv, p.now(sv))
            L.set_guard(("po", 0), ss, p.now(ss))
            L.set_guard(("po", 1), ss, p.now(ss))
            L.set_guard(("po", 0), sv, p.now(sv))
            L.set_guard(("po", 1), sv, p.now(sv))
        oT_done = p.now(sv)

        for tt in range(4):
            for f2 in range(2):
                bk = 2 + (tt * 2 + f2) % 2
                for ci in range(4):
                    w = ()
                    if ci == 0:
                        w = ((sv, oT_done), (gx, w_wdma)) + \
                            L.guard(("pb", bk))

                    def pm(e, tt=tt, f2=f2, ci=ci, bk=bk):
                        return e.matmul(
                            out=pb[bk][:],
                            lhsT=oTb[:, ci * 512 + tt * 128:
                                     ci * 512 + (tt + 1) * 128],
                            rhs=wpsb[:, ci * D + f2 * 512:
                                     ci * D + (f2 + 1) * 512],
                            start=(ci == 0), stop=(ci == 3))
                    p.add("tensor", pm, waits=w,
                          inc=st if ci == 3 else None)
                w_pm = p.now(st)
                dst = stage[:, tt * D + f2 * 512:
                            tt * D + f2 * 512 + 512]

                def ev(e, dst=dst, bk=bk):
                    return e.tensor_copy(out=dst, in_=pb[bk][:])
                p.add("vector", ev,
                      waits=((st, w_pm),) + L.guard("stage"),
                      inc=sv)
                L.set_guard(("pb", bk), sv, p.now(sv))
        w_stage = p.now(sv)

        def sdma(e, c=c):
            return e.dma_start(
                out=s0p[c * 512:(c + 1) * 512, :].rearrange(
                    "(t p) n -> p t n", p=128),
                in_=stage[:].rearrange("p (t n) -> p t n", t=4))
        p.add("gpsimd", sdma, waits=((sv, w_stage),), inc=so, inc_by=16)
        L.set_guard("stage", so, p.now(so))

    if not d.get('skip_tail'):
        g0src = d.get('g0_src_ap')
        if g0src is None:
            g0src = x_own
        for oc in range(d.get('g0_chunks', 4)):
            par = oc % 2
            cbuf = xbuf[:, par * 4 * D:(par + 1) * 4 * D]

            def xdma(e, oc=oc, cbuf=cbuf):
                return e.dma_start(
                    out=cbuf.rearrange("p (t n) -> p t n", t=4),
                    in_=g0src[oc * 512:(oc + 1) * 512, :].rearrange(
                        "(t p) n -> p t n", p=128))
            p.add("gpsimd", xdma, waits=L.guard(("xbuf", par)),
                  inc=gx, inc_by=16)
            w_x = p.now(gx)
            xnT_done = chunk_ln_transpose(cbuf, ((gx, w_x),))
            L.set_guard(("xbuf", par), sv, xnT_done)
            for tt in range(4):
                for f2 in range(2):
                    bk = (tt * 2 + f2) % 2
                    for ci in range(8):
                        w = ()
                        if ci == 0:
                            w = ((sv, xnT_done), (ss, w_sgn)) + \
                                L.guard(("pb", bk))

                        def mfn(e, tt=tt, f2=f2, ci=ci, bk=bk):
                            return e.matmul(
                                out=pb[bk][:],
                                lhsT=xnT[:, ci * 512 + tt * 128:
                                         ci * 512 + (tt + 1) * 128],
                                rhs=sgnT[:, ci * D + f2 * 512:
                                         ci * D + (f2 + 1) * 512],
                                start=(ci == 0), stop=(ci == 7))
                        p.add("tensor", mfn, waits=w,
                              inc=st if ci == 7 else None)
                    w_mm = p.now(st)
                    dst = gstage[:, tt * D + f2 * 512:
                                 tt * D + f2 * 512 + 512]

                    def ev(e, dst=dst, bk=bk):
                        return e.activation(out=dst, in_=pb[bk][:],
                                            func=AF.Silu)
                    p.add("scalar", ev,
                          waits=((st, w_mm),) + L.guard("gstage"),
                          inc=ss)
                    L.set_guard(("pb", bk), ss, p.now(ss))
            w_ev = p.now(ss)

            def gdma(e, oc=oc):
                return e.dma_start(
                    out=g0[oc * 512:(oc + 1) * 512, :].rearrange(
                        "(t p) n -> p t n", p=128),
                    in_=gstage[:].rearrange("p (t n) -> p t n", t=4))
            p.add("sync", gdma, waits=((ss, w_ev),), inc=so, inc_by=16)
            L.set_guard("gstage", so, p.now(so))


    p.add("sync", lambda e: e.nop(), waits=((so, p.now(so)),))


def build_launch_A():
    L = Launch()
    d = {}
    d["x_full"] = L.inp("x_full", [S, D])
    d["x_own"] = L.inp("x_own", [TOK, D])
    d["w1p"] = L.inp("w1p", [D, 1536])
    d["b1p"] = L.inp("b1p", [128, 12])
    d["w0t"] = L.inp("w0t", [D, D])
    d["wpmy"] = L.inp("wpmy", [512, D])
    d["idn"] = L.inp("idn", [128, 128], BF16)
    d["dmask"] = L.inp("dmask", [4 * 128, 1024], BF16)
    d["s0p"] = L.out("s0p", [S, D])
    d["g0"] = L.out("g0", [TOK, D])
    with L.es:
        phase_A(L, d)
        L.p.emit()
    return L.nc



def phase_C(L, d):
    nc, p = L.nc, L.p
    wq = d.get("wq"); bq = d.get("bq")
    wk = d.get("wk"); bk = d.get("bk")
    wv = d.get("wv"); bv = d.get("bv")
    wc = d.get("wc"); we = d.get("we")
    idn = d.get("idn")
    trimask = d.get("trimask")
    duplo = d.get("duplo")
    duphi = d.get("duphi")
    qfo = d.get("qfo")
    numx = d.get("numx")
    tex = d.get("tex")
    x1_own_src = d.get("x1_own_src")

    sb, ps = L.sb, L.ps
    x1bP = [sb(f"x1b{i}", [128, 4 * D]) for i in range(2)]
    xn_bfP = [sb(f"xn_bf{i}", [128, 4 * D], BF16) for i in range(2)]
    xnTP = [sb(f"xnT{i}", [128, 8 * 512], BF16) for i in range(2)]
    wqsb = sb("wqsb", [128, 8 * D], BF16)
    wksb = sb("wksb", [128, 8 * 256], BF16)
    wvsb = sb("wvsb", [128, 8 * 256], BF16)
    wcsb = sb("wcsb", [128, 2 * 64], BF16)
    wesb = sb("wesb", [64, 256], BF16)
    bqsb = sb("bqsb", [128, 8])
    bksb = sb("bksb", [128, 2])
    bvsb = sb("bvsb", [128, 2])
    idsb = sb("idsb", [128, 128], BF16)
    trisb = sb("trisb", [128, 128], BF16)
    dlosb = sb("dlosb", [128, 128], BF16)
    dhisb = sb("dhisb", [128, 128], BF16)
    kfdupP = [sb(f"kfdup{i}", [128, 4 * 512], BF16) for i in range(2)]
    sprevDP = [sb(f"sprevD{i}", [128, 260], BF16) for i in range(2)]
    qtmpP = [sb(f"qtmp{i}", [128, 512]) for i in range(2)]
    t1P = [sb(f"t1_{i}", [128, 512]) for i in range(2)]
    t2P = [sb(f"t2_{i}", [128, 512]) for i in range(2)]
    t3P = [sb(f"t3_{i}", [128, 512]) for i in range(2)]
    qfTP = [sb(f"qfT{i}", [128, 8 * 512], BF16) for i in range(2)]
    krawP = [sb(f"kraw{i}", [128, 2 * 512], BF16) for i in range(2)]
    vrawP = [sb(f"vraw{i}", [128, 2 * 512], BF16) for i in range(2)]
    ckTP = [sb(f"ckT{i}", [64, 512], BF16) for i in range(2)]
    cvTP = [sb(f"cvT{i}", [64, 512], BF16) for i in range(2)]
    kfTP = [sb(f"kfT{i}", [128, 2 * 512], BF16) for i in range(2)]
    kftokP = [sb(f"kftok{i}", [128, 4 * 256], BF16) for i in range(2)]
    v2eP = [sb(f"v2e{i}", [128, 4 * 260], BF16) for i in range(2)]
    asbP = [sb(f"asb{i}", [128, 128], BF16) for i in range(2)]
    sprev = sb("sprev", [64, 260], BF16)
    sprev32 = sb("sprev32", [64, 260])
    numstage = sb("numstage", [128, 16 * 65])
    lsq = sb("lsq", [128, D])
    lsm = sb("lsm", [128, 32])
    consts = sb("consts", [128, 4])

    pb = [ps(f"pb{i}", [128, 512]) for i in range(4)]
    pnb = [ps(f"pn{i}", [128, 512]) for i in range(2)]
    pn = [t[:, 0:65] for t in pnb]
    pSb = ps("pS", [128, 512])
    pS = pSb[0:64, 0:260]
    ptb = ps("pt0", [128, 1024], BF16)
    pt0 = ptb[:, 0:512]
    ptP = [ptb[:, 0:512], ptb[:, 0:512]]

    dx = L.sem("dx")
    gx = L.sem("gx")
    st = L.sem("st")
    sv = L.sem("sv")
    ss = L.sem("ss")
    so = L.sem("so")
    sg = L.sem("sg")
    L.sv, L.ss = sv, ss

    eps5 = consts[:, 0:1]

    def ln_sc(i):
        return {"sq": lsq, "s": lsm[:, 6 * i:6 * i + 6]}

    p.add("vector", lambda e: e.memset(eps5, 1e-5), inc=sv)
    for par in range(2):
        p.add("vector", lambda e, par=par: e.memset(v2eP[par][:], 1.0),
              inc=sv)
    base_scr = d.get("base_scr")
    p.add("vector", lambda e: e.memset(sprev32[:], 0.0), inc=sv)
    ones_rdy = p.now(sv)
    p.add("sync", lambda e: e.dma_start(out=idsb[:], in_=idn),
          inc=dx, inc_by=16)
    p.add("sync", lambda e: e.dma_start(out=trisb[:], in_=trimask),
          inc=dx, inc_by=16)
    p.add("sync", lambda e: e.dma_start(out=dlosb[:], in_=duplo),
          inc=dx, inc_by=16)
    p.add("sync", lambda e: e.dma_start(out=dhisb[:], in_=duphi),
          inc=dx, inc_by=16)
    for bsrc, bdst in [(bq, bqsb), (bk, bksb), (bv, bvsb)]:
        p.add("sync", lambda e, s=bsrc, d=bdst: e.dma_start(out=d[:], in_=s),
              inc=dx, inc_by=16)
    w_small = p.now(dx)
    p.add("gpsimd", lambda e: e.dma_start(
        out=wqsb[:].rearrange("p (c n) -> p c n", c=8),
        in_=wq.rearrange("(c p) n -> p c n", p=128)), inc=gx, inc_by=16)
    p.add("gpsimd", lambda e: e.dma_start(
        out=wksb[:].rearrange("p (c n) -> p c n", c=8),
        in_=wk.rearrange("(c p) n -> p c n", p=128)), inc=gx, inc_by=16)
    p.add("gpsimd", lambda e: e.dma_start(
        out=wvsb[:].rearrange("p (c n) -> p c n", c=8),
        in_=wv.rearrange("(c p) n -> p c n", p=128)), inc=gx, inc_by=16)
    p.add("gpsimd", lambda e: e.dma_start(
        out=wcsb[:].rearrange("p (c n) -> p c n", c=2),
        in_=wc.rearrange("(c p) n -> p c n", p=128)), inc=gx, inc_by=16)
    p.add("gpsimd", lambda e: e.dma_start(out=wesb[:], in_=we),
          inc=gx, inc_by=16)
    w_wdma = p.now(gx)
    p.add("sync", lambda e: e.dma_start(out=sprev32[:], in_=base_scr),
          waits=((sv, ones_rdy),), inc=dx, inc_by=16)
    w_bl = p.now(dx)
    p.add("vector", lambda e: e.tensor_copy(out=sprev[:], in_=sprev32[:]),
          waits=((dx, w_bl),), inc=sv)
    sprev_rdy = p.now(sv)
    def ddm0(e):
        return e.matmul(out=pb[3][:, 0:260], lhsT=dlosb[0:64, :],
                        rhs=sprev[:], start=True, stop=True)
    p.add("tensor", ddm0, waits=((sv, sprev_rdy), (dx, w_small)), inc=st)
    w_dd0 = p.now(st)
    p.add("vector", lambda e: e.tensor_copy(out=sprevDP[0][:],
                                            in_=pb[3][:, 0:260]),
          waits=((st, w_dd0),), inc=sv)
    L.set_guard(("pb", 3), sv, p.now(sv))

    st_kfdup = {}
    st_v2 = {}
    st_kftok = {}
    st_qf = {}
    qfdma_so = {}

    def emit_ln(oc):
        par = oc % 2
        x1b, xn_bf, xnT = x1bP[par], xn_bfP[par], xnTP[par]
        def dx1(e, oc=oc, x1b=x1b):
            return e.dma_start(
                out=x1b[:].rearrange("p (t n) -> p t n", t=4),
                in_=x1_own_src[oc * 512:(oc + 1) * 512, :].rearrange(
                    "(t p) n -> p t n", p=128))
        p.add("sync", dx1, waits=L.guard(("x1b", par)), inc=dx, inc_by=16)
        w_x1d = p.now(dx)
        tiles = [(x1b[:, t * D:(t + 1) * D], xn_bf[:, t * D:(t + 1) * D],
                  ln_sc(t)) for t in range(4)]
        w_apply = emit_ln_tiles(L, tiles, waits=((dx, w_x1d),), eps_ap=eps5)
        L.set_guard(("x1b", par), sv, w_apply)
        for ci in range(8):
            bkp = ci % 2
            for t in range(4):
                def tfn(e, ci=ci, t=t, xn_bf=xn_bf, bkp=bkp):
                    return e.transpose(
                        out=ptP[bkp][:, t * 128:(t + 1) * 128],
                        in_=xn_bf[:, t * D + ci * 128:t * D + ci * 128 + 128],
                        identity=idsb[:])
                p.add("tensor", tfn,
                      waits=(((sv, w_apply), (dx, w_small)) +
                             L.guard("ptC") if t == 0 else ()),
                      inc=st if t == 3 else None)
            w_t = p.now(st)
            def cfn(e, ci=ci, xnT=xnT, bkp=bkp):
                return e.tensor_copy(out=xnT[:, ci * 512:(ci + 1) * 512],
                                     in_=ptP[bkp][:])
            p.add("vector", cfn, waits=((st, w_t),), inc=sv)
            L.set_guard("ptC", sv, p.now(sv))
        return p.now(sv)

    ln_done = {}

    def emit_q(oc):
        par = oc % 2
        xnT, qfT = xnTP[par], qfTP[par]
        xnT_done = ln_done[oc]
        qf_guard = L.guard(("qfT", par))
        for f1 in range(8):
            bk_ = f1 % 2
            fp = f1 % 2
            qtmp, t1, t2, t3 = qtmpP[fp], t1P[fp], t2P[fp], t3P[fp]
            for ci in range(8):
                w = ()
                if ci == 0:
                    w = ((sv, xnT_done), (gx, w_wdma)) + L.guard(("pb", bk_))
                def mfn(e, f1=f1, ci=ci, bk_=bk_, xnT=xnT):
                    return e.matmul(
                        out=pb[bk_][:],
                        lhsT=wqsb[:, ci * D + f1 * 128:ci * D + (f1 + 1) * 128],
                        rhs=xnT[:, ci * 512:(ci + 1) * 512],
                        start=(ci == 0), stop=(ci == 7))
                p.add("tensor", mfn, waits=w, inc=st if ci == 7 else None)
            w_mm = p.now(st)
            def efn(e, f1=f1, bk_=bk_, qtmp=qtmp):
                return e.activation(out=qtmp[:], in_=pb[bk_][:],
                                    func=AF.Identity,
                                    bias=bqsb[:, f1:f1 + 1], scale=1.0)
            p.add("scalar", efn,
                  waits=((st, w_mm), (dx, w_small)) + L.guard(("qtmp", fp)),
                  inc=ss)
            w_e = p.now(ss)
            L.set_guard(("pb", bk_), ss, w_e)
            p.add("vector", lambda e, qtmp=qtmp, t1=t1: e.tensor_scalar_min(
                out=t1[:], in0=qtmp[:], scalar1=0.0),
                waits=((ss, w_e),) + L.guard(("t1", fp)), inc=sv)
            p.add("vector", lambda e, qtmp=qtmp, t3=t3: e.tensor_scalar_max(
                out=t3[:], in0=qtmp[:], scalar1=0.0),
                waits=L.guard(("t3", fp)), inc=sv)
            w_mn = p.now(sv)
            L.set_guard(("qtmp", fp), sv, w_mn)
            p.add("scalar", lambda e, t1=t1, t2=t2: e.activation(
                out=t2[:], in_=t1[:], func=AF.Exp),
                waits=((sv, w_mn),) + L.guard(("t2", fp)), inc=ss)
            w_ex = p.now(ss)
            L.set_guard(("t1", fp), ss, w_ex)
            dst = qfT[:, f1 * 512:(f1 + 1) * 512]
            p.add("vector", lambda e, dst=dst, t2=t2, t3=t3: e.tensor_add(
                out=dst, in0=t2[:], in1=t3[:]),
                waits=((ss, w_ex), (sv, w_mn)) + (qf_guard if f1 == 0 else ()),
                inc=sv)
            L.set_guard(("t2", fp), sv, p.now(sv))
            L.set_guard(("t3", fp), sv, p.now(sv))
        qf_done = p.now(sv)
        def qfdma(e, oc=oc, qfT=qfT):
            return e.dma_start(
                out=qfo[:, oc * 512:(oc + 1) * 512].rearrange(
                    "(c p) n -> p c n", p=128),
                in_=qfT[:].rearrange("p (c n) -> p c n", c=8))
        p.add("gpsimd", qfdma, waits=((sv, qf_done),), inc=so, inc_by=16)
        qfdma_so[oc] = p.now(so)
        st_qf[oc] = qf_done

    def emit_kv(oc):
        par = oc % 2
        xnT = xnTP[par]
        kraw, vraw = krawP[par], vrawP[par]
        ckT, cvT, kfT = ckTP[par], cvTP[par], kfTP[par]
        kftok, v2e, kfdup = kftokP[par], v2eP[par], kfdupP[par]
        xnT_done = ln_done[oc]
        for (wsb, bsb, dstraw) in ((wksb, bksb, kraw), (wvsb, bvsb, vraw)):
            for f1 in range(2):
                bk_ = 2 + f1
                for ci in range(8):
                    w = ()
                    if ci == 0:
                        w = ((sv, xnT_done), (gx, w_wdma)) + \
                            L.guard(("pb", bk_))
                    def mfn(e, f1=f1, ci=ci, bk_=bk_, wsb=wsb, xnT=xnT):
                        return e.matmul(
                            out=pb[bk_][:],
                            lhsT=wsb[:, ci * 256 + f1 * 128:
                                     ci * 256 + (f1 + 1) * 128],
                            rhs=xnT[:, ci * 512:(ci + 1) * 512],
                            start=(ci == 0), stop=(ci == 7))
                    p.add("tensor", mfn, waits=w, inc=st if ci == 7 else None)
                w_mm = p.now(st)
                def efn(e, f1=f1, bk_=bk_, bsb=bsb, dstraw=dstraw):
                    return e.activation(
                        out=dstraw[:, f1 * 512:(f1 + 1) * 512],
                        in_=pb[bk_][:], func=AF.Identity,
                        bias=bsb[:, f1:f1 + 1], scale=1.0)
                p.add("scalar", efn, waits=((st, w_mm), (dx, w_small)),
                      inc=ss)
                L.set_guard(("pb", bk_), ss, p.now(ss))
        kv_done = p.now(ss)
        for (rawb, cdst) in ((kraw, ckT), (vraw, cvT)):
            for ci in range(2):
                def mfn(e, ci=ci, rawb=rawb):
                    return e.matmul(
                        out=pb[0][0:64, :],
                        lhsT=wcsb[:, ci * 64:(ci + 1) * 64],
                        rhs=rawb[:, ci * 512:(ci + 1) * 512],
                        start=(ci == 0), stop=(ci == 1))
                p.add("tensor", mfn,
                      waits=((ss, kv_done),) + L.guard(("pb", 0))
                      if ci == 0 else (), inc=st if ci == 1 else None)
            w_mm = p.now(st)
            def efn(e, cdst=cdst):
                return e.tensor_copy(out=cdst[:], in_=pb[0][0:64, :])
            p.add("vector", efn, waits=((st, w_mm),), inc=sv)
            L.set_guard(("pb", 0), sv, p.now(sv))
        c_done = p.now(sv)
        for f1 in range(2):
            fp = f1 % 2
            qtmp, t1, t2, t3 = qtmpP[fp], t1P[fp], t2P[fp], t3P[fp]
            def mfn(e, f1=f1, ckT=ckT):
                return e.matmul(
                    out=pb[1][:],
                    lhsT=wesb[:, f1 * 128:(f1 + 1) * 128],
                    rhs=ckT[:], start=True, stop=True)
            p.add("tensor", mfn,
                  waits=((sv, c_done), (gx, w_wdma)) + L.guard(("pb", 1)),
                  inc=st)
            w_mm = p.now(st)
            def efn(e, qtmp=qtmp):
                return e.copy(out=qtmp[:], in_=pb[1][:])
            p.add("scalar", efn,
                  waits=((st, w_mm),) + L.guard(("qtmp", fp)), inc=ss)
            w_e = p.now(ss)
            L.set_guard(("pb", 1), ss, w_e)
            p.add("vector", lambda e, qtmp=qtmp, t1=t1: e.tensor_scalar_min(
                out=t1[:], in0=qtmp[:], scalar1=0.0),
                waits=((ss, w_e),) + L.guard(("t1", fp)), inc=sv)
            p.add("vector", lambda e, qtmp=qtmp, t3=t3: e.tensor_scalar_max(
                out=t3[:], in0=qtmp[:], scalar1=0.0),
                waits=L.guard(("t3", fp)), inc=sv)
            w_mn = p.now(sv)
            L.set_guard(("qtmp", fp), sv, w_mn)
            p.add("scalar", lambda e, t1=t1, t2=t2: e.activation(
                out=t2[:], in_=t1[:], func=AF.Exp),
                waits=((sv, w_mn),) + L.guard(("t2", fp)), inc=ss)
            w_ex = p.now(ss)
            L.set_guard(("t1", fp), ss, w_ex)
            dst = kfT[:, f1 * 512:(f1 + 1) * 512]
            p.add("vector", lambda e, dst=dst, t2=t2, t3=t3: e.tensor_add(
                out=dst, in0=t2[:], in1=t3[:]),
                waits=((ss, w_ex), (sv, w_mn)), inc=sv)
            L.set_guard(("t2", fp), sv, p.now(sv))
            L.set_guard(("t3", fp), sv, p.now(sv))
        kf_done = p.now(sv)
        for t in range(4):
            def mfn(e, t=t, cvT=cvT):
                return e.matmul(
                    out=pb[2][:, 0:256],
                    lhsT=cvT[:, t * 128:(t + 1) * 128],
                    rhs=wesb[:], start=True, stop=True)
            p.add("tensor", mfn,
                  waits=((sv, c_done), (gx, w_wdma)) + L.guard(("pb", 2)),
                  inc=st)
            w_mm = p.now(st)
            for g in range(4):
                dst = v2e[:, t * 260 + g * 65:t * 260 + g * 65 + 64]
                def efn(e, dst=dst, g=g):
                    return e.tensor_copy(out=dst,
                                         in_=pb[2][:, g * 64:(g + 1) * 64])
                p.add("vector", efn,
                      waits=((st, w_mm), (sv, ones_rdy)) +
                      (L.guard(("v2e", par)) if t == 0 and g == 0 else ()),
                      inc=sv if g == 3 else None)
            L.set_guard(("pb", 2), sv, p.now(sv))
        v2_done = p.now(sv)
        for ci in range(2):
            for t in range(4):
                bkp = (ci * 4 + t) % 2
                def tfn(e, ci=ci, t=t, kfT=kfT, bkp=bkp):
                    return e.transpose(
                        out=ptP[bkp][:, 0:128],
                        in_=kfT[:, ci * 512 + t * 128:ci * 512 + (t + 1) * 128],
                        identity=idsb[:])
                p.add("tensor", tfn,
                      waits=((sv, kf_done),) + L.guard("ptC"), inc=st)
                w_t = p.now(st)
                dst = kftok[:, t * 256 + ci * 128:t * 256 + (ci + 1) * 128]
                def cfn(e, dst=dst, bkp=bkp):
                    return e.tensor_copy(out=dst, in_=ptP[bkp][:, 0:128])
                p.add("vector", cfn,
                      waits=((st, w_t),) +
                      (L.guard(("kftok", par)) if ci == 0 and t == 0 else ()),
                      inc=sv)
                L.set_guard("ptC", sv, p.now(sv))
        kftok_done = p.now(sv)
        for g in range(4):
            sel = dlosb if g % 2 == 0 else dhisb
            def dmm(e, g=g, sel=sel, kfT=kfT):
                return e.matmul(
                    out=pb[3][:],
                    lhsT=sel[:], rhs=kfT[:, (g // 2) * 512:(g // 2 + 1) * 512],
                    start=True, stop=True)
            p.add("tensor", dmm,
                  waits=((sv, kf_done), (dx, w_small)) + L.guard(("pb", 3)),
                  inc=st)
            w_d2 = p.now(st)
            def dcp(e, g=g, kfdup=kfdup):
                return e.tensor_copy(out=kfdup[:, g * 512:(g + 1) * 512],
                                     in_=pb[3][:])
            p.add("vector", dcp,
                  waits=((st, w_d2),) +
                  (L.guard(("kfdup", par)) if g == 0 else ()), inc=sv)
            L.set_guard(("pb", 3), sv, p.now(sv))
        kfdup_done = p.now(sv)
        st_kfdup[oc] = kfdup_done
        st_v2[oc] = v2_done
        st_kftok[oc] = kftok_done

    def emit_scan_t(oc, t):
        par = oc % 2
        qfT, kftok = qfTP[par], kftokP[par]
        v2e, kfdup = v2eP[par], kfdupP[par]
        j = 4 * oc + t
        jpar = j % 2
        for g in range(4):
            def smm(e, t=t, g=g, kftok=kftok, v2e=v2e):
                return e.matmul(
                    out=pS[:, g * 65:(g + 1) * 65],
                    lhsT=kftok[:, t * 256 + g * 64:t * 256 + (g + 1) * 64],
                    rhs=v2e[:, t * 260 + g * 65:t * 260 + (g + 1) * 65],
                    start=(g == 0), stop=(g == 3))
            p.add("tensor", smm,
                  waits=((sv, max(st_kftok[oc], st_v2[oc])),) +
                  (L.guard("pS") if g == 0 else ()),
                  inc=st if g == 3 else None)
        w_s = p.now(st)
        p.add("vector", lambda e: e.tensor_add(
            out=sprev32[:], in0=sprev32[:], in1=pS[:]),
            waits=((st, w_s),), inc=sv)
        w_acc = p.now(sv)
        p.add("vector", lambda e: e.tensor_copy(
            out=sprev[:], in_=sprev32[:]),
            waits=((sv, w_acc),) + L.guard("sprev"), inc=sv)
        w_cast = p.now(sv)
        L.set_guard("pS", sv, w_cast)
        ddm_emitted = [False]

        def emit_ddm():
            ddm_emitted[0] = True
            def ddm(e):
                return e.matmul(out=pb[3][:, 0:260], lhsT=dlosb[0:64, :],
                                rhs=sprev[:], start=True, stop=True)
            p.add("tensor", ddm,
                  waits=((sv, w_cast),) + L.guard(("pb", 3)), inc=st)
            w_dd = p.now(st)
            L.set_guard("sprev", st, w_dd)
            def dcp(e, jpar=jpar):
                return e.tensor_copy(
                    out=sprevDP[1 - jpar][:], in_=pb[3][:, 0:260])
            p.add("vector", dcp,
                  waits=((st, w_dd),) + L.guard(("sprevD", 1 - jpar)),
                  inc=sv)
            L.set_guard(("pb", 3), sv, p.now(sv))

        sprevD = sprevDP[jpar]
        pend = []

        def emit_n(item):
            h, qlh, g, hp2, hb, w_m = item
            nb = pn[hp2]
            def n1(e, nb=nb, t=t, g=g, hp2=hp2, v2e=v2e):
                return e.matmul(
                    out=nb[:], lhsT=asbP[hp2][:],
                    rhs=v2e[:, t * 260 + g * 65:t * 260 + (g + 1) * 65],
                    start=True, stop=False)
            def n2(e, nb=nb, qlh=qlh, g=g, hb=hb, sprevD=sprevD):
                return e.matmul(
                    out=nb[:], lhsT=qlh,
                    rhs=sprevD[hb:hb + 64, g * 65:(g + 1) * 65],
                    start=False, stop=True)
            p.add("tensor", n1,
                  waits=((sv, max(w_m, st_v2[oc])),) +
                  L.guard(("pn", hp2)))
            p.add("tensor", n2, inc=st)
            w_n = p.now(st)
            L.set_guard(("asb", hp2), st, w_n)
            dst = numstage[:, h * 65:(h + 1) * 65]
            def nev(e, dst=dst, nb=nb):
                return e.tensor_copy(out=dst, in_=nb[:])
            p.add("vector", nev,
                  waits=((st, w_n),) + L.guard("numstage"), inc=sv)
            L.set_guard(("pn", hp2), sv, p.now(sv))

        for h in range(16):
            g = h // 4
            hp2 = h % 2
            hb = (h % 2) * 64
            klh = kfdup[hb:hb + 64,
                        g * 512 + t * 128:g * 512 + (t + 1) * 128]
            qslice = qfT[:, (h // 2) * 512 + t * 128:
                         (h // 2) * 512 + (t + 1) * 128]
            qlh = qslice[hb:hb + 64, :]
            def amm(e, klh=klh, qlh=qlh, hp2=hp2):
                return e.matmul(out=pb[hp2][:, 0:128], lhsT=klh,
                                rhs=qlh, start=True, stop=True)
            p.add("tensor", amm,
                  waits=((sv, st_kfdup[oc]),) + L.guard(("pb", hp2)),
                  inc=st)
            w_a = p.now(st)
            def msk(e, hp2=hp2):
                return e.tensor_mul(out=asbP[hp2][:], in0=pb[hp2][:, 0:128],
                                    in1=trisb[:])
            p.add("vector", msk,
                  waits=((st, w_a), (dx, w_small)) + L.guard(("asb", hp2)),
                  inc=sv)
            w_m = p.now(sv)
            L.set_guard(("pb", hp2), sv, w_m)
            pend.append((h, qlh, g, hp2, hb, w_m))
            if h == 1:
                emit_ddm()
            if len(pend) > 2:
                emit_n(pend.pop(0))
        while pend:
            emit_n(pend.pop(0))
        if not ddm_emitted[0]:
            emit_ddm()
        L.set_guard(("sprevD", jpar), st, p.now(st))
        w_ns = p.now(sv)
        w_ng = p.now(sg)
        def ndma(e, j=j):
            return e.dma_start(
                out=numx[j * 128:(j + 1) * 128, :],
                in_=numstage[:])
        p.add("sync", ndma, waits=((sv, w_ns), (sg, w_ng)), inc=so,
              inc_by=16)
        L.set_guard("numstage", so, p.now(so))

    ln_done[0] = emit_ln(0)
    emit_q(0)
    emit_kv(0)
    for oc in range(4):
        par = oc % 2
        if oc + 1 < 4:
            ln_done[oc + 1] = emit_ln(oc + 1)
        emit_scan_t(oc, 0)
        emit_scan_t(oc, 1)
        if oc + 1 < 4:
            emit_q(oc + 1)
        emit_scan_t(oc, 2)
        if oc + 1 < 4:
            emit_kv(oc + 1)
        emit_scan_t(oc, 3)
        w_st = p.now(st)
        L.set_guard(("kfdup", par), st, w_st)
        L.set_guard(("v2e", par), st, w_st)
        L.set_guard(("kftok", par), st, w_st)
        L.guards[("qfT", par)] = [(st, w_st), (so, qfdma_so[oc])]

    def tdma(e):
        return e.dma_start(
            out=tex.rearrange("(g p) n -> p g n", p=64),
            in_=sprev32[:].rearrange("p (g n) -> p g n", g=4))
    p.add("sync", tdma, waits=((sv, p.now(sv)),), inc=so, inc_by=16)

    p.add("sync", lambda e: e.nop(), waits=((so, p.now(so)),))



def phase_D(L, d):
    nc, p = L.nc, L.p
    x1i = d.get("x1i")
    qft = d.get("qft")
    numx = d.get("numx")
    texd = d.get("texd")
    wo = d.get("wo")
    bo = d.get("bo")
    gam = d.get("gam")
    bet = d.get("bet")
    gw = d.get("gw")
    selm = d.get("selm")
    wgu = d.get("wgu")
    wd = d.get("wd")
    idn = d.get("idn")
    yout = d.get("yout")
    s_scr = d["s_scr"]

    sb, ps = L.sb, L.ps
    x1b = sb("x1b", [128, D])
    nmx = sb("nmx", [128, 16 * 65])
    qfb = sb("qfb", [128, 8 * 128], BF16)
    texsb = sb("texsb", [128, 260], BF16)
    dnbuf = sb("dnbuf", [128, 16])
    rcbuf = sb("rcbuf", [128, 16])
    obuf = sb("obuf", [128, D], BF16)
    oTch = sb("oTch", [128, 8 * 128], BF16)
    xn1tok = sb("xn1tok", [128, D])
    xnbf = sb("xnbf", [128, D], BF16)
    ybuf = sb("ybuf", [128, D])
    ytmp = sb("ytmp", [128, D])
    sstage = sb("sstage", [128, D], BF16)
    bo_b = sb("bo_b", [128, D], BF16)
    gam_b = sb("gam_b", [128, D], BF16)
    bet_b = sb("bet_b", [128, D], BF16)
    ones1 = sb("ones1", [1, 128])
    ones1b = sb("ones1b", [1, 128], BF16)
    idsb = sb("idsb", [128, 128], BF16)
    lsq = sb("lsq", [128, D])
    rowbuf = lsq[0:1, :]
    lsm = sb("lsm", [128, 32])
    consts = sb("consts", [128, 4])
    xn1T = sb("xn1T", [128, 8 * TOK], BF16)
    acc_scr = d["acc_scr"]
    gwsb = sb("gwsb", [128, 8 * 8], BF16)
    lgbuf = sb("lgbuf", [128, 16 * 8])
    ebuf = sb("ebuf", [128, 16 * 8])
    smb = sb("smb", [128, 16])
    m1b = sb("m1b", [128, 16])
    m2b = sb("m2b", [128, 16])
    rnb = sb("rnb", [128, 16])
    mskb = sb("mskb", [128, 8])
    pmb = sb("pmb", [128, 16 * 8])
    p2b = sb("p2b", [128, 16 * 8])
    cwbuf = sb("cwbuf", [128, 16 * 8])
    cwbf = sb("cwbf", [128, 16 * 8], BF16)
    cwT = sb("cwT", [8, TOK], BF16)
    selsb = sb("selsb", [8, 8 * 128], BF16)
    cwb = sb("cwb", [128, 4 * 512], BF16)
    wgbuf = sb("wgbuf", [128, 2 * 8 * 512], BF16)
    gbuf16 = sb("gbuf16", [128, 8 * 128], BF16)
    wdsb = sb("wdsb", [128, 22 * D], BF16)
    wosb = wdsb[:, 0:8 * D]
    tgbuf = sb("tgbuf", [128, 2 * 1024], BF16)
    hbuf = sb("hbuf", [128, 22 * 1024], BF16)
    tbm = sb("tbm", [128, 2 * 1024], BF16)

    pq = [ps("pq0", [128, 1024]), ps("pq1", [128, 1024])]
    pb = [pq[0][:, 0:512], pq[0][:, 512:1024],
          pq[1][:, 0:512], pq[1][:, 512:1024]]
    pnb01 = ps("pn01", [128, 1024])
    pn = [pnb01[:, 0:65], pnb01[:, 512:577]]
    pnb = [pnb01[:, 0:512], pnb01[:, 512:1024]]
    ptb = ps("pt0", [128, 1024], BF16)
    pt0 = ptb[:, 0:512]

    dx = L.sem("dx")
    gx = L.sem("gx")
    st = L.sem("st")
    sv = L.sem("sv")
    ss = L.sem("ss")
    so = L.sem("so")
    sg = L.sem("sg")
    L.sv, L.ss = sv, ss

    eps5 = consts[:, 0:1]
    eps6 = consts[:, 1:2]

    def ln_sc(i):
        return {"sq": lsq, "s": lsm[:, 6 * i:6 * i + 6]}

    p.add("vector", lambda e: e.memset(eps5, 1e-5), inc=sv)
    p.add("vector", lambda e: e.memset(eps6, 1e-6), inc=sv)
    p.add("vector", lambda e: e.memset(ones1[:], 1.0), inc=sv)
    w_ones = p.now(sv)
    p.add("sync", lambda e: e.dma_start(out=idsb[:], in_=idn),
          inc=dx, inc_by=16)
    w_id = p.now(dx)

    cc_wait = d.get("cc_wait")
    tex_all = d.get("tex_all")
    if tex_all is not None:
        for half in range(2):
            def tdma(e, half=half):
                return e.dma_start(
                    out=texsb[half * 64:(half + 1) * 64, :].rearrange(
                        "p (g n) -> p g n", g=4),
                    in_=tex_all[0:256, :].rearrange(
                        "(g p) n -> p g n", p=64))
            p.add("gpsimd", tdma,
                  waits=(cc_wait,) if (cc_wait and half == 0) else (),
                  inc=gx, inc_by=16)
        w_tex0 = p.now(gx)
        p.add("sync", lambda e: e.dma_start(out=lsm[:, 30:31],
                                            in_=d["oddmask"]),
              inc=dx, inc_by=16)
        w_om = p.now(dx)
        p.add("vector", lambda e: e.tensor_scalar_mul(
            out=texsb[:], in0=texsb[:], scalar1=lsm[:, 30:31]),
            waits=((gx, w_tex0), (dx, w_om)), inc=sv)
        tex_ready = ("sv", p.now(sv))
    else:
        p.add("gpsimd", lambda e: e.dma_start(out=texsb[:], in_=texd),
              inc=gx, inc_by=16)
        tex_ready = None
    p.add("sync", lambda e: e.dma_start(out=selsb[:], in_=selm),
          inc=dx, inc_by=16)
    p.add("gpsimd", lambda e: e.dma_start(
        out=gwsb[:].rearrange("p (c n) -> p c n", c=8),
        in_=gw.rearrange("(c p) n -> p c n", p=128)), inc=gx, inc_by=16)
    p.add("gpsimd", lambda e: e.dma_start(
        out=wosb[:].rearrange("p (c n) -> p c n", c=8),
        in_=wo.rearrange("(c p) n -> p c n", p=128)), inc=gx, inc_by=16)
    w_pre = p.now(gx)
    p.add("vector", lambda e: e.memset(ones1b[:], 1.0), inc=sv)
    for i, (src, dst) in enumerate(((bo, bo_b), (gam, gam_b),
                                    (bet, bet_b))):
        def rdma(e, src=src):
            return e.dma_start(out=rowbuf[:], in_=src)
        p.add("sync", rdma, waits=L.guard("rowbuf"), inc=dx, inc_by=16)
        w_r = p.now(dx)
        for h2 in range(2):
            def bcm(e, h2=h2):
                return e.matmul(
                    out=pb[h2][:], lhsT=ones1[:],
                    rhs=rowbuf[0:1, h2 * 512:(h2 + 1) * 512],
                    start=True, stop=True)
            p.add("tensor", bcm,
                  waits=((sv, w_ones), (dx, w_r)) + L.guard(("pb", h2)),
                  inc=st)
            w_b = p.now(st)
            def bcp(e, dst=dst, h2=h2):
                return e.tensor_copy(out=dst[:, h2 * 512:(h2 + 1) * 512],
                                     in_=pb[h2][:])
            p.add("vector", bcp, waits=((st, w_b),), inc=sv)
            L.set_guard(("pb", h2), sv, p.now(sv))
        L.set_guard("rowbuf", st, p.now(st))
    bcast_rdy = p.now(sv)
    bcast_st = p.now(st)

    for j in range(16):
        def d1(e, j=j):
            return e.dma_start(out=x1b[:],
                               in_=x1i[j * 128:(j + 1) * 128, :])
        p.add("sync", d1, waits=L.guard("x1b"), inc=dx, inc_by=16)
        def d2(e, j=j):
            return e.dma_start(out=nmx[:],
                               in_=numx[j * 128:(j + 1) * 128, :])
        p.add("sync", d2, waits=L.guard("nmx"), inc=dx, inc_by=16)
        w_d12 = p.now(dx)
        def d3(e, j=j):
            return e.dma_start(
                out=qfb[:].rearrange("p (c n) -> p c n", c=8),
                in_=qft[:, j * 128:(j + 1) * 128].rearrange(
                    "(c p) n -> p c n", p=128))
        p.add("gpsimd", d3, waits=L.guard("qfb"), inc=gx, inc_by=16)
        w_d3 = p.now(gx)

        w_ln = emit_ln_tiles(L, [(x1b[:], xn1tok[:], ln_sc(0))],
                             waits=((dx, w_d12), (st, bcast_st)) +
                             L.guard("xn1tok"),
                             eps_ap=eps5)
        p.add("vector", lambda e: e.tensor_copy(out=xnbf[:], in_=xn1tok[:]),
              waits=((sv, w_ln),), inc=sv)
        w_xbf = p.now(sv)
        for ci in range(8):
            def tfn(e, ci=ci, j=j):
                return e.transpose(
                    out=pt0[:, 0:128],
                    in_=xnbf[:, ci * 128:(ci + 1) * 128],
                    identity=idsb[:])
            p.add("tensor", tfn,
                  waits=((sv, w_xbf), (dx, w_id)) + L.guard("ptD"),
                  inc=st)
            w_t = p.now(st)
            def cfn(e, ci=ci, j=j):
                return e.tensor_copy(
                    out=xn1T[:, ci * TOK + j * 128:ci * TOK + (j + 1) * 128],
                    in_=pt0[:, 0:128])
            p.add("vector", cfn, waits=((st, w_t),), inc=sv)
            L.set_guard("ptD", sv, p.now(sv))

        for h in range(16):
            g = h // 4
            hb = (h % 2) * 64
            qlh = qfb[hb:hb + 64, (h // 2) * 128:(h // 2 + 1) * 128]
            nb = pn[h % 2]
            def cmm(e, qlh=qlh, g=g, hb=hb, nb=nb):
                return e.matmul(
                    out=nb[:], lhsT=qlh,
                    rhs=texsb[hb:hb + 64, g * 65:(g + 1) * 65],
                    start=True, stop=True)
            cw_ = ((gx, max(w_d3, w_pre)),) + L.guard(("pn", h % 2))
            if tex_ready is not None:
                cw_ = cw_ + ((sv, tex_ready[1]),)
            p.add("tensor", cmm, waits=cw_, inc=st)
            w_c = p.now(st)
            dst = nmx[:, h * 65:(h + 1) * 65]
            def nadd(e, dst=dst, nb=nb):
                return e.tensor_add(out=dst, in0=dst, in1=nb[:])
            p.add("vector", nadd,
                  waits=((st, w_c), (dx, w_d12)), inc=sv)
            L.set_guard(("pn", h % 2), sv, p.now(sv))
        w_nm = p.now(sv)
        L.set_guard("qfb", st, p.now(st))
        p.add("vector", lambda e: e.tensor_scalar_add(
            out=dnbuf[:], in0=nmx[:].rearrange(
                "p (h n) -> p h n", h=16)[:, :, 64],
            scalar1=1e-6), waits=((sv, w_nm),), inc=sv)
        w_dn = p.now(sv)
        p.add("vector", lambda e: e.reciprocal(out=rcbuf[:], in_=dnbuf[:]),
              waits=((sv, w_dn),), inc=sv)
        w_rc = p.now(sv)
        for h in range(16):
            def odiv(e, h=h):
                return e.tensor_scalar_mul(
                    out=obuf[:, h * 64:(h + 1) * 64],
                    in0=nmx[:, h * 65:h * 65 + 64],
                    scalar1=rcbuf[:, h:h + 1])
            p.add("vector", odiv,
                  waits=((sv, w_rc),) + L.guard("obuf"), inc=sv)
        w_o = p.now(sv)
        L.set_guard("nmx", sv, w_o)

        for ci in range(8):
            def tfn(e, ci=ci):
                return e.transpose(
                    out=pt0[:, 0:128],
                    in_=obuf[:, ci * 128:(ci + 1) * 128],
                    identity=idsb[:])
            p.add("tensor", tfn,
                  waits=((sv, w_o),) + L.guard("ptD"), inc=st)
            w_t = p.now(st)
            def cfn(e, ci=ci):
                return e.tensor_copy(
                    out=oTch[:, ci * 128:(ci + 1) * 128],
                    in_=pt0[:, 0:128])
            p.add("vector", cfn,
                  waits=((st, w_t),) + L.guard("oTch"), inc=sv)
            L.set_guard("ptD", sv, p.now(sv))
        w_oT = p.now(sv)
        L.set_guard("obuf", sv, w_oT)

        for f2 in range(2):
            bk_ = 2 + f2
            for ci in range(8):
                w = ()
                if ci == 0:
                    w = ((sv, w_oT), (gx, w_pre)) + L.guard(("pb", bk_))
                def mfn(e, f2=f2, ci=ci, bk_=bk_):
                    return e.matmul(
                        out=pb[bk_][:],
                        lhsT=oTch[:, ci * 128:(ci + 1) * 128],
                        rhs=wosb[:, ci * D + f2 * 512:ci * D + (f2 + 1) * 512],
                        start=(ci == 0), stop=(ci == 7))
                p.add("tensor", mfn, waits=w, inc=st if ci == 7 else None)
            w_mm = p.now(st)
            cs = slice(f2 * 512, (f2 + 1) * 512)
            def v1(e, cs=cs, bk_=bk_):
                return e.tensor_add(out=ytmp[:, cs], in0=pb[bk_][:],
                                    in1=xn1tok[:, cs])
            p.add("vector", v1,
                  waits=((st, w_mm), (sv, bcast_rdy)) + L.guard("ytmp"),
                  inc=sv)
            w_v1 = p.now(sv)
            L.set_guard(("pb", bk_), sv, w_v1)
            def v2(e, cs=cs):
                return e.tensor_add(out=ybuf[:, cs], in0=ytmp[:, cs],
                                    in1=bo_b[:, cs])
            p.add("vector", v2, waits=((sv, w_v1),) + L.guard("ybuf"),
                  inc=sv)
        w_y = p.now(sv)
        L.set_guard("ytmp", sv, w_y)
        L.set_guard("xn1tok", sv, w_y)
        w_s = emit_ln_tiles(L, [(ybuf[:], sstage[:], ln_sc(1))],
                            waits=((sv, w_y),) + L.guard("sstage"),
                            eps_ap=eps5,
                            gamma_beta=(ytmp[:], gam_b[:], bet_b[:]))
        L.set_guard("ybuf", sv, w_s)
        L.set_guard("ytmp", sv, w_s)
        def sdma(e, j=j):
            return e.dma_start(out=s_scr[j * 128:(j + 1) * 128, :],
                               in_=sstage[:])
        p.add("sync", sdma, waits=((sv, w_s),), inc=so, inc_by=16)
        L.set_guard("sstage", so, p.now(so))
        L.set_guard("x1b", sv, w_ln)
    xn1T_rdy = p.now(sv)

    for j in range(16):
        for ci in range(8):
            w = ()
            if ci == 0:
                w = ((sv, xn1T_rdy), (gx, w_pre)) + L.guard(("pn", 0))
            def lmm(e, j=j, ci=ci):
                return e.matmul(
                    out=pnb[0][:, 0:8],
                    lhsT=xn1T[:, ci * TOK + j * 128:ci * TOK + (j + 1) * 128],
                    rhs=gwsb[:, ci * 8:(ci + 1) * 8],
                    start=(ci == 0), stop=(ci == 7))
            p.add("tensor", lmm, waits=w, inc=st if ci == 7 else None)
        w_mm = p.now(st)
        def lev(e, j=j):
            return e.tensor_copy(out=lgbuf[:, j * 8:(j + 1) * 8],
                                 in_=pnb[0][:, 0:8])
        p.add("vector", lev, waits=((st, w_mm),), inc=sv)
        L.set_guard(("pn", 0), sv, p.now(sv))
    w_lg = p.now(sv)
    p.add("scalar", lambda e: e.activation(out=ebuf[:], in_=lgbuf[:],
                                           func=AF.Exp),
          waits=((sv, w_lg),), inc=ss)
    w_e = p.now(ss)
    p.add("vector", lambda e: e.tensor_reduce(
        out=smb[:], in_=ebuf[:].rearrange("p (j n) -> p j n", j=16),
        axis=AX.X, op=ALU.add), waits=((ss, w_e),), inc=sv)
    w_sm = p.now(sv)
    p.add("vector", lambda e: e.reciprocal(out=rnb[:], in_=smb[:]),
          waits=((sv, w_sm),), inc=sv)
    w_rn = p.now(sv)
    for j in range(16):
        p.add("vector", lambda e, j=j: e.tensor_scalar_mul(
            out=ebuf[:, j * 8:(j + 1) * 8],
            in0=ebuf[:, j * 8:(j + 1) * 8], scalar1=rnb[:, j:j + 1]),
            waits=((sv, w_rn),) if j == 0 else (), inc=sv)
    w_pr = p.now(sv)
    p.add("vector", lambda e: e.tensor_reduce(
        out=m1b[:], in_=ebuf[:].rearrange("p (j n) -> p j n", j=16),
        axis=AX.X, op=ALU.max), waits=((sv, w_pr),), inc=sv)
    w_m1 = p.now(sv)
    for j in range(16):
        p.add("vector", lambda e, j=j: e.tensor_single_scalar(
            out=mskb[:], in_=ebuf[:, j * 8:(j + 1) * 8],
            scalar=m1b[:, j:j + 1], op=ALU.is_equal),
            waits=((sv, w_m1),) + L.guard("mskb"), inc=sv)
        w_mk = p.now(sv)
        p.add("vector", lambda e, j=j: e.tensor_mul(
            out=pmb[:, j * 8:(j + 1) * 8],
            in0=ebuf[:, j * 8:(j + 1) * 8], in1=mskb[:]),
            waits=((sv, w_mk),), inc=sv)
        w_pm = p.now(sv)
        L.set_guard("mskb", sv, w_pm)
        p.add("vector", lambda e, j=j: e.tensor_sub(
            out=p2b[:, j * 8:(j + 1) * 8],
            in0=ebuf[:, j * 8:(j + 1) * 8],
            in1=pmb[:, j * 8:(j + 1) * 8]),
            waits=((sv, w_pm),), inc=sv)
    w_p2 = p.now(sv)
    p.add("vector", lambda e: e.tensor_reduce(
        out=m2b[:], in_=p2b[:].rearrange("p (j n) -> p j n", j=16),
        axis=AX.X, op=ALU.max), waits=((sv, w_p2),), inc=sv)
    w_m2 = p.now(sv)
    for j in range(16):
        p.add("vector", lambda e, j=j: e.tensor_single_scalar(
            out=mskb[:], in_=p2b[:, j * 8:(j + 1) * 8],
            scalar=m2b[:, j:j + 1], op=ALU.is_equal),
            waits=((sv, w_m2),) + L.guard("mskb"), inc=sv)
        w_mk = p.now(sv)
        p.add("vector", lambda e, j=j: e.tensor_mul(
            out=p2b[:, j * 8:(j + 1) * 8],
            in0=p2b[:, j * 8:(j + 1) * 8], in1=mskb[:]),
            waits=((sv, w_mk),), inc=sv)
        L.set_guard("mskb", sv, p.now(sv))
    w_pm2 = p.now(sv)
    p.add("vector", lambda e: e.tensor_add(out=m1b[:], in0=m1b[:],
                                           in1=m2b[:]),
          waits=((sv, w_pm2),), inc=sv)
    w_dsum = p.now(sv)
    p.add("vector", lambda e: e.reciprocal(out=rnb[:], in_=m1b[:]),
          waits=((sv, w_dsum),), inc=sv)
    w_rn2 = p.now(sv)
    p.add("vector", lambda e: e.tensor_add(out=cwbuf[:], in0=pmb[:],
                                           in1=p2b[:]),
          waits=((sv, w_rn2),), inc=sv)
    w_cu = p.now(sv)
    for j in range(16):
        p.add("vector", lambda e, j=j: e.tensor_scalar_mul(
            out=cwbuf[:, j * 8:(j + 1) * 8],
            in0=cwbuf[:, j * 8:(j + 1) * 8], scalar1=rnb[:, j:j + 1]),
            waits=((sv, w_cu),) if j == 0 else (), inc=sv)
    w_cw0 = p.now(sv)
    p.add("vector", lambda e: e.tensor_copy(out=cwbf[:], in_=cwbuf[:]),
          waits=((sv, w_cw0),), inc=sv)
    w_cw = p.now(sv)
    for j in range(16):
        def tfn(e, j=j):
            return e.transpose(out=ptb[0:8, 0:128],
                               in_=cwbf[:, j * 8:(j + 1) * 8],
                               identity=idsb[:])
        p.add("tensor", tfn, waits=((sv, w_cw),) + L.guard("ptD"), inc=st)
        w_t = p.now(st)
        def cfn(e, j=j):
            return e.tensor_copy(out=cwT[:, j * 128:(j + 1) * 128],
                                 in_=ptb[0:8, 0:128])
        p.add("vector", cfn, waits=((st, w_t),), inc=sv)
        L.set_guard("ptD", sv, p.now(sv))
    cwT_rdy = p.now(sv)

    def tail_j(j, acc_wait):
        bx, bs, bg, bo, bt, by = (x1b, sstage, gbuf16, obuf, ytmp, ybuf)
        gk = 0
        def d1(e, j=j, bx=bx):
            return e.dma_start(out=bx[:],
                               in_=x1i[j * 128:(j + 1) * 128, :])
        p.add("sync", d1, waits=L.guard("x1b"), inc=dx, inc_by=16)
        def d2(e, j=j, bs=bs):
            return e.dma_start(out=bs[:],
                               in_=s_scr[j * 128:(j + 1) * 128, :])
        p.add("sync", d2, waits=L.guard("sstage"), inc=dx, inc_by=16)
        def d3(e, j=j, bg=bg):
            return e.dma_start(
                out=bg[:].rearrange("p (c n) -> p c n", c=8),
                in_=acc_scr[:, j * 128:(j + 1) * 128].rearrange(
                    "(c p) n -> p c n", p=128))
        p.add("gpsimd", d3, waits=(acc_wait,) + L.guard("gbuf16"),
              inc=gx, inc_by=16)
        w_d = p.now(dx)
        w_g0 = p.now(gx)
        def gsil(e, bg=bg):
            return e.activation(out=bg[:], in_=bg[:], func=AF.Silu)
        p.add("scalar", gsil, waits=((gx, w_g0),), inc=ss)
        w_gs = p.now(ss)
        for ci in range(8):
            def tfn(e, ci=ci, bg=bg):
                return e.transpose(
                    out=pt0[:, 0:128],
                    in_=bg[:, ci * 128:(ci + 1) * 128],
                    identity=idsb[:])
            p.add("tensor", tfn,
                  waits=((ss, w_gs),) + L.guard("ptD"), inc=st)
            w_t = p.now(st)
            def cfn(e, ci=ci, bo=bo):
                return e.tensor_copy(out=bo[:, ci * 128:(ci + 1) * 128],
                                     in_=pt0[:, 0:128])
            p.add("vector", cfn,
                  waits=((st, w_t),) +
                  (L.guard("obuf") if ci == 0 else ()),
                  inc=sv)
            L.set_guard("ptD", sv, p.now(sv))
        w_g = p.now(sv)
        L.set_guard("gbuf16", st, p.now(st))
        p.add("vector", lambda e, bt=bt, bs=bs, bo=bo: e.tensor_mul(
            out=bt[:], in0=bs[:], in1=bo[:]),
              waits=((sv, w_g), (dx, w_d)) + L.guard("ytmp"), inc=sv)
        w_t1 = p.now(sv)
        L.set_guard("obuf", sv, w_t1)
        L.set_guard("sstage", sv, w_t1)
        p.add("vector", lambda e, by=by, bx=bx, bt=bt: e.tensor_add(
            out=by[:], in0=bx[:], in1=bt[:]),
              waits=((sv, w_t1),) + L.guard("ybuf"), inc=sv)
        w_t2 = p.now(sv)
        L.set_guard("ytmp", sv, w_t2)
        L.set_guard("x1b", sv, w_t2)
        def ydma(e, j=j, by=by):
            return e.dma_start(out=yout[j * 128:(j + 1) * 128, :],
                               in_=by[:])
        p.add("sync", ydma, waits=((sv, w_t2),), inc=so, inc_by=16)
        L.set_guard("ybuf", so, p.now(so))

    NCH = 11
    aw_acc = {}
    for e9 in range(9):
        if e9 > 0:
            for tc in range(4):
                def bcm(e, e9=e9, tc=tc):
                    return e.matmul(
                        out=pb[3][:],
                        lhsT=selsb[:, (e9 - 1) * 128:e9 * 128],
                        rhs=cwT[:, tc * 512:(tc + 1) * 512],
                        start=True, stop=True)
                p.add("tensor", bcm,
                      waits=((sv, cwT_rdy),) + L.guard(("pb", 3)), inc=st)
                w_b = p.now(st)
                def bcp(e, tc=tc):
                    return e.tensor_copy(
                        out=cwb[:, tc * 512:(tc + 1) * 512], in_=pb[3][:])
                p.add("vector", bcp,
                      waits=((st, w_b),) + L.guard("cwb"), inc=sv)
                L.set_guard(("pb", 3), sv, p.now(sv))
        cwb_rdy = p.now(sv)
        wdw0 = L.guard("wdsb")
        if e9 == 0:
            wdw0 = wdw0 + ((st, p.now(st)),)
        w_wd = None
        for tp in range(2):
            for ch in range(NCH):
                par = ch % 2
                wgv = wgbuf[:, par * 8 * 512:(par + 1) * 8 * 512]
                def wgdma(e, e9=e9, ch=ch, wgv=wgv):
                    return e.dma_start(
                        out=wgv.rearrange("p (c n) -> p c n", c=8),
                        in_=wgu[e9 * D:(e9 + 1) * D,
                                ch * 512:(ch + 1) * 512].rearrange(
                            "(c p) n -> p c n", p=128))
                p.add("gpsimd", wgdma,
                      waits=L.guard(("wgbuf", par)), inc=gx, inc_by=16)
                w_wg = p.now(gx)
                if tp == 0:
                    def wdp(e, e9=e9, k=ch):
                        return e.dma_start(
                            out=wdsb[:, k * 2 * D:(k + 1) * 2 * D].rearrange(
                                "p (c n) -> p c n", c=2),
                            in_=wd[e9 * 2816 + k * 256:
                                   e9 * 2816 + (k + 1) * 256, :].rearrange(
                                "(c p) n -> p c n", p=128))
                    p.add("gpsimd", wdp, waits=wdw0 if ch == 0 else (),
                          inc=gx, inc_by=16)
                    if ch == NCH - 1:
                        w_wd = p.now(gx)
                for f1 in range(4):
                    pdst = pq[f1 % 2]
                    for ci in range(8):
                        w = ()
                        if ci == 0:
                            w = ((gx, w_wg), (sv, xn1T_rdy)) + \
                                L.guard(("pq", f1 % 2))
                        for th in range(2):
                            def gmm(e, f1=f1, ci=ci, th=th, tp=tp,
                                    wgv=wgv, pdst=pdst):
                                return e.matmul(
                                    out=pdst[:, th * 512:(th + 1) * 512],
                                    lhsT=wgv[:, ci * 512 + f1 * 128:
                                             ci * 512 + (f1 + 1) * 128],
                                    rhs=xn1T[:, ci * TOK + tp * 1024 +
                                             th * 512:
                                             ci * TOK + tp * 1024 +
                                             (th + 1) * 512],
                                    start=(ci == 0), stop=(ci == 7))
                            p.add("tensor", gmm, waits=w if th == 0 else (),
                                  inc=st if (ci == 7 and th == 1) else None)
                            w = ()
                    w_mm = p.now(st)
                    if f1 < 2:
                        def sev(e, f1=f1, pdst=pdst):
                            return e.activation(
                                out=tgbuf[:, f1 * 1024:(f1 + 1) * 1024],
                                in_=pdst[:], func=AF.Silu)
                        p.add("scalar", sev,
                              waits=((st, w_mm),) +
                              (L.guard("tgbuf") if f1 == 0 else ()),
                              inc=ss)
                        L.set_guard(("pq", f1 % 2), ss, p.now(ss))
                    else:
                        hidx = 2 * ch + (f1 - 2)
                        def hev(e, f1=f1, hidx=hidx, pdst=pdst):
                            return e.tensor_mul(
                                out=hbuf[:, hidx * 1024:(hidx + 1) * 1024],
                                in0=tgbuf[:, (f1 - 2) * 1024:
                                          (f1 - 1) * 1024],
                                in1=pdst[:])
                        p.add("vector", hev,
                              waits=((st, w_mm), (ss, p.now(ss))) +
                              L.guard("hbuf"), inc=sv)
                        L.set_guard(("pq", f1 % 2), sv, p.now(sv))
                L.set_guard(("wgbuf", par), st, p.now(st))
                L.set_guard("tgbuf", sv, p.now(sv))
                if e9 == 8 and tp == 1 and 1 <= ch <= 8:
                    tail_j(ch - 1, aw_acc[0])
            w_h = p.now(sv)
            for oi in range(8):
                for c22 in range(22):
                    w = ()
                    if c22 == 0:
                        w = ((sv, w_h), (gx, w_wd)) + L.guard("pnb")
                    for th in range(2):
                        def dmm(e, oi=oi, c22=c22, th=th):
                            return e.matmul(
                                out=pnb[th][:],
                                lhsT=wdsb[:, c22 * D + oi * 128:
                                          c22 * D + (oi + 1) * 128],
                                rhs=hbuf[:, c22 * 1024 + th * 512:
                                         c22 * 1024 + (th + 1) * 512],
                                start=(c22 == 0), stop=(c22 == 21))
                        p.add("tensor", dmm, waits=w if th == 0 else (),
                              inc=st if (c22 == 21 and th == 1) else None)
                        w = ()
                w_mm = p.now(st)
                tpar = oi % 2
                tbv = tbm[:, tpar * 1024:(tpar + 1) * 1024]
                if e9 == 0:
                    def tev(e, tbv=tbv):
                        return e.tensor_copy(out=tbv, in_=pnb01[:])
                    p.add("vector", tev,
                          waits=((st, w_mm),) + L.guard(("tbm", tpar)),
                          inc=sv)
                else:
                    def tev(e, tbv=tbv, tp=tp):
                        return e.tensor_mul(
                            out=tbv, in0=pnb01[:],
                            in1=cwb[:, tp * 1024:(tp + 1) * 1024])
                    p.add("vector", tev,
                          waits=((st, w_mm), (sv, cwb_rdy)) +
                          L.guard(("tbm", tpar)), inc=sv)
                w_tb = p.now(sv)
                L.set_guard("pnb", sv, w_tb)
                def adma(e, oi=oi, tp=tp, e9=e9, tbv=tbv):
                    return e.dma_start(
                        out=acc_scr[oi * 128:(oi + 1) * 128,
                                    tp * 1024:(tp + 1) * 1024],
                        in_=tbv,
                        accum_op=(ALU.bypass if e9 == 0 else ALU.add))
                p.add("gpsimd", adma,
                      waits=((sv, w_tb),) + L.guard(("accr", oi, tp)),
                      inc=gx, inc_by=16)
                L.set_guard(("accr", oi, tp), gx, p.now(gx))
                L.set_guard(("tbm", tpar), gx, p.now(gx))
            if e9 == 8:
                aw_acc[tp] = (gx, p.now(gx))
                if tp == 1:
                    for j in range(8, 16):
                        tail_j(j, aw_acc[1])
        if e9 > 0:
            L.set_guard("cwb", gx, p.now(gx))
        L.set_guard("wdsb", st, p.now(st))

    p.add("sync", lambda e: e.nop(), waits=((so, p.now(so)),))


def build_launch_D():
    L = Launch()
    d = {}
    d["x1i"] = L.inp("x1i", [TOK, D])
    d["qft"] = L.inp("qft", [D, TOK])
    d["numx"] = L.inp("numx", [TOK, 16 * 65])
    d["texd"] = L.inp("texd", [128, 260])
    d["wo"] = L.inp("wo", [D, D])
    d["bo"] = L.inp("bo", [1, D])
    d["gam"] = L.inp("gam", [1, D])
    d["bet"] = L.inp("bet", [1, D])
    d["gw"] = L.inp("gw", [D, 8])
    d["selm"] = L.inp("selm", [8, 8 * 128], BF16)
    d["wgu"] = L.inp("wgu", [9 * D, 5632])
    d["wd"] = L.inp("wd", [9 * 2816, D])
    d["idn"] = L.inp("idn", [128, 128], BF16)
    d["yout"] = L.out("yout", [TOK, D])
    d["s_scr"] = L.scr("s_scr", [TOK, D], BF16)
    d["acc_scr"] = L.scr("acc_scr", [D, TOK], BF16)
    with L.es:
        phase_D(L, d)
        L.p.emit()
    return L.nc





def phase_B(L, d):
    nc, p = L.nc, L.p
    s0p0 = d["s0p0"]
    s0p1 = d["s0p1"]
    x_full = d["x_full"]
    bp = d["bp"]
    oddmask = d["oddmask"]
    wk = d["wk"]; bk = d["bk"]; wv = d["wv"]; bv = d["bv"]
    wc = d["wc"]; we = d["we"]
    w0t = d["w0t"]
    xnt_scr = d["xnt"]
    idn = d["idn"]
    x1own_scr = d["x1i"]
    base_scr = d["base_scr"]

    sb, ps = L.sb, L.ps
    pbf = sb("pbf", [128, 4 * D])
    pbs = sb("pbs", [128, 4 * D])
    gbf = sb("gbf", [128, 4 * D])
    gbs = sb("gbs", [128, 4 * D])
    sgnT = sb("sgnT", [128, 8 * D], BF16)
    xbf = sb("xbf", [128, 4 * D])
    xbs = sb("xbs", [128, 4 * D])
    bpb = sb("bpb", [128, D])
    bpsb = sb("bpsb", [1, D])
    msk = sb("msk", [128, 1])
    xn_bf = sb("xn_bf", [128, 4 * D], BF16)
    xnT = sb("xnT", [128, 8 * 512], BF16)
    wksb = sb("wksb", [128, 8 * 256], BF16)
    wvsb = sb("wvsb", [128, 8 * 256], BF16)
    wcsb = sb("wcsb", [128, 2 * 64], BF16)
    wesb = sb("wesb", [64, 256], BF16)
    bksb = sb("bksb", [128, 2])
    bvsb = sb("bvsb", [128, 2])
    idsb = sb("idsb", [128, 128], BF16)
    qtmp = sb("qtmp", [128, 512])
    t1 = sb("t1", [128, 512])
    t2 = sb("t2", [128, 512])
    t3 = sb("t3", [128, 512])
    kraw = sb("kraw", [128, 2 * 512], BF16)
    vraw = sb("vraw", [128, 2 * 512], BF16)
    ckT = sb("ckT", [64, 512], BF16)
    cvT = sb("cvT", [64, 512], BF16)
    kfT = sb("kfT", [128, 2 * 512], BF16)
    kftok = sb("kftok", [128, 4 * 256], BF16)
    v2e = sb("v2e", [128, 4 * 260], BF16)
    base32 = sb("base32", [64, 260])
    lsq = sb("lsq", [128, D])
    lsm = sb("lsm", [128, 32])
    consts = sb("consts", [128, 4])

    pb = [ps(f"pb{i}", [128, 512]) for i in range(4)]
    pSb = ps("pS", [128, 512])
    pS = pSb[0:64, 0:260]
    ptb = ps("pt0", [128, 2048], BF16)
    pt0 = ptb[:, 0:512]
    ptP = [ptb[:, 0:512], ptb[:, 1024:1536]]

    dx = L.sem("dx")
    gx = L.sem("gx")
    st = L.sem("st")
    sv = L.sem("sv")
    ss = L.sem("ss")
    so = L.sem("so")
    L.sv, L.ss = sv, ss
    eps5 = consts[:, 0:1]

    def ln_sc(i):
        return {"sq": lsq, "s": lsm[:, 6 * i:6 * i + 6]}

    p.add("vector", lambda e: e.memset(eps5, 1e-5), inc=sv)
    p.add("vector", lambda e: e.memset(v2e[:], 1.0), inc=sv)
    p.add("vector", lambda e: e.memset(base32[:], 0.0), inc=sv)
    p.add("vector", lambda e: e.memset(qtmp[0:1, 0:128], 1.0), inc=sv)
    w_ones = p.now(sv)
    p.add("sync", lambda e: e.dma_start(out=idsb[:], in_=idn), inc=dx, inc_by=16)
    p.add("sync", lambda e: e.dma_start(out=bpsb[:], in_=bp), inc=dx, inc_by=16)
    p.add("sync", lambda e: e.dma_start(out=msk[:], in_=oddmask), inc=dx, inc_by=16)
    for bsrc, bdst in ((bk, bksb), (bv, bvsb)):
        p.add("sync", lambda e, s=bsrc, dd=bdst: e.dma_start(out=dd[:], in_=s),
              inc=dx, inc_by=16)
    w_small = p.now(dx)
    p.add("gpsimd", lambda e: e.dma_start(
        out=wksb[:].rearrange("p (c n) -> p c n", c=8),
        in_=wk.rearrange("(c p) n -> p c n", p=128)), inc=gx, inc_by=16)
    p.add("gpsimd", lambda e: e.dma_start(
        out=wvsb[:].rearrange("p (c n) -> p c n", c=8),
        in_=wv.rearrange("(c p) n -> p c n", p=128)), inc=gx, inc_by=16)
    p.add("gpsimd", lambda e: e.dma_start(
        out=wcsb[:].rearrange("p (c n) -> p c n", c=2),
        in_=wc.rearrange("(c p) n -> p c n", p=128)), inc=gx, inc_by=16)
    p.add("gpsimd", lambda e: e.dma_start(out=wesb[:], in_=we), inc=gx, inc_by=16)
    p.add("gpsimd", lambda e: e.dma_start(
        out=sgnT[:].rearrange("p (c n) -> p c n", c=8),
        in_=w0t.rearrange("(c p) n -> p c n", p=128)), inc=gx, inc_by=16)
    w_wdma = p.now(gx)
    p.add("scalar", lambda e: e.activation(out=sgnT[:], in_=sgnT[:],
                                           func=AF.Sign),
          waits=((gx, w_wdma),), inc=ss)
    w_sgn = p.now(ss)
    for h2 in range(2):
        def bcm(e, h2=h2):
            return e.matmul(out=pb[h2][:], lhsT=qtmp[0:1, 0:128],
                            rhs=bpsb[0:1, h2 * 512:(h2 + 1) * 512],
                            start=True, stop=True)
        p.add("tensor", bcm, waits=((sv, w_ones), (dx, w_small)), inc=st)
        w_b = p.now(st)
        p.add("vector", lambda e, h2=h2: e.tensor_copy(
            out=bpb[:, h2 * 512:(h2 + 1) * 512], in_=pb[h2][:]),
            waits=((st, w_b),), inc=sv)
    bpb_rdy = p.now(sv)

    def ln_to_xnT(buf, dmawaits):
        tiles = [(buf[:, t * D:(t + 1) * D], xn_bf[:, t * D:(t + 1) * D],
                  ln_sc(t)) for t in range(4)]
        w_apply = emit_ln_tiles(L, tiles, waits=dmawaits, eps_ap=eps5)
        for ci in range(8):
            for t in range(4):
                def tfn(e, ci=ci, t=t):
                    return e.transpose(
                        out=pt0[:, t * 128:(t + 1) * 128],
                        in_=xn_bf[:, t * D + ci * 128:t * D + ci * 128 + 128],
                        identity=idsb[:])
                p.add("tensor", tfn,
                      waits=(((sv, w_apply), (dx, w_small)) + L.guard("ptB")
                             if t == 0 else ()),
                      inc=st if t == 3 else None)
            w_t = p.now(st)
            def cfn(e, ci=ci):
                return e.tensor_copy(out=xnT[:, ci * 512:(ci + 1) * 512],
                                     in_=pt0[:])
            p.add("vector", cfn, waits=((st, w_t),), inc=sv)
            L.set_guard("ptB", sv, p.now(sv))
        return w_apply, p.now(sv)

    def g0_block(gb, xn_wait, gkey):
        for tt in range(4):
            for f2 in range(2):
                bk_ = (tt * 2 + f2) % 2
                for ci in range(8):
                    w = ()
                    if ci == 0:
                        w = (xn_wait, (ss, w_sgn)) + \
                            L.guard(("pb", bk_))
                    def mfn(e, tt=tt, f2=f2, ci=ci, bk_=bk_):
                        return e.matmul(
                            out=pb[bk_][:],
                            lhsT=xnT[:, ci * 512 + tt * 128:
                                     ci * 512 + (tt + 1) * 128],
                            rhs=sgnT[:, ci * D + f2 * 512:
                                     ci * D + (f2 + 1) * 512],
                            start=(ci == 0), stop=(ci == 7))
                    p.add("tensor", mfn, waits=w, inc=st if ci == 7 else None)
                w_mm = p.now(st)
                dst = gb[:, tt * D + f2 * 512:tt * D + f2 * 512 + 512]
                def ev(e, dst=dst, bk_=bk_):
                    return e.activation(out=dst, in_=pb[bk_][:], func=AF.Silu)
                p.add("scalar", ev, waits=((st, w_mm),) + L.guard(gkey),
                      inc=ss)
                L.set_guard(("pb", bk_), ss, p.now(ss))
        return p.now(ss)

    for c in range(4):
        rf = slice(c * 512, (c + 1) * 512)
        rs = slice(2048 + c * 512, 2048 + (c + 1) * 512)
        for (dstb, r0) in ((xbf, rf), (xbs, rs)):
            def dxm(e, dstb=dstb, r0=r0):
                return e.dma_start(
                    out=dstb[:].rearrange("p (t n) -> p t n", t=4),
                    in_=x_full[r0, :].rearrange("(t p) n -> p t n", p=128))
            p.add("sync", dxm, waits=L.guard("pb_bufs"), inc=dx, inc_by=16)
        w_dx = p.now(dx)
        for (dstb, r0) in ((pbf, rf), (pbs, rs)):
            def dp1(e, dstb=dstb, r0=r0):
                return e.dma_start(
                    out=dstb[:].rearrange("p (t n) -> p t n", t=4),
                    in_=s0p0[r0, :].rearrange("(t p) n -> p t n", p=128))
            p.add("gpsimd", dp1, waits=L.guard("pb_bufs"), inc=gx, inc_by=16)
            w1_ = p.now(gx)
            def dp2(e, dstb=dstb, r0=r0):
                return e.dma_start(
                    out=dstb[:].rearrange("p (t n) -> p t n", t=4),
                    in_=s0p1[r0, :].rearrange("(t p) n -> p t n", p=128),
                    accum_op=ALU.add)
            p.add("gpsimd", dp2, waits=((gx, w1_),), inc=gx, inc_by=16)
        w_d = p.now(gx)
        def xldf(e, c=c):
            return e.dma_start(out=xnT[:], in_=xnt_scr[c])
        p.add("gpsimd", xldf, waits=L.guard("xnT_rd"), inc=gx, inc_by=16)
        gf_done = g0_block(gbf, (gx, p.now(gx)), "gbf")
        L.set_guard("xnT_rd", st, p.now(st))
        def xlds(e, c=c):
            return e.dma_start(out=xnT[:], in_=xnt_scr[c + 4])
        p.add("gpsimd", xlds, waits=L.guard("xnT_rd"), inc=gx, inc_by=16)
        gs_done = g0_block(gbs, (gx, p.now(gx)), "gbs")
        L.set_guard("xnT_rd", st, p.now(st))
        for buf in (pbf, pbs):
            for t in range(4):
                cs = slice(t * D, (t + 1) * D)
                p.add("vector", lambda e, buf=buf, cs=cs: e.tensor_add(
                    out=buf[:, cs], in0=buf[:, cs], in1=bpb[:]),
                    waits=((gx, w_d), (sv, bpb_rdy)), inc=sv)
        w_v1 = p.now(sv)
        for (buf, gb, gw_) in ((pbf, gbf, gf_done), (pbs, gbs, gs_done)):
            for t in range(4):
                cs = slice(t * D, (t + 1) * D)
                p.add("vector", lambda e, buf=buf, gb=gb, cs=cs: e.tensor_mul(
                    out=buf[:, cs], in0=buf[:, cs], in1=gb[:, cs]),
                    waits=((sv, w_v1), (ss, gw_)), inc=sv)
        w_v2 = p.now(sv)
        L.set_guard("gbf", sv, w_v2)
        L.set_guard("gbs", sv, w_v2)
        for (buf, xb) in ((pbf, xbf), (pbs, xbs)):
            for t in range(4):
                cs = slice(t * D, (t + 1) * D)
                p.add("vector", lambda e, buf=buf, xb=xb, cs=cs: e.tensor_add(
                    out=xb[:, cs], in0=xb[:, cs], in1=buf[:, cs]),
                    waits=((sv, w_v2), (dx, w_dx)), inc=sv)
        w_x1 = p.now(sv)
        for t in range(4):
            cs = slice(t * D, (t + 1) * D)
            p.add("vector", lambda e, cs=cs: e.tensor_sub(
                out=pbf[:, cs], in0=xbs[:, cs], in1=xbf[:, cs]),
                waits=((sv, w_x1),), inc=sv)
        w_s1 = p.now(sv)
        for t in range(4):
            cs = slice(t * D, (t + 1) * D)
            p.add("vector", lambda e, cs=cs: e.scalar_tensor_tensor(
                out=pbs[:, cs], in0=pbf[:, cs], scalar=msk[:, 0:1],
                in1=xbf[:, cs], op0=ALU.mult, op1=ALU.add),
                waits=((sv, w_s1), (dx, w_small)), inc=sv)
        w_own = p.now(sv)
        def o3(e, c=c):
            return e.dma_start(
                out=x1own_scr[c * 512:(c + 1) * 512, :].rearrange(
                    "(t p) n -> p t n", p=128),
                in_=pbs[:].rearrange("p (t n) -> p t n", t=4))
        p.add("sync", o3, waits=((sv, w_own),), inc=so, inc_by=16)
        w_so = p.now(so)

        tiles = [(xbf[:, t * D:(t + 1) * D], xn_bf[:, t * D:(t + 1) * D],
                  ln_sc(t)) for t in range(4)]
        w_apply = emit_ln_tiles(L, tiles, waits=((sv, w_x1),), eps_ap=eps5)
        L.set_guard("pb_bufs", so, w_so)
        for ci in range(8):
            bkp = ci % 2
            for t in range(4):
                def tfn(e, ci=ci, t=t, bkp=bkp):
                    return e.transpose(
                        out=ptP[bkp][:, t * 128:(t + 1) * 128],
                        in_=xn_bf[:, t * D + ci * 128:t * D + ci * 128 + 128],
                        identity=idsb[:])
                p.add("tensor", tfn,
                      waits=(((sv, w_apply), (dx, w_small)) +
                             L.guard(("ptB", bkp))
                             if t == 0 else ()),
                      inc=st if t == 3 else None)
            w_t = p.now(st)
            def cfn(e, ci=ci, bkp=bkp):
                return e.tensor_copy(out=xnT[:, ci * 512:(ci + 1) * 512],
                                     in_=ptP[bkp][:])
            p.add("vector", cfn, waits=((st, w_t),), inc=sv)
            L.set_guard(("ptB", bkp), sv, p.now(sv))
        xnT_done = p.now(sv)
        for (wsb, bsb, dstraw) in ((wksb, bksb, kraw), (wvsb, bvsb, vraw)):
            for f1 in range(2):
                bk_ = 2 + f1
                for ci in range(8):
                    w = ()
                    if ci == 0:
                        w = ((sv, xnT_done), (gx, w_wdma)) +                             L.guard(("pb", bk_))
                    def mfn(e, f1=f1, ci=ci, bk_=bk_, wsb=wsb):
                        return e.matmul(
                            out=pb[bk_][:],
                            lhsT=wsb[:, ci * 256 + f1 * 128:
                                     ci * 256 + (f1 + 1) * 128],
                            rhs=xnT[:, ci * 512:(ci + 1) * 512],
                            start=(ci == 0), stop=(ci == 7))
                    p.add("tensor", mfn, waits=w, inc=st if ci == 7 else None)
                w_mm = p.now(st)
                def efn(e, f1=f1, bk_=bk_, bsb=bsb, dstraw=dstraw):
                    return e.activation(
                        out=dstraw[:, f1 * 512:(f1 + 1) * 512],
                        in_=pb[bk_][:], func=AF.Identity,
                        bias=bsb[:, f1:f1 + 1], scale=1.0)
                p.add("scalar", efn, waits=((st, w_mm), (dx, w_small)), inc=ss)
                L.set_guard(("pb", bk_), ss, p.now(ss))
        kv_done = p.now(ss)
        L.set_guard("xnT_rd", st, p.now(st))
        for (rawb, cdst) in ((kraw, ckT), (vraw, cvT)):
            for ci in range(2):
                def mfn(e, ci=ci, rawb=rawb):
                    return e.matmul(
                        out=pb[0][0:64, :],
                        lhsT=wcsb[:, ci * 64:(ci + 1) * 64],
                        rhs=rawb[:, ci * 512:(ci + 1) * 512],
                        start=(ci == 0), stop=(ci == 1))
                p.add("tensor", mfn,
                      waits=((ss, kv_done),) + L.guard(("pb", 0))
                      if ci == 0 else (), inc=st if ci == 1 else None)
            w_mm = p.now(st)
            def efn(e, cdst=cdst):
                return e.tensor_copy(out=cdst[:], in_=pb[0][0:64, :])
            p.add("vector", efn, waits=((st, w_mm),), inc=sv)
            L.set_guard(("pb", 0), sv, p.now(sv))
        c_done = p.now(sv)
        for f1 in range(2):
            def mfn(e, f1=f1):
                return e.matmul(out=pb[1][:],
                                lhsT=wesb[:, f1 * 128:(f1 + 1) * 128],
                                rhs=ckT[:], start=True, stop=True)
            p.add("tensor", mfn,
                  waits=((sv, c_done), (gx, w_wdma)) + L.guard(("pb", 1)),
                  inc=st)
            w_mm = p.now(st)
            def efn(e):
                return e.copy(out=qtmp[:], in_=pb[1][:])
            p.add("scalar", efn, waits=((st, w_mm),) + L.guard("qtmp"), inc=ss)
            w_e = p.now(ss)
            L.set_guard(("pb", 1), ss, w_e)
            p.add("vector", lambda e: e.tensor_scalar_min(
                out=t1[:], in0=qtmp[:], scalar1=0.0),
                waits=((ss, w_e),) + L.guard("t1"), inc=sv)
            p.add("vector", lambda e: e.tensor_scalar_max(
                out=t3[:], in0=qtmp[:], scalar1=0.0), inc=sv)
            w_mn = p.now(sv)
            L.set_guard("qtmp", sv, w_mn)
            p.add("scalar", lambda e: e.activation(
                out=t2[:], in_=t1[:], func=AF.Exp),
                waits=((sv, w_mn),) + L.guard("t2"), inc=ss)
            w_ex = p.now(ss)
            L.set_guard("t1", ss, w_ex)
            dst = kfT[:, f1 * 512:(f1 + 1) * 512]
            p.add("vector", lambda e, dst=dst: e.tensor_add(
                out=dst, in0=t2[:], in1=t3[:]),
                waits=((ss, w_ex), (sv, w_mn)), inc=sv)
            L.set_guard("t2", sv, p.now(sv))
            L.set_guard("t3", sv, p.now(sv))
        kf_done = p.now(sv)
        for t in range(4):
            def mfn(e, t=t):
                return e.matmul(out=pb[2][:, 0:256],
                                lhsT=cvT[:, t * 128:(t + 1) * 128],
                                rhs=wesb[:], start=True, stop=True)
            p.add("tensor", mfn,
                  waits=((sv, c_done), (gx, w_wdma)) + L.guard(("pb", 2)),
                  inc=st)
            w_mm = p.now(st)
            for g in range(4):
                dst = v2e[:, t * 260 + g * 65:t * 260 + g * 65 + 64]
                def efn(e, dst=dst, g=g):
                    return e.tensor_copy(out=dst,
                                         in_=pb[2][:, g * 64:(g + 1) * 64])
                p.add("vector", efn, waits=((st, w_mm),),
                      inc=sv if g == 3 else None)
            L.set_guard(("pb", 2), sv, p.now(sv))
        v2_done = p.now(sv)
        for ci in range(2):
            for t in range(4):
                def tfn(e, ci=ci, t=t):
                    return e.transpose(
                        out=pt0[:, 0:128],
                        in_=kfT[:, ci * 512 + t * 128:ci * 512 + (t + 1) * 128],
                        identity=idsb[:])
                p.add("tensor", tfn,
                      waits=((sv, kf_done),) + L.guard("ptB"), inc=st)
                w_t = p.now(st)
                dst = kftok[:, t * 256 + ci * 128:t * 256 + (ci + 1) * 128]
                def cfn(e, dst=dst):
                    return e.tensor_copy(out=dst, in_=pt0[:, 0:128])
                p.add("vector", cfn, waits=((st, w_t),), inc=sv)
                L.set_guard("ptB", sv, p.now(sv))
        kftok_done = p.now(sv)
        for t in range(4):
            for g in range(4):
                def smm(e, t=t, g=g):
                    return e.matmul(
                        out=pS[:, g * 65:(g + 1) * 65],
                        lhsT=kftok[:, t * 256 + g * 64:t * 256 + (g + 1) * 64],
                        rhs=v2e[:, t * 260 + g * 65:t * 260 + (g + 1) * 65],
                        start=(g == 0), stop=(g == 3))
                p.add("tensor", smm,
                      waits=((sv, max(kftok_done, v2_done)),) +
                      (L.guard("pS") if g == 0 else ()),
                      inc=st if g == 3 else None)
            w_s = p.now(st)
            p.add("vector", lambda e: e.tensor_add(
                out=base32[:], in0=base32[:], in1=pS[:]),
                waits=((st, w_s),), inc=sv)
            L.set_guard("pS", sv, p.now(sv))
        L.set_guard("pb_bufs", sv, p.now(sv))

    p.add("vector", lambda e: e.tensor_scalar_mul(
        out=base32[:], in0=base32[:], scalar1=msk[0:64, 0:1]),
        waits=((sv, p.now(sv)),), inc=sv)
    w_bm = p.now(sv)
    def bdma(e):
        return e.dma_start(out=base_scr, in_=base32[:])
    p.add("sync", bdma, waits=((sv, w_bm),), inc=so, inc_by=16)
    p.add("sync", lambda e: e.nop(), waits=((so, p.now(so)),))



PAIRS = [[0, 1], [2, 3], [4, 5], [6, 7]]


def build_fused():
    L = Launch()
    p = L.p
    d = {}
    d["x_full"] = L.inp("x_full", [S, D])
    d["w1p"] = L.inp("w1p", [D, 2 * 1536])
    d["b1p"] = L.inp("b1p", [128, 24])
    d["w0t"] = L.inp("w0t", [D, D])
    d["wpmy"] = L.inp("wpmy", [D, D])
    d["idn"] = L.inp("idn", [128, 128], BF16)
    d["dmask"] = L.inp("dmask", [4 * 128, 1024], BF16)
    d["bp"] = L.inp("bp", [1, D])
    d["wq"] = L.inp("wq", [D, D])
    d["bq"] = L.inp("bq", [128, 8])
    d["wk"] = L.inp("wk", [D, 256])
    d["bk"] = L.inp("bk", [128, 2])
    d["wv"] = L.inp("wv", [D, 256])
    d["bv"] = L.inp("bv", [128, 2])
    d["wc"] = L.inp("wc", [256, 64])
    d["we"] = L.inp("we", [64, 256])
    d["trimask"] = L.inp("trimask", [128, 128], BF16)
    d["duplo"] = L.inp("duplo", [128, 128], BF16)
    d["duphi"] = L.inp("duphi", [128, 128], BF16)
    d["wo"] = L.inp("wo", [D, D])
    d["bo"] = L.inp("bo", [1, D])
    d["gam"] = L.inp("gam", [1, D])
    d["bet"] = L.inp("bet", [1, D])
    d["gw"] = L.inp("gw", [D, 8])
    d["selm"] = L.inp("selm", [8, 8 * 128], BF16)
    d["wgu"] = L.inp("wgu", [9 * D, 5632])
    d["wd"] = L.inp("wd", [9 * 2816, D])
    d["oddmask"] = L.inp("oddmask", [128, 1])
    d["texd"] = L.inp("texd", [128, 260])
    d["yout"] = L.out("yout", [TOK, D])
    s0p0 = L.scr("s0p0_scr", [S, D])
    s0p1 = L.scr("s0p1_scr", [S, D])
    d["xnt"] = L.scr("xnt_scr", [8, 128, 8 * 512], BF16)
    d["g0"] = L.scr("g0_scr", [S, D])
    d["x1i"] = L.scr("x1_own_scr", [TOK, D])
    d["x1_own_src"] = d["x1i"]
    d["base_scr"] = L.scr("base_scr", [64, 260])
    d["qfo"] = L.scr("qf_scr", [D, TOK], BF16)
    d["qft"] = d["qfo"]
    d["numx"] = L.scr("numx_scr", [TOK, 16 * 65])
    d["tex"] = L.scr("tex_scr", [256, 65])
    d["s_scr"] = L.scr("s_scr", [TOK, D], BF16)
    d["acc_scr"] = L.scr("acc_scr", [D, TOK], BF16)

    with L.es:
        sbar = p.sem("sbar")
        for g in range(2):
            dg = dict(d)
            dg["w1p"] = d["w1p"][:, g * 1536:(g + 1) * 1536]
            dg["b1p"] = d["b1p"][:, g * 12:(g + 1) * 12]
            dg["wpmy"] = d["wpmy"][g * 512:(g + 1) * 512, :]
            dg["s0p"] = (s0p0, s0p1)[g]
            dg["x_own"] = d["x_full"]
            dg["skip_tail"] = True
            if g == 0:
                dg["save_xnt"] = d["xnt"]
            else:
                dg["load_xnt"] = d["xnt"]
            L.sfx = f"_a{g}"
            with contextlib.ExitStack() as sc:
                L.scope = sc
                phase_A(L, dg)
            L.guards.clear()
            p.barrier(sbar)
        dB = dict(d)
        dB["s0p0"] = s0p0
        dB["s0p1"] = s0p1
        L.sfx = "_b"
        with contextlib.ExitStack() as sc:
            L.scope = sc
            phase_B(L, dB)
        L.guards.clear()
        p.barrier(sbar)
        L.sfx = "_c"
        with contextlib.ExitStack() as sc:
            L.scope = sc
            phase_C(L, d)
        L.guards.clear()
        p.barrier(sbar)
        L.sfx = "_d"
        with contextlib.ExitStack() as sc:
            L.scope = sc
            phase_D(L, d)
        p.emit()
    return L.nc


def _fused_inputs(inputs):
    x = np.asarray(inputs["L0_attn_x"])
    W1 = np.asarray(inputs["L0_attn_W1"])
    b1 = np.asarray(inputs["L0_attn_b1"])
    Wp = np.asarray(inputs["L0_attn_Wp"])
    W0 = np.asarray(inputs["L0_ffn_weight"])
    perm = np.concatenate([_perm_w1_cols(0), _perm_w1_cols(1)])
    w1p = np.ascontiguousarray(W1[:, perm])
    b1p = np.ascontiguousarray(b1[perm].reshape(24, 128).T)
    bp = np.asarray(inputs["L0_attn_bp"]).reshape(1, D)
    tri = _tri_mask()
    lo, hi = _dup_mats()
    dm = _diag_masks()
    Wo = np.asarray(inputs["L1_attn_Wo"])
    bo = np.asarray(inputs["L1_attn_bo"]).reshape(1, D)
    gam = np.asarray(inputs["L1_attn_gamma"]).reshape(1, D)
    bet = np.asarray(inputs["L1_attn_beta"]).reshape(1, D)
    gw = np.asarray(inputs["L1_ffn_gate_w"])
    wperm = _wgu_perm()
    wgu_all = np.concatenate(
        [np.asarray(inputs["L1_ffn_base_wgu"])[None]] +
        [np.asarray(inputs["L1_ffn_exp_wgu"])], axis=0)[:, :, wperm]
    wgu_all = np.ascontiguousarray(wgu_all.reshape(9 * D, 5632))
    wd_all = np.concatenate(
        [np.asarray(inputs["L1_ffn_base_wd"])[None]] +
        [np.asarray(inputs["L1_ffn_exp_wd"])], axis=0)
    wd_all = np.ascontiguousarray(wd_all.reshape(9 * 2816, D))
    sel = _sel_mat()
    zeros_tex = np.zeros((128, 260), np.float32)
    in_maps = []
    for core in range(NCORE):
        b, h = core // 2, core % 2
        in_maps.append({
            "x_full": np.ascontiguousarray(x[b]),
            "w1p": w1p, "b1p": b1p,
            "w0t": np.ascontiguousarray(W0.T),
            "wpmy": Wp,
            "idn": _IDN, "dmask": dm,
            "bp": np.ascontiguousarray(bp),
            "wq": np.asarray(inputs["L1_attn_Wq"]),
            "bq": np.ascontiguousarray(
                np.asarray(inputs["L1_attn_bq"]).reshape(8, 128).T),
            "wk": np.asarray(inputs["L1_attn_Wk"]),
            "bk": np.ascontiguousarray(
                np.asarray(inputs["L1_attn_bk"]).reshape(2, 128).T),
            "wv": np.asarray(inputs["L1_attn_Wv"]),
            "bv": np.ascontiguousarray(
                np.asarray(inputs["L1_attn_bv"]).reshape(2, 128).T),
            "wc": np.asarray(inputs["L1_attn_Wc"]),
            "we": np.asarray(inputs["L1_attn_We"]),
            "trimask": tri, "duplo": lo, "duphi": hi,
            "wo": Wo, "bo": np.ascontiguousarray(bo),
            "gam": np.ascontiguousarray(gam),
            "bet": np.ascontiguousarray(bet),
            "gw": gw, "selm": sel,
            "wgu": wgu_all, "wd": wd_all,
            "oddmask": np.full((128, 1), float(h), np.float32),
            "texd": zeros_tex,
        })
    return in_maps



def _perm_w1_cols(half):
    heads = range(half * 8, half * 8 + 8)
    cols = []
    for part in range(3):
        for hh in heads:
            base = hh * 192 + part * 64
            cols.extend(range(base, base + 64))
    return np.array(cols)


def _diag_masks():
    j = np.arange(4)[:, None, None]
    k = np.arange(128)[None, :, None]
    q = np.arange(512)[None, None, :]
    m = (128 * j + k <= q).astype(np.float32)
    m = np.concatenate([m, m], axis=-1)
    return np.ascontiguousarray(m.reshape(4 * 128, 1024).astype(BF))


_IDN = np.eye(128, dtype=np.float32).astype(BF)


def _launch_A_inputs(inputs):
    x = np.asarray(inputs["L0_attn_x"])
    W1 = np.asarray(inputs["L0_attn_W1"])
    b1 = np.asarray(inputs["L0_attn_b1"])
    Wp = np.asarray(inputs["L0_attn_Wp"])
    W0 = np.asarray(inputs["L0_ffn_weight"])
    dm = _diag_masks()
    in_maps = []
    for core in range(NCORE):
        b, h = core // 2, core % 2
        perm = _perm_w1_cols(h)
        w1p = np.ascontiguousarray(W1[:, perm])
        b1p = np.ascontiguousarray(b1[perm].reshape(12, 128).T)
        wpmy = np.ascontiguousarray(Wp[h * 512:(h + 1) * 512, :])
        in_maps.append({
            "x_full": np.ascontiguousarray(x[b]),
            "x_own": np.ascontiguousarray(x[b, h * TOK:(h + 1) * TOK]),
            "w1p": w1p, "b1p": b1p,
            "w0t": np.ascontiguousarray(W0.T), "wpmy": wpmy,
            "idn": _IDN, "dmask": dm,
        })
    return in_maps


def _tri_mask():
    j = np.arange(128)[:, None]
    i = np.arange(128)[None, :]
    return np.ascontiguousarray((j <= i).astype(np.float32).astype(BF))


def _dup_mats():
    c = np.arange(128)[:, None]
    f = np.arange(128)[None, :]
    lo = ((c < 64) & (f % 64 == c)).astype(np.float32).astype(BF)
    hi = ((c >= 64) & (f % 64 == c - 64)).astype(np.float32).astype(BF)
    return np.ascontiguousarray(lo), np.ascontiguousarray(hi)


def _launch_C_inputs(inputs, resA):
    x = np.asarray(inputs["L0_attn_x"])
    bp = np.asarray(inputs["L0_attn_bp"]).reshape(1, D)
    Wq = np.asarray(inputs["L1_attn_Wq"])
    bq = np.asarray(inputs["L1_attn_bq"]).reshape(8, 128).T
    Wk = np.asarray(inputs["L1_attn_Wk"])
    bk = np.asarray(inputs["L1_attn_bk"]).reshape(2, 128).T
    Wv = np.asarray(inputs["L1_attn_Wv"])
    bv = np.asarray(inputs["L1_attn_bv"]).reshape(2, 128).T
    Wc = np.asarray(inputs["L1_attn_Wc"])
    We = np.asarray(inputs["L1_attn_We"])
    tri = _tri_mask()
    lo, hi = _dup_mats()
    in_maps = []
    for core in range(NCORE):
        b, h = core // 2, core % 2
        rows = slice(h * TOK, (h + 1) * TOK)
        partner = 2 * b + (1 - h)
        in_maps.append({
            "p_self": np.ascontiguousarray(np.asarray(resA[core]["s0p"])[rows]),
            "p_part": np.ascontiguousarray(np.asarray(resA[partner]["s0p"])[rows]),
            "g0": np.asarray(resA[core]["g0"]),
            "x_own": np.ascontiguousarray(x[b, rows]),
            "bp": np.ascontiguousarray(bp),
            "wq": Wq, "bq": np.ascontiguousarray(bq),
            "wk": Wk, "bk": np.ascontiguousarray(bk),
            "wv": Wv, "bv": np.ascontiguousarray(bv),
            "wc": Wc, "we": We,
            "idn": _IDN, "trimask": tri, "duplo": lo, "duphi": hi,
        })
    return in_maps


def _sel_mat():
    c = np.arange(8)[:, None]
    ep = np.arange(8 * 128)[None, :] // 128
    return np.ascontiguousarray((c == ep).astype(np.float32).astype(BF))


def _wgu_perm():
    idx = []
    for ch in range(11):
        idx.extend(range(ch * 256, (ch + 1) * 256))
        idx.extend(range(2816 + ch * 256, 2816 + (ch + 1) * 256))
    return np.array(idx)


def _launch_D_inputs(inputs, resC):
    Wo = np.asarray(inputs["L1_attn_Wo"])
    bo = np.asarray(inputs["L1_attn_bo"]).reshape(1, D)
    gam = np.asarray(inputs["L1_attn_gamma"]).reshape(1, D)
    bet = np.asarray(inputs["L1_attn_beta"]).reshape(1, D)
    gw = np.asarray(inputs["L1_ffn_gate_w"])
    perm = _wgu_perm()
    wgu_all = np.concatenate(
        [np.asarray(inputs["L1_ffn_base_wgu"])[None]] +
        [np.asarray(inputs["L1_ffn_exp_wgu"])], axis=0)[:, :, perm]
    wgu_all = np.ascontiguousarray(wgu_all.reshape(9 * D, 5632))
    wd_all = np.concatenate(
        [np.asarray(inputs["L1_ffn_base_wd"])[None]] +
        [np.asarray(inputs["L1_ffn_exp_wd"])], axis=0)
    wd_all = np.ascontiguousarray(wd_all.reshape(9 * 2816, D))
    sel = _sel_mat()
    in_maps = []
    for core in range(NCORE):
        b, h = core // 2, core % 2
        partner = 2 * b + (1 - h)
        if h == 0:
            texd = np.zeros((128, 260), np.float32)
        else:
            t = np.asarray(resC[partner]["tex"]).reshape(4, 64, 65)
            t = np.ascontiguousarray(t.transpose(1, 0, 2).reshape(64, 260))
            texd = np.concatenate([t, t], axis=0)
        in_maps.append({
            "x1i": np.asarray(resC[core]["x1o"]),
            "qft": np.asarray(resC[core]["qfo"]),
            "numx": np.asarray(resC[core]["numx"]),
            "texd": np.ascontiguousarray(texd),
            "wo": Wo, "bo": np.ascontiguousarray(bo),
            "gam": np.ascontiguousarray(gam),
            "bet": np.ascontiguousarray(bet),
            "gw": gw, "selm": sel,
            "wgu": wgu_all, "wd": wd_all,
            "idn": _IDN,
        })
    return in_maps


_NC_CACHE = {}


def _get_nc(name, builder):
    if name not in _NC_CACHE:
        _NC_CACHE[name] = builder()
    return _NC_CACHE[name]


def kernel(**inputs):
    res = run_bass_kernel_spmd(_get_nc("F", build_fused),
                               _fused_inputs(inputs),
                               core_ids=list(range(NCORE))).results
    out = np.zeros((NSEQ, S, D), np.float32)
    for core in range(NCORE):
        b, h = core // 2, core % 2
        out[b, h * TOK:(h + 1) * TOK] = np.asarray(res[core]["yout"])
    return out


# version 22
# speedup vs baseline: 1.0367x; 1.0367x over previous
import contextlib
import numpy as np
import ml_dtypes

import concourse.bass as bass
import concourse.mybir as mybir
from concourse.bass_utils import run_bass_kernel_spmd

F32 = mybir.dt.float32
BF16 = mybir.dt.bfloat16
AF = mybir.ActivationFunctionType
ALU = mybir.AluOpType
AX = mybir.AxisListType
BF = ml_dtypes.bfloat16

D = 1024
S = 4096
NSEQ = 4
NCORE = 8
TOK = 2048
HD = 64
MYH = 8
NEG_SHIFT = -24.0

ENGS = ("sync", "tensor", "vector", "scalar", "gpsimd")


class Prog:

    def __init__(self, nc, es):
        self.nc = nc
        self.es = es
        self.ops = {e: [] for e in ENGS}
        self.cnt = {}

    def sem(self, name):
        h = self.es.enter_context(self.nc.semaphore(name))
        self.cnt[name] = 0
        return h

    def add(self, eng, fn, waits=(), inc=None, inc_by=1):
        waits = tuple((s, v) for (s, v) in waits if v > 0)
        for s, v in waits:
            assert v <= self.cnt[s.name], f"wait on future value of {s.name}"
        if inc is not None:
            self.cnt[inc.name] += inc_by
        self.ops[eng].append((fn, waits, inc, inc_by))
        return self.cnt[inc.name] if inc is not None else None

    def now(self, semh):
        return self.cnt[semh.name]

    def barrier(self, semh):
        for e in ENGS:
            self.add(e, lambda eng: eng.nop(), inc=semh)
        tgt = self.cnt[semh.name]
        for e in ENGS:
            self.add(e, lambda eng: eng.nop(), waits=((semh, tgt),))
        return tgt

    def emit(self):
        with self.nc.Block() as block:
            def make(eng_name):
                def body(eng):
                    for fn, waits, inc, inc_by in self.ops[eng_name]:
                        for (s, v) in waits:
                            eng.wait_ge(s, v)
                        inst = fn(eng)
                        if inc is not None:
                            if inst is None:
                                inst = eng.nop()
                            inst.then_inc(inc, inc_by)
                return body
            block.sync(make("sync"))
            block.tensor(make("tensor"))
            block.vector(make("vector"))
            block.scalar(make("scalar"))
            block.gpsimd(make("gpsimd"))


class Launch:

    def __init__(self):
        self.nc = bass.Bass("TRN2", target_bir_lowering=False)
        self.es = contextlib.ExitStack()
        self.p = Prog(self.nc, self.es)
        self.scope = self.es
        self.sfx = ""
        self.guards = {}

    def inp(self, name, shape, dt=F32):
        return self.nc.dram_tensor(name, shape, dt, kind="ExternalInput").ap()

    def out(self, name, shape, dt=F32):
        return self.nc.dram_tensor(name, shape, dt, kind="ExternalOutput").ap()

    def scr(self, name, shape, dt=F32):
        return self.nc.dram_tensor(name, shape, dt, kind="Internal").ap()

    def sb(self, name, shape, dt=F32):
        return self.scope.enter_context(
            self.nc.sbuf_tensor(name + self.sfx, shape, dt)).ap()

    def ps(self, name, shape, dt=F32):
        return self.scope.enter_context(
            self.nc.psum_tensor(name + self.sfx, shape, dt)).ap()

    def sem(self, name):
        return self.p.sem(name + self.sfx)

    def guard(self, key):
        return tuple(self.guards.get(key, ()))

    def set_guard(self, key, sem, val):
        self.guards[key] = [(sem, val)]



def emit_ln_tiles(L, tiles, waits=(), eps_ap=None, gamma_beta=None):
    p, sv, ss = L.p, L.sv, L.ss
    n = tiles[0][0].shape[-1]
    inv = 1.0 / n
    sls = [[sc["s"][:, i:i + 1] for i in range(6)] for (_, _, sc) in tiles]
    for (x_ap, _, sc), sl in zip(tiles, sls):
        p.add("scalar", lambda e, x_ap=x_ap, sc=sc, sl=sl: e.activation(
            out=sc["sq"], in_=x_ap, func=AF.Square, accum_out=sl[0]),
            waits=waits, inc=ss)
    w_ss = p.now(ss)
    for (x_ap, _, _), sl in zip(tiles, sls):
        p.add("vector", lambda e, x_ap=x_ap, sl=sl: e.reduce_sum(
            out=sl[1], in_=x_ap, axis=AX.X), waits=waits, inc=sv)
    w1 = p.now(sv)
    for sl in sls:
        p.add("vector", lambda e, sl=sl: e.tensor_scalar_mul(
            out=sl[2], in0=sl[1], scalar1=inv), waits=((sv, w1),), inc=sv)
    w2 = p.now(sv)
    for sl in sls:
        p.add("vector", lambda e, sl=sl: e.tensor_mul(
            out=sl[3], in0=sl[2], in1=sl[2]), waits=((sv, w2),), inc=sv)
    w3 = p.now(sv)
    for sl in sls:
        p.add("vector", lambda e, sl=sl: e.scalar_tensor_tensor(
            out=sl[4], in0=sl[0], scalar=inv, in1=sl[3],
            op0=ALU.mult, op1=ALU.subtract),
            waits=((sv, w3), (ss, w_ss)), inc=sv)
    w4 = p.now(sv)
    for sl in sls:
        p.add("scalar", lambda e, sl=sl: e.activation(
            out=sl[5], in_=sl[4], func=AF.Sqrt, bias=eps_ap),
            waits=((sv, w4),), inc=ss)
    w_s2 = p.now(ss)
    for sl in sls:
        p.add("vector", lambda e, sl=sl: e.reciprocal(out=sl[4], in_=sl[5]),
              waits=((ss, w_s2),), inc=sv)
    w5 = p.now(sv)
    for (x_ap, xn_out_ap, sc), sl in zip(tiles, sls):
        if gamma_beta is None:
            p.add("vector", lambda e, x_ap=x_ap, o=xn_out_ap, sl=sl:
                  e.tensor_scalar(out=o, in0=x_ap, scalar1=sl[2],
                                  scalar2=sl[4], op0=ALU.subtract,
                                  op1=ALU.mult), waits=((sv, w5),), inc=sv)
        else:
            tmp, gb, bb = gamma_beta
            p.add("vector", lambda e, x_ap=x_ap, sl=sl, tmp=tmp:
                  e.tensor_scalar(out=tmp, in0=x_ap, scalar1=sl[2],
                                  scalar2=sl[4], op0=ALU.subtract,
                                  op1=ALU.mult), waits=((sv, w5),), inc=sv)
    if gamma_beta is not None:
        assert len(tiles) == 1
        w6 = p.now(sv)
        tmp, gb, bb = gamma_beta
        xn_out_ap = tiles[0][1]
        p.add("vector", lambda e: e.tensor_mul(out=tmp, in0=tmp, in1=gb),
              waits=((sv, w6),), inc=sv)
        w7 = p.now(sv)
        p.add("vector", lambda e: e.tensor_add(out=xn_out_ap, in0=tmp, in1=bb),
              waits=((sv, w7),), inc=sv)
    return p.now(sv)



def phase_A(L, d):
    nc, p = L.nc, L.p
    x_full = d["x_full"]
    x_own = d["x_own"]
    w1p = d["w1p"]
    b1p = d["b1p"]
    w0t = d["w0t"]
    wpmy = d["wpmy"]
    idn = d["idn"]
    dmask = d["dmask"]
    s0p = d["s0p"]
    g0 = d["g0"]

    sb, ps = L.sb, L.ps
    xbuf = sb("xbuf", [128, 2 * 4 * D], BF16)
    xn_bf = sb("xn_bf", [128, 4 * D], BF16)
    xnT = sb("xnT", [128, 8 * 512], BF16)
    w1sb = sb("w1sb", [128, 8 * 1536], BF16)
    b1sb = sb("b1sb", [128, 12])
    sgnT = sb("sgnT", [128, 8 * D], BF16)
    wpsb = sb("wpsb", [128, 4 * D], BF16)
    idsb = sb("idsb", [128, 128], BF16)
    msksb = sb("msksb", [128, 4 * 1024], BF16)
    negm = msksb
    qC = sb("qC", [128, 4 * 512], BF16)
    kT = sb("kT", [128, 4 * S], BF16)
    vC = sb("vC", [128, 4 * 512], BF16)
    vtok = sb("vtok", [128, 32 * MYH * 65], BF16)
    oTb = sb("oTb", [128, 4 * 512], BF16)
    ptile = sb("ptile", [128, 2 * 1024], BF16)
    stage = sb("stage", [128, 4 * D], BF16)
    gstage = sb("gstage", [128, 4 * D])
    ones64 = sb("ones64", [1, 64], BF16)
    o_s = sb("o_s", [64, 512])
    rd = sb("rd", [1, 512], BF16)
    lsq = sb("lsq", [128, D])
    lsm = sb("lsm", [128, 32])
    consts = sb("consts", [128, 4])

    pq = [ps("pq0", [128, 1024]), ps("pq1", [128, 1024])]
    pb = [pq[0][:, 0:512], pq[0][:, 512:1024],
          pq[1][:, 0:512], pq[1][:, 512:1024]]
    pob = [ps("po0", [128, 512]), ps("po1", [128, 512])]
    po = [t[0:65, :] for t in pob]
    ptb = ps("pt0", [128, 2048], BF16)
    pt = [ptb[:, 0:512], ptb[:, 1024:1536]]

    dx = L.sem("dx")
    gx = L.sem("gx")
    st = L.sem("st")
    sv = L.sem("sv")
    ss = L.sem("ss")
    so = L.sem("so")
    L.sv, L.ss = sv, ss

    eps5 = consts[:, 0:1]
    neg24 = consts[:, 1:2]

    def ln_sc(i):
        return {"sq": lsq, "s": lsm[:, 6 * i:6 * i + 6]}

    p.add("vector", lambda e: e.memset(eps5, 1e-5), inc=sv)
    p.add("vector", lambda e: e.memset(neg24, NEG_SHIFT), inc=sv)
    p.add("vector", lambda e: e.memset(ones64[:], 1.0), inc=sv)
    p.add("vector", lambda e: e.memset(vtok[:], 1.0), inc=sv)
    vtok_ready = p.now(sv)
    p.add("sync", lambda e: e.dma_start(out=idsb[:], in_=idn),
          inc=dx, inc_by=16)
    w_id = p.now(dx)
    p.add("sync", lambda e: e.dma_start(
        out=msksb[:].rearrange("p (j n) -> p j n", j=4),
        in_=dmask.rearrange("(j p) n -> p j n", p=128)), inc=dx, inc_by=16)
    p.add("sync", lambda e: e.dma_start(out=b1sb[:], in_=b1p),
          inc=dx, inc_by=16)
    w_small = p.now(dx)
    p.add("gpsimd", lambda e: e.dma_start(
        out=w1sb[:].rearrange("p (c n) -> p c n", c=8),
        in_=w1p.rearrange("(c p) n -> p c n", p=128)), inc=gx, inc_by=16)
    p.add("gpsimd", lambda e: e.dma_start(
        out=sgnT[:].rearrange("p (c n) -> p c n", c=8),
        in_=w0t.rearrange("(c p) n -> p c n", p=128)), inc=gx, inc_by=16)
    p.add("gpsimd", lambda e: e.dma_start(
        out=wpsb[:].rearrange("p (c n) -> p c n", c=4),
        in_=wpmy.rearrange("(c p) n -> p c n", p=128)), inc=gx, inc_by=16)
    w_wdma = p.now(gx)

    for jp in range(4):
        p.add("vector", lambda e, jp=jp: e.tensor_scalar(
            out=negm[:, jp * 1024:(jp + 1) * 1024],
            in0=msksb[:, jp * 1024:(jp + 1) * 1024],
            scalar1=400.0, scalar2=-400.0, op0=ALU.mult, op1=ALU.add),
            waits=((dx, w_small),), inc=sv)
    negm_rdy = p.now(sv)
    p.add("scalar", lambda e: e.activation(out=sgnT[:], in_=sgnT[:],
                                           func=AF.Sign),
          waits=((gx, w_wdma),), inc=ss)
    w_sgn = p.now(ss)

    def emit_chunk_ln(cbuf, dmawaits):
        tiles = [(cbuf[:, t * D:(t + 1) * D],
                  xn_bf[:, t * D:(t + 1) * D], ln_sc(t))
                 for t in range(4)]
        return emit_ln_tiles(L, tiles, waits=dmawaits, eps_ap=eps5)

    def emit_chunk_transposes(w_apply):
        for ci in range(8):
            bk = ci % 2
            for t in range(4):
                def tfn(e, ci=ci, t=t, bk=bk):
                    return e.transpose(
                        out=pt[bk][:, t * 128:(t + 1) * 128],
                        in_=xn_bf[:, t * D + ci * 128:
                                  t * D + ci * 128 + 128],
                        identity=idsb[:])
                p.add("tensor", tfn,
                      waits=(((sv, w_apply),) + L.guard(("pt", bk))
                             if t == 0 else ()),
                      inc=st if t == 3 else None)
            w_t = p.now(st)

            def cfn(e, ci=ci, bk=bk):
                return e.tensor_copy(out=xnT[:, ci * 512:(ci + 1) * 512],
                                     in_=pt[bk][:])
            p.add("vector", cfn,
                  waits=((st, w_t),) +
                  (L.guard("xnT_w") if ci == 0 else ()), inc=sv)
            L.set_guard(("pt", bk), sv, p.now(sv))
        return p.now(sv)

    def chunk_ln_transpose(cbuf, dmawaits):
        return emit_chunk_transposes(emit_chunk_ln(cbuf, dmawaits))

    save_xnt = d.get("save_xnt")
    load_xnt = d.get("load_xnt")

    def emit_prefetch(c):
        par = c % 2
        cbuf = xbuf[:, par * 4 * D:(par + 1) * 4 * D]
        if load_xnt is None:
            def xdma(e, c=c, cbuf=cbuf):
                return e.dma_start(
                    out=cbuf.rearrange("p (t n) -> p t n", t=4),
                    in_=x_full[c * 512:(c + 1) * 512, :].rearrange(
                        "(t p) n -> p t n", p=128))
            p.add("gpsimd", xdma, waits=L.guard(("xbuf", par)),
                  inc=gx, inc_by=16)
            w_x = p.now(gx)
            w_apply = emit_chunk_ln(cbuf, ((gx, w_x),))
            L.set_guard(("xbuf", par), sv, w_apply)
            return w_apply
        else:
            def xld(e, c=c):
                return e.dma_start(out=xnT[:], in_=load_xnt[c])
            p.add("gpsimd", xld, waits=L.guard("xnT_rd"), inc=gx, inc_by=16)
            return p.now(gx)

    pfet = {0: emit_prefetch(0)}
    for c in range(8):
        if load_xnt is None:
            xnT_done = emit_chunk_transposes(pfet[c])
            xn_wait = (sv, xnT_done)
            if save_xnt is not None:
                def xtd(e, c=c):
                    return e.dma_start(out=save_xnt[c], in_=xnT[:])
                p.add("sync", xtd, waits=((sv, xnT_done),),
                      inc=so, inc_by=16)
                L.set_guard("xnT_w", so, p.now(so))
        else:
            xn_wait = (gx, pfet[c])

        evac_tgt = {}
        for f1 in range(12):
            bk = f1 % 2
            for ci in range(8):
                w = ()
                if ci == 0:
                    w = (xn_wait, (gx, w_wdma)) + \
                        L.guard(("pb", bk))

                def mfn(e, f1=f1, ci=ci, bk=bk):
                    return e.matmul(
                        out=pb[bk][:],
                        lhsT=w1sb[:, ci * 1536 + f1 * 128:
                                  ci * 1536 + (f1 + 1) * 128],
                        rhs=xnT[:, ci * 512:(ci + 1) * 512],
                        start=(ci == 0), stop=(ci == 7))
                p.add("tensor", mfn, waits=w, inc=st if ci == 7 else None)
            w_mm = p.now(st)

            if f1 < 4:
                edst = qC[:, f1 * 512:(f1 + 1) * 512]
            elif f1 < 8:
                edst = kT[:, (f1 - 4) * S + c * 512:
                          (f1 - 4) * S + (c + 1) * 512]
            else:
                edst = vC[:, (f1 - 8) * 512:(f1 - 7) * 512]

            def efn(e, f1=f1, bk=bk, edst=edst):
                return e.activation(
                    out=edst,
                    in_=pb[bk][:], func=AF.Identity,
                    bias=b1sb[:, f1:f1 + 1], scale=1.0)
            p.add("scalar", efn, waits=((st, w_mm), (dx, w_small)), inc=ss)
            evac_tgt[f1] = p.now(ss)
            L.set_guard(("pb", bk), ss, evac_tgt[f1])
        L.set_guard("xnT_rd", st, p.now(st))

        for hp in range(4):
            for tb in range(4):
                j = 4 * c + tb
                bk = tb % 2

                def tfn(e, hp=hp, tb=tb, bk=bk):
                    return e.transpose(
                        out=pt[bk][:, 0:128],
                        in_=vC[:, hp * 512 + tb * 128:
                                hp * 512 + (tb + 1) * 128],
                        identity=idsb[:])
                p.add("tensor", tfn,
                      waits=((ss, evac_tgt[8 + hp]),) +
                      L.guard(("pt", bk)), inc=st)
                w_t = p.now(st)
                base = (j * MYH + 2 * hp) * 65

                def c1(e, base=base, bk=bk):
                    return e.tensor_copy(out=vtok[:, base:base + 64],
                                         in_=pt[bk][:, 0:64])

                def c2(e, base=base, bk=bk):
                    return e.tensor_copy(
                        out=vtok[:, base + 65:base + 65 + 64],
                        in_=pt[bk][:, 64:128])
                p.add("vector", c1, waits=((st, w_t), (sv, vtok_ready)))
                p.add("vector", c2, inc=sv)
                L.set_guard(("pt", bk), sv, p.now(sv))
        vtok_done = p.now(sv)
        cur_sync = ((ss, p.now(ss)), (sv, p.now(sv)))

        for hp in range(4):
            qA = qC[:, hp * 512:(hp + 1) * 512]
            kA = kT[:, hp * S:(hp + 1) * S]
            njc = 4 * c + 4
            exp_done = {}
            pend = []
            def flush_av(av):
                a1f, a2f, avw = av
                p.add("tensor", a1f, waits=avw)
                p.add("tensor", a2f, inc=st)
            for j in range(njc):
                bA, bB = 2 * (j % 2), 2 * (j % 2) + 1
                w = [(ss, evac_tgt[4 + hp])]
                if j >= 2:
                    w.append((ss, exp_done[j - 2]))
                else:
                    w += list(L.guard(("pb", bA)) + L.guard(("pb", bB)))
                    w += list(cur_sync)

                diag = j >= 4 * c

                def s1(e, j=j, qA=qA, kA=kA, bA=bA, diag=diag):
                    return e.matmul(
                        out=pb[bA][:], lhsT=kA[0:64, j * 128:(j + 1) * 128],
                        rhs=qA[0:64, :],
                        start=True, stop=not diag)

                def s2(e, j=j, qA=qA, kA=kA, bB=bB, diag=diag):
                    return e.matmul(
                        out=pb[bB][:],
                        lhsT=kA[64:128, j * 128:(j + 1) * 128],
                        rhs=qA[64:128, :],
                        start=True, stop=not diag)
                p.add("tensor", s1, waits=w)
                p.add("tensor", s2, inc=None if diag else st)
                if diag:
                    nm = negm[:, (j - 4 * c) * 1024:(j - 4 * c) * 1024 + 512]

                    def mA(e, bA=bA, nm=nm):
                        return e.matmul(out=pb[bA][:], lhsT=idsb[:],
                                        rhs=nm, start=False, stop=True)

                    def mB(e, bB=bB, nm=nm):
                        return e.matmul(out=pb[bB][:], lhsT=idsb[:],
                                        rhs=nm, start=False, stop=True)
                    p.add("tensor", mA, waits=((sv, negm_rdy), (dx, w_id)))
                    p.add("tensor", mB, inc=st)
                w_s = p.now(st)
                pAB = ptile[:, (j % 2) * 1024:(j % 2) * 1024 + 1024]
                pA = pAB[:, 0:512]
                pB = pAB[:, 512:1024]

                def e12(e, par=j % 2, pAB=pAB):
                    return e.activation(out=pAB, in_=pq[par][:], func=AF.Exp,
                                        bias=neg24, scale=0.125)
                p.add("scalar", e12, waits=((st, w_s),), inc=ss)
                exp_done[j] = p.now(ss)
                avw = [(ss, exp_done[j]), (sv, vtok_done)]
                vb = (j * MYH + 2 * hp) * 65
                if j == 0:
                    avw += list(L.guard(("po", 0)) + L.guard(("po", 1)))

                def a1(e, pA=pA, vb=vb, j=j, njc=njc):
                    return e.matmul(out=po[0][:],
                                    lhsT=vtok[:, vb:vb + 65], rhs=pA,
                                    start=(j == 0), stop=(j == njc - 1))

                def a2(e, pB=pB, vb=vb, j=j, njc=njc):
                    return e.matmul(out=po[1][:],
                                    lhsT=vtok[:, vb + 65:vb + 130], rhs=pB,
                                    start=(j == 0), stop=(j == njc - 1))
                pend.append((a1, a2, tuple(avw)))
                if len(pend) > 1:
                    flush_av(pend.pop(0))
            while pend:
                flush_av(pend.pop(0))
            av_done = p.now(st)

            for hh in range(2):
                pox = po[hh]

                def r1(e, pox=pox):
                    with nc.allow_low_precision(
                            reason="bf16 denom broadcast within tolerance"):
                        return e.reciprocal(out=rd[:], in_=pox[64:65, :])
                p.add("vector", r1,
                      waits=((st, av_done),) + L.guard("rd"), inc=sv)
                w_r = p.now(sv)

                def bc(e):
                    return e.matmul(out=pq[0][0:64, 0:512], lhsT=ones64[:],
                                    rhs=rd[:], start=True, stop=True)
                p.add("tensor", bc,
                      waits=((sv, w_r),) + L.guard(("pb", 0)), inc=st)
                w_b = p.now(st)
                L.set_guard("rd", st, w_b)

                def oc(e, pox=pox):
                    return e.copy(out=o_s[:], in_=pox[0:64, :])
                p.add("scalar", oc,
                      waits=((st, av_done),) + L.guard("o_s"), inc=ss)
                w_oc = p.now(ss)
                dst = oTb[64 * hh:64 * hh + 64,
                          hp * 512:(hp + 1) * 512]

                def dv(e, dst=dst):
                    return e.tensor_mul(out=dst, in0=o_s[:],
                                        in1=pq[0][0:64, 0:512])
                p.add("vector", dv, waits=((st, w_b), (ss, w_oc)), inc=sv)
                L.set_guard("o_s", sv, p.now(sv))
                L.set_guard(("pb", 0), sv, p.now(sv))
            L.set_guard(("po", 0), ss, p.now(ss))
            L.set_guard(("po", 1), ss, p.now(ss))
            L.set_guard(("po", 0), sv, p.now(sv))
            L.set_guard(("po", 1), sv, p.now(sv))
            cur_sync = ((ss, p.now(ss)), (sv, p.now(sv)))
            if hp == 1 and c + 1 < 8:
                pfet[c + 1] = emit_prefetch(c + 1)
        oT_done = p.now(sv)

        for tt in range(4):
            for f2 in range(2):
                bk = 2 + (tt * 2 + f2) % 2
                for ci in range(4):
                    w = ()
                    if ci == 0:
                        w = ((sv, oT_done), (gx, w_wdma)) + \
                            L.guard(("pb", bk))

                    def pm(e, tt=tt, f2=f2, ci=ci, bk=bk):
                        return e.matmul(
                            out=pb[bk][:],
                            lhsT=oTb[:, ci * 512 + tt * 128:
                                     ci * 512 + (tt + 1) * 128],
                            rhs=wpsb[:, ci * D + f2 * 512:
                                     ci * D + (f2 + 1) * 512],
                            start=(ci == 0), stop=(ci == 3))
                    p.add("tensor", pm, waits=w,
                          inc=st if ci == 3 else None)
                w_pm = p.now(st)
                dst = stage[:, tt * D + f2 * 512:
                            tt * D + f2 * 512 + 512]

                def ev(e, dst=dst, bk=bk):
                    return e.tensor_copy(out=dst, in_=pb[bk][:])
                p.add("vector", ev,
                      waits=((st, w_pm),) + L.guard("stage"),
                      inc=sv)
                L.set_guard(("pb", bk), sv, p.now(sv))
        w_stage = p.now(sv)

        def sdma(e, c=c):
            return e.dma_start(
                out=s0p[c * 512:(c + 1) * 512, :].rearrange(
                    "(t p) n -> p t n", p=128),
                in_=stage[:].rearrange("p (t n) -> p t n", t=4),
                accum_op=(ALU.add if d.get("s0p_accum") else ALU.bypass))
        p.add("gpsimd", sdma, waits=((sv, w_stage),), inc=so, inc_by=16)
        L.set_guard("stage", so, p.now(so))

    if not d.get('skip_tail'):
        g0src = d.get('g0_src_ap')
        if g0src is None:
            g0src = x_own
        for oc in range(d.get('g0_chunks', 4)):
            par = oc % 2
            cbuf = xbuf[:, par * 4 * D:(par + 1) * 4 * D]

            def xdma(e, oc=oc, cbuf=cbuf):
                return e.dma_start(
                    out=cbuf.rearrange("p (t n) -> p t n", t=4),
                    in_=g0src[oc * 512:(oc + 1) * 512, :].rearrange(
                        "(t p) n -> p t n", p=128))
            p.add("gpsimd", xdma, waits=L.guard(("xbuf", par)),
                  inc=gx, inc_by=16)
            w_x = p.now(gx)
            xnT_done = chunk_ln_transpose(cbuf, ((gx, w_x),))
            L.set_guard(("xbuf", par), sv, xnT_done)
            for tt in range(4):
                for f2 in range(2):
                    bk = (tt * 2 + f2) % 2
                    for ci in range(8):
                        w = ()
                        if ci == 0:
                            w = ((sv, xnT_done), (ss, w_sgn)) + \
                                L.guard(("pb", bk))

                        def mfn(e, tt=tt, f2=f2, ci=ci, bk=bk):
                            return e.matmul(
                                out=pb[bk][:],
                                lhsT=xnT[:, ci * 512 + tt * 128:
                                         ci * 512 + (tt + 1) * 128],
                                rhs=sgnT[:, ci * D + f2 * 512:
                                         ci * D + (f2 + 1) * 512],
                                start=(ci == 0), stop=(ci == 7))
                        p.add("tensor", mfn, waits=w,
                              inc=st if ci == 7 else None)
                    w_mm = p.now(st)
                    dst = gstage[:, tt * D + f2 * 512:
                                 tt * D + f2 * 512 + 512]

                    def ev(e, dst=dst, bk=bk):
                        return e.activation(out=dst, in_=pb[bk][:],
                                            func=AF.Silu)
                    p.add("scalar", ev,
                          waits=((st, w_mm),) + L.guard("gstage"),
                          inc=ss)
                    L.set_guard(("pb", bk), ss, p.now(ss))
            w_ev = p.now(ss)

            def gdma(e, oc=oc):
                return e.dma_start(
                    out=g0[oc * 512:(oc + 1) * 512, :].rearrange(
                        "(t p) n -> p t n", p=128),
                    in_=gstage[:].rearrange("p (t n) -> p t n", t=4))
            p.add("sync", gdma, waits=((ss, w_ev),), inc=so, inc_by=16)
            L.set_guard("gstage", so, p.now(so))


    p.add("sync", lambda e: e.nop(), waits=((so, p.now(so)),))


def build_launch_A():
    L = Launch()
    d = {}
    d["x_full"] = L.inp("x_full", [S, D])
    d["x_own"] = L.inp("x_own", [TOK, D])
    d["w1p"] = L.inp("w1p", [D, 1536])
    d["b1p"] = L.inp("b1p", [128, 12])
    d["w0t"] = L.inp("w0t", [D, D])
    d["wpmy"] = L.inp("wpmy", [512, D])
    d["idn"] = L.inp("idn", [128, 128], BF16)
    d["dmask"] = L.inp("dmask", [4 * 128, 1024], BF16)
    d["s0p"] = L.out("s0p", [S, D])
    d["g0"] = L.out("g0", [TOK, D])
    with L.es:
        phase_A(L, d)
        L.p.emit()
    return L.nc



def phase_C(L, d):
    nc, p = L.nc, L.p
    wq = d.get("wq"); bq = d.get("bq")
    wk = d.get("wk"); bk = d.get("bk")
    wv = d.get("wv"); bv = d.get("bv")
    wc = d.get("wc"); we = d.get("we")
    idn = d.get("idn")
    trimask = d.get("trimask")
    duplo = d.get("duplo")
    duphi = d.get("duphi")
    qfo = d.get("qfo")
    numx = d.get("numx")
    tex = d.get("tex")
    x1_own_src = d.get("x1_own_src")

    sb, ps = L.sb, L.ps
    x1bP = [sb(f"x1b{i}", [128, 4 * D]) for i in range(2)]
    xn_bfP = [sb(f"xn_bf{i}", [128, 4 * D], BF16) for i in range(2)]
    xnTP = [sb(f"xnT{i}", [128, 8 * 512], BF16) for i in range(2)]
    wqsb = sb("wqsb", [128, 8 * D], BF16)
    wksb = sb("wksb", [128, 8 * 256], BF16)
    wvsb = sb("wvsb", [128, 8 * 256], BF16)
    wcsb = sb("wcsb", [128, 2 * 64], BF16)
    wesb = sb("wesb", [64, 256], BF16)
    bqsb = sb("bqsb", [128, 8])
    bksb = sb("bksb", [128, 2])
    bvsb = sb("bvsb", [128, 2])
    idsb = sb("idsb", [128, 128], BF16)
    trisb = sb("trisb", [128, 128], BF16)
    dlosb = sb("dlosb", [128, 128], BF16)
    dhisb = sb("dhisb", [128, 128], BF16)
    kfdupP = [sb(f"kfdup{i}", [128, 4 * 512], BF16) for i in range(2)]
    sprevDP = [sb(f"sprevD{i}", [128, 260], BF16) for i in range(2)]
    qtmpP = [sb(f"qtmp{i}", [128, 512]) for i in range(2)]
    t1P = [sb(f"t1_{i}", [128, 512]) for i in range(2)]
    t2P = [sb(f"t2_{i}", [128, 512]) for i in range(2)]
    t3P = [sb(f"t3_{i}", [128, 512]) for i in range(2)]
    qfTP = [sb(f"qfT{i}", [128, 8 * 512], BF16) for i in range(2)]
    krawP = [sb(f"kraw{i}", [128, 2 * 512], BF16) for i in range(2)]
    vrawP = [sb(f"vraw{i}", [128, 2 * 512], BF16) for i in range(2)]
    ckTP = [sb(f"ckT{i}", [64, 512], BF16) for i in range(2)]
    cvTP = [sb(f"cvT{i}", [64, 512], BF16) for i in range(2)]
    kfTP = [sb(f"kfT{i}", [128, 2 * 512], BF16) for i in range(2)]
    kftokP = [sb(f"kftok{i}", [128, 4 * 256], BF16) for i in range(2)]
    v2eP = [sb(f"v2e{i}", [128, 4 * 260], BF16) for i in range(2)]
    asbP = [sb(f"asb{i}", [128, 128], BF16) for i in range(2)]
    sprev = sb("sprev", [64, 260], BF16)
    sprev32 = sb("sprev32", [64, 260])
    numstage = sb("numstage", [128, 16 * 65])
    lsq = sb("lsq", [128, D])
    lsm = sb("lsm", [128, 32])
    consts = sb("consts", [128, 4])

    pb = [ps(f"pb{i}", [128, 512]) for i in range(4)]
    pnb = [ps(f"pn{i}", [128, 512]) for i in range(2)]
    pn = [t[:, 0:65] for t in pnb]
    pSb = ps("pS", [128, 512])
    pS = pSb[0:64, 0:260]
    ptb = ps("pt0", [128, 1024], BF16)
    pt0 = ptb[:, 0:512]
    ptP = [ptb[:, 0:512], ptb[:, 0:512]]

    dx = L.sem("dx")
    gx = L.sem("gx")
    st = L.sem("st")
    sv = L.sem("sv")
    ss = L.sem("ss")
    so = L.sem("so")
    sg = L.sem("sg")
    L.sv, L.ss = sv, ss

    eps5 = consts[:, 0:1]

    def ln_sc(i):
        return {"sq": lsq, "s": lsm[:, 6 * i:6 * i + 6]}

    p.add("vector", lambda e: e.memset(eps5, 1e-5), inc=sv)
    for par in range(2):
        p.add("vector", lambda e, par=par: e.memset(v2eP[par][:], 1.0),
              inc=sv)
    base_scr = d.get("base_scr")
    p.add("vector", lambda e: e.memset(sprev32[:], 0.0), inc=sv)
    ones_rdy = p.now(sv)
    p.add("sync", lambda e: e.dma_start(out=idsb[:], in_=idn),
          inc=dx, inc_by=16)
    p.add("sync", lambda e: e.dma_start(out=trisb[:], in_=trimask),
          inc=dx, inc_by=16)
    p.add("sync", lambda e: e.dma_start(out=dlosb[:], in_=duplo),
          inc=dx, inc_by=16)
    p.add("sync", lambda e: e.dma_start(out=dhisb[:], in_=duphi),
          inc=dx, inc_by=16)
    for bsrc, bdst in [(bq, bqsb), (bk, bksb), (bv, bvsb)]:
        p.add("sync", lambda e, s=bsrc, d=bdst: e.dma_start(out=d[:], in_=s),
              inc=dx, inc_by=16)
    w_small = p.now(dx)
    p.add("gpsimd", lambda e: e.dma_start(
        out=wqsb[:].rearrange("p (c n) -> p c n", c=8),
        in_=wq.rearrange("(c p) n -> p c n", p=128)), inc=gx, inc_by=16)
    p.add("gpsimd", lambda e: e.dma_start(
        out=wksb[:].rearrange("p (c n) -> p c n", c=8),
        in_=wk.rearrange("(c p) n -> p c n", p=128)), inc=gx, inc_by=16)
    p.add("gpsimd", lambda e: e.dma_start(
        out=wvsb[:].rearrange("p (c n) -> p c n", c=8),
        in_=wv.rearrange("(c p) n -> p c n", p=128)), inc=gx, inc_by=16)
    p.add("gpsimd", lambda e: e.dma_start(
        out=wcsb[:].rearrange("p (c n) -> p c n", c=2),
        in_=wc.rearrange("(c p) n -> p c n", p=128)), inc=gx, inc_by=16)
    p.add("gpsimd", lambda e: e.dma_start(out=wesb[:], in_=we),
          inc=gx, inc_by=16)
    w_wdma = p.now(gx)
    p.add("sync", lambda e: e.dma_start(out=sprev32[:], in_=base_scr),
          waits=((sv, ones_rdy),), inc=dx, inc_by=16)
    w_bl = p.now(dx)
    p.add("vector", lambda e: e.tensor_copy(out=sprev[:], in_=sprev32[:]),
          waits=((dx, w_bl),), inc=sv)
    sprev_rdy = p.now(sv)
    def ddm0(e):
        return e.matmul(out=pb[3][:, 0:260], lhsT=dlosb[0:64, :],
                        rhs=sprev[:], start=True, stop=True)
    p.add("tensor", ddm0, waits=((sv, sprev_rdy), (dx, w_small)), inc=st)
    w_dd0 = p.now(st)
    p.add("vector", lambda e: e.tensor_copy(out=sprevDP[0][:],
                                            in_=pb[3][:, 0:260]),
          waits=((st, w_dd0),), inc=sv)
    L.set_guard(("pb", 3), sv, p.now(sv))

    st_kfdup = {}
    st_v2 = {}
    st_kftok = {}
    st_qf = {}
    qfdma_so = {}

    def emit_ln(oc):
        par = oc % 2
        x1b, xn_bf, xnT = x1bP[par], xn_bfP[par], xnTP[par]
        def dx1(e, oc=oc, x1b=x1b):
            return e.dma_start(
                out=x1b[:].rearrange("p (t n) -> p t n", t=4),
                in_=x1_own_src[oc * 512:(oc + 1) * 512, :].rearrange(
                    "(t p) n -> p t n", p=128))
        p.add("sync", dx1, waits=L.guard(("x1b", par)), inc=dx, inc_by=16)
        w_x1d = p.now(dx)
        tiles = [(x1b[:, t * D:(t + 1) * D], xn_bf[:, t * D:(t + 1) * D],
                  ln_sc(t)) for t in range(4)]
        w_apply = emit_ln_tiles(L, tiles, waits=((dx, w_x1d),), eps_ap=eps5)
        L.set_guard(("x1b", par), sv, w_apply)
        for ci in range(8):
            bkp = ci % 2
            for t in range(4):
                def tfn(e, ci=ci, t=t, xn_bf=xn_bf, bkp=bkp):
                    return e.transpose(
                        out=ptP[bkp][:, t * 128:(t + 1) * 128],
                        in_=xn_bf[:, t * D + ci * 128:t * D + ci * 128 + 128],
                        identity=idsb[:])
                p.add("tensor", tfn,
                      waits=(((sv, w_apply), (dx, w_small)) +
                             L.guard("ptC") if t == 0 else ()),
                      inc=st if t == 3 else None)
            w_t = p.now(st)
            def cfn(e, ci=ci, xnT=xnT, bkp=bkp):
                return e.tensor_copy(out=xnT[:, ci * 512:(ci + 1) * 512],
                                     in_=ptP[bkp][:])
            p.add("vector", cfn, waits=((st, w_t),), inc=sv)
            L.set_guard("ptC", sv, p.now(sv))
        return p.now(sv)

    ln_done = {}

    def emit_q(oc):
        par = oc % 2
        xnT, qfT = xnTP[par], qfTP[par]
        xnT_done = ln_done[oc]
        qf_guard = L.guard(("qfT", par))
        for f1 in range(8):
            bk_ = f1 % 2
            fp = f1 % 2
            qtmp, t1, t2, t3 = qtmpP[fp], t1P[fp], t2P[fp], t3P[fp]
            for ci in range(8):
                w = ()
                if ci == 0:
                    w = ((sv, xnT_done), (gx, w_wdma)) + L.guard(("pb", bk_))
                def mfn(e, f1=f1, ci=ci, bk_=bk_, xnT=xnT):
                    return e.matmul(
                        out=pb[bk_][:],
                        lhsT=wqsb[:, ci * D + f1 * 128:ci * D + (f1 + 1) * 128],
                        rhs=xnT[:, ci * 512:(ci + 1) * 512],
                        start=(ci == 0), stop=(ci == 7))
                p.add("tensor", mfn, waits=w, inc=st if ci == 7 else None)
            w_mm = p.now(st)
            def efn(e, f1=f1, bk_=bk_, qtmp=qtmp):
                return e.activation(out=qtmp[:], in_=pb[bk_][:],
                                    func=AF.Identity,
                                    bias=bqsb[:, f1:f1 + 1], scale=1.0)
            p.add("scalar", efn,
                  waits=((st, w_mm), (dx, w_small)) + L.guard(("qtmp", fp)),
                  inc=ss)
            w_e = p.now(ss)
            L.set_guard(("pb", bk_), ss, w_e)
            p.add("vector", lambda e, qtmp=qtmp, t1=t1: e.tensor_scalar_min(
                out=t1[:], in0=qtmp[:], scalar1=0.0),
                waits=((ss, w_e),) + L.guard(("t1", fp)), inc=sv)
            p.add("vector", lambda e, qtmp=qtmp, t3=t3: e.tensor_scalar_max(
                out=t3[:], in0=qtmp[:], scalar1=0.0),
                waits=L.guard(("t3", fp)), inc=sv)
            w_mn = p.now(sv)
            L.set_guard(("qtmp", fp), sv, w_mn)
            p.add("scalar", lambda e, t1=t1, t2=t2: e.activation(
                out=t2[:], in_=t1[:], func=AF.Exp),
                waits=((sv, w_mn),) + L.guard(("t2", fp)), inc=ss)
            w_ex = p.now(ss)
            L.set_guard(("t1", fp), ss, w_ex)
            dst = qfT[:, f1 * 512:(f1 + 1) * 512]
            p.add("vector", lambda e, dst=dst, t2=t2, t3=t3: e.tensor_add(
                out=dst, in0=t2[:], in1=t3[:]),
                waits=((ss, w_ex), (sv, w_mn)) + (qf_guard if f1 == 0 else ()),
                inc=sv)
            L.set_guard(("t2", fp), sv, p.now(sv))
            L.set_guard(("t3", fp), sv, p.now(sv))
        qf_done = p.now(sv)
        def qfdma(e, oc=oc, qfT=qfT):
            return e.dma_start(
                out=qfo[:, oc * 512:(oc + 1) * 512].rearrange(
                    "(c p) n -> p c n", p=128),
                in_=qfT[:].rearrange("p (c n) -> p c n", c=8))
        p.add("gpsimd", qfdma, waits=((sv, qf_done),), inc=so, inc_by=16)
        qfdma_so[oc] = p.now(so)
        st_qf[oc] = qf_done

    def emit_kv(oc):
        par = oc % 2
        xnT = xnTP[par]
        kraw, vraw = krawP[par], vrawP[par]
        ckT, cvT, kfT = ckTP[par], cvTP[par], kfTP[par]
        kftok, v2e, kfdup = kftokP[par], v2eP[par], kfdupP[par]
        xnT_done = ln_done[oc]
        for (wsb, bsb, dstraw) in ((wksb, bksb, kraw), (wvsb, bvsb, vraw)):
            for f1 in range(2):
                bk_ = 2 + f1
                for ci in range(8):
                    w = ()
                    if ci == 0:
                        w = ((sv, xnT_done), (gx, w_wdma)) + \
                            L.guard(("pb", bk_))
                    def mfn(e, f1=f1, ci=ci, bk_=bk_, wsb=wsb, xnT=xnT):
                        return e.matmul(
                            out=pb[bk_][:],
                            lhsT=wsb[:, ci * 256 + f1 * 128:
                                     ci * 256 + (f1 + 1) * 128],
                            rhs=xnT[:, ci * 512:(ci + 1) * 512],
                            start=(ci == 0), stop=(ci == 7))
                    p.add("tensor", mfn, waits=w, inc=st if ci == 7 else None)
                w_mm = p.now(st)
                def efn(e, f1=f1, bk_=bk_, bsb=bsb, dstraw=dstraw):
                    return e.activation(
                        out=dstraw[:, f1 * 512:(f1 + 1) * 512],
                        in_=pb[bk_][:], func=AF.Identity,
                        bias=bsb[:, f1:f1 + 1], scale=1.0)
                p.add("scalar", efn, waits=((st, w_mm), (dx, w_small)),
                      inc=ss)
                L.set_guard(("pb", bk_), ss, p.now(ss))
        kv_done = p.now(ss)
        for (rawb, cdst) in ((kraw, ckT), (vraw, cvT)):
            for ci in range(2):
                def mfn(e, ci=ci, rawb=rawb):
                    return e.matmul(
                        out=pb[0][0:64, :],
                        lhsT=wcsb[:, ci * 64:(ci + 1) * 64],
                        rhs=rawb[:, ci * 512:(ci + 1) * 512],
                        start=(ci == 0), stop=(ci == 1))
                p.add("tensor", mfn,
                      waits=((ss, kv_done),) + L.guard(("pb", 0))
                      if ci == 0 else (), inc=st if ci == 1 else None)
            w_mm = p.now(st)
            def efn(e, cdst=cdst):
                return e.tensor_copy(out=cdst[:], in_=pb[0][0:64, :])
            p.add("vector", efn, waits=((st, w_mm),), inc=sv)
            L.set_guard(("pb", 0), sv, p.now(sv))
        c_done = p.now(sv)
        for f1 in range(2):
            fp = f1 % 2
            qtmp, t1, t2, t3 = qtmpP[fp], t1P[fp], t2P[fp], t3P[fp]
            def mfn(e, f1=f1, ckT=ckT):
                return e.matmul(
                    out=pb[1][:],
                    lhsT=wesb[:, f1 * 128:(f1 + 1) * 128],
                    rhs=ckT[:], start=True, stop=True)
            p.add("tensor", mfn,
                  waits=((sv, c_done), (gx, w_wdma)) + L.guard(("pb", 1)),
                  inc=st)
            w_mm = p.now(st)
            def efn(e, qtmp=qtmp):
                return e.copy(out=qtmp[:], in_=pb[1][:])
            p.add("scalar", efn,
                  waits=((st, w_mm),) + L.guard(("qtmp", fp)), inc=ss)
            w_e = p.now(ss)
            L.set_guard(("pb", 1), ss, w_e)
            p.add("vector", lambda e, qtmp=qtmp, t1=t1: e.tensor_scalar_min(
                out=t1[:], in0=qtmp[:], scalar1=0.0),
                waits=((ss, w_e),) + L.guard(("t1", fp)), inc=sv)
            p.add("vector", lambda e, qtmp=qtmp, t3=t3: e.tensor_scalar_max(
                out=t3[:], in0=qtmp[:], scalar1=0.0),
                waits=L.guard(("t3", fp)), inc=sv)
            w_mn = p.now(sv)
            L.set_guard(("qtmp", fp), sv, w_mn)
            p.add("scalar", lambda e, t1=t1, t2=t2: e.activation(
                out=t2[:], in_=t1[:], func=AF.Exp),
                waits=((sv, w_mn),) + L.guard(("t2", fp)), inc=ss)
            w_ex = p.now(ss)
            L.set_guard(("t1", fp), ss, w_ex)
            dst = kfT[:, f1 * 512:(f1 + 1) * 512]
            p.add("vector", lambda e, dst=dst, t2=t2, t3=t3: e.tensor_add(
                out=dst, in0=t2[:], in1=t3[:]),
                waits=((ss, w_ex), (sv, w_mn)), inc=sv)
            L.set_guard(("t2", fp), sv, p.now(sv))
            L.set_guard(("t3", fp), sv, p.now(sv))
        kf_done = p.now(sv)
        for t in range(4):
            def mfn(e, t=t, cvT=cvT):
                return e.matmul(
                    out=pb[2][:, 0:256],
                    lhsT=cvT[:, t * 128:(t + 1) * 128],
                    rhs=wesb[:], start=True, stop=True)
            p.add("tensor", mfn,
                  waits=((sv, c_done), (gx, w_wdma)) + L.guard(("pb", 2)),
                  inc=st)
            w_mm = p.now(st)
            for g in range(4):
                dst = v2e[:, t * 260 + g * 65:t * 260 + g * 65 + 64]
                def efn(e, dst=dst, g=g):
                    return e.tensor_copy(out=dst,
                                         in_=pb[2][:, g * 64:(g + 1) * 64])
                p.add("vector", efn,
                      waits=((st, w_mm), (sv, ones_rdy)) +
                      (L.guard(("v2e", par)) if t == 0 and g == 0 else ()),
                      inc=sv if g == 3 else None)
            L.set_guard(("pb", 2), sv, p.now(sv))
        v2_done = p.now(sv)
        for ci in range(2):
            for t in range(4):
                bkp = (ci * 4 + t) % 2
                def tfn(e, ci=ci, t=t, kfT=kfT, bkp=bkp):
                    return e.transpose(
                        out=ptP[bkp][:, 0:128],
                        in_=kfT[:, ci * 512 + t * 128:ci * 512 + (t + 1) * 128],
                        identity=idsb[:])
                p.add("tensor", tfn,
                      waits=((sv, kf_done),) + L.guard("ptC"), inc=st)
                w_t = p.now(st)
                dst = kftok[:, t * 256 + ci * 128:t * 256 + (ci + 1) * 128]
                def cfn(e, dst=dst, bkp=bkp):
                    return e.tensor_copy(out=dst, in_=ptP[bkp][:, 0:128])
                p.add("vector", cfn,
                      waits=((st, w_t),) +
                      (L.guard(("kftok", par)) if ci == 0 and t == 0 else ()),
                      inc=sv)
                L.set_guard("ptC", sv, p.now(sv))
        kftok_done = p.now(sv)
        for g in range(4):
            sel = dlosb if g % 2 == 0 else dhisb
            def dmm(e, g=g, sel=sel, kfT=kfT):
                return e.matmul(
                    out=pb[3][:],
                    lhsT=sel[:], rhs=kfT[:, (g // 2) * 512:(g // 2 + 1) * 512],
                    start=True, stop=True)
            p.add("tensor", dmm,
                  waits=((sv, kf_done), (dx, w_small)) + L.guard(("pb", 3)),
                  inc=st)
            w_d2 = p.now(st)
            def dcp(e, g=g, kfdup=kfdup):
                return e.tensor_copy(out=kfdup[:, g * 512:(g + 1) * 512],
                                     in_=pb[3][:])
            p.add("vector", dcp,
                  waits=((st, w_d2),) +
                  (L.guard(("kfdup", par)) if g == 0 else ()), inc=sv)
            L.set_guard(("pb", 3), sv, p.now(sv))
        kfdup_done = p.now(sv)
        st_kfdup[oc] = kfdup_done
        st_v2[oc] = v2_done
        st_kftok[oc] = kftok_done

    def emit_scan_t(oc, t):
        par = oc % 2
        qfT, kftok = qfTP[par], kftokP[par]
        v2e, kfdup = v2eP[par], kfdupP[par]
        j = 4 * oc + t
        jpar = j % 2
        for g in range(4):
            def smm(e, t=t, g=g, kftok=kftok, v2e=v2e):
                return e.matmul(
                    out=pS[:, g * 65:(g + 1) * 65],
                    lhsT=kftok[:, t * 256 + g * 64:t * 256 + (g + 1) * 64],
                    rhs=v2e[:, t * 260 + g * 65:t * 260 + (g + 1) * 65],
                    start=(g == 0), stop=(g == 3))
            p.add("tensor", smm,
                  waits=((sv, max(st_kftok[oc], st_v2[oc])),) +
                  (L.guard("pS") if g == 0 else ()),
                  inc=st if g == 3 else None)
        w_s = p.now(st)
        p.add("vector", lambda e: e.tensor_add(
            out=sprev32[:], in0=sprev32[:], in1=pS[:]),
            waits=((st, w_s),), inc=sv)
        w_acc = p.now(sv)
        p.add("vector", lambda e: e.tensor_copy(
            out=sprev[:], in_=sprev32[:]),
            waits=((sv, w_acc),) + L.guard("sprev"), inc=sv)
        w_cast = p.now(sv)
        L.set_guard("pS", sv, w_cast)
        ddm_emitted = [False]

        def emit_ddm():
            ddm_emitted[0] = True
            def ddm(e):
                return e.matmul(out=pb[3][:, 0:260], lhsT=dlosb[0:64, :],
                                rhs=sprev[:], start=True, stop=True)
            p.add("tensor", ddm,
                  waits=((sv, w_cast),) + L.guard(("pb", 3)), inc=st)
            w_dd = p.now(st)
            L.set_guard("sprev", st, w_dd)
            def dcp(e, jpar=jpar):
                return e.tensor_copy(
                    out=sprevDP[1 - jpar][:], in_=pb[3][:, 0:260])
            p.add("vector", dcp,
                  waits=((st, w_dd),) + L.guard(("sprevD", 1 - jpar)),
                  inc=sv)
            L.set_guard(("pb", 3), sv, p.now(sv))

        sprevD = sprevDP[jpar]
        pend = []

        def emit_n(item):
            h, qlh, g, hp2, hb, w_m = item
            nb = pn[hp2]
            def n1(e, nb=nb, t=t, g=g, hp2=hp2, v2e=v2e):
                return e.matmul(
                    out=nb[:], lhsT=asbP[hp2][:],
                    rhs=v2e[:, t * 260 + g * 65:t * 260 + (g + 1) * 65],
                    start=True, stop=False)
            def n2(e, nb=nb, qlh=qlh, g=g, hb=hb, sprevD=sprevD):
                return e.matmul(
                    out=nb[:], lhsT=qlh,
                    rhs=sprevD[hb:hb + 64, g * 65:(g + 1) * 65],
                    start=False, stop=True)
            p.add("tensor", n1,
                  waits=((sv, max(w_m, st_v2[oc])),) +
                  L.guard(("pn", hp2)))
            p.add("tensor", n2, inc=st)
            w_n = p.now(st)
            L.set_guard(("asb", hp2), st, w_n)
            dst = numstage[:, h * 65:(h + 1) * 65]
            def nev(e, dst=dst, nb=nb):
                return e.tensor_copy(out=dst, in_=nb[:])
            p.add("vector", nev,
                  waits=((st, w_n),) + L.guard("numstage"), inc=sv)
            L.set_guard(("pn", hp2), sv, p.now(sv))

        for h in range(16):
            g = h // 4
            hp2 = h % 2
            hb = (h % 2) * 64
            klh = kfdup[hb:hb + 64,
                        g * 512 + t * 128:g * 512 + (t + 1) * 128]
            qslice = qfT[:, (h // 2) * 512 + t * 128:
                         (h // 2) * 512 + (t + 1) * 128]
            qlh = qslice[hb:hb + 64, :]
            def amm(e, klh=klh, qlh=qlh, hp2=hp2):
                return e.matmul(out=pb[hp2][:, 0:128], lhsT=klh,
                                rhs=qlh, start=True, stop=True)
            p.add("tensor", amm,
                  waits=((sv, st_kfdup[oc]),) + L.guard(("pb", hp2)),
                  inc=st)
            w_a = p.now(st)
            def msk(e, hp2=hp2):
                return e.tensor_mul(out=asbP[hp2][:], in0=pb[hp2][:, 0:128],
                                    in1=trisb[:])
            p.add("vector", msk,
                  waits=((st, w_a), (dx, w_small)) + L.guard(("asb", hp2)),
                  inc=sv)
            w_m = p.now(sv)
            L.set_guard(("pb", hp2), sv, w_m)
            pend.append((h, qlh, g, hp2, hb, w_m))
            if h == 1:
                emit_ddm()
            if len(pend) > 2:
                emit_n(pend.pop(0))
        while pend:
            emit_n(pend.pop(0))
        if not ddm_emitted[0]:
            emit_ddm()
        L.set_guard(("sprevD", jpar), st, p.now(st))
        w_ns = p.now(sv)
        w_ng = p.now(sg)
        def ndma(e, j=j):
            return e.dma_start(
                out=numx[j * 128:(j + 1) * 128, :],
                in_=numstage[:])
        p.add("sync", ndma, waits=((sv, w_ns), (sg, w_ng)), inc=so,
              inc_by=16)
        L.set_guard("numstage", so, p.now(so))

    ln_done[0] = emit_ln(0)
    emit_q(0)
    emit_kv(0)
    for oc in range(4):
        par = oc % 2
        if oc + 1 < 4:
            ln_done[oc + 1] = emit_ln(oc + 1)
        emit_scan_t(oc, 0)
        emit_scan_t(oc, 1)
        if oc + 1 < 4:
            emit_q(oc + 1)
        emit_scan_t(oc, 2)
        if oc + 1 < 4:
            emit_kv(oc + 1)
        emit_scan_t(oc, 3)
        w_st = p.now(st)
        L.set_guard(("kfdup", par), st, w_st)
        L.set_guard(("v2e", par), st, w_st)
        L.set_guard(("kftok", par), st, w_st)
        L.guards[("qfT", par)] = [(st, w_st), (so, qfdma_so[oc])]

    def tdma(e):
        return e.dma_start(
            out=tex.rearrange("(g p) n -> p g n", p=64),
            in_=sprev32[:].rearrange("p (g n) -> p g n", g=4))
    p.add("sync", tdma, waits=((sv, p.now(sv)),), inc=so, inc_by=16)

    p.add("sync", lambda e: e.nop(), waits=((so, p.now(so)),))



def phase_D(L, d):
    nc, p = L.nc, L.p
    x1i = d.get("x1i")
    qft = d.get("qft")
    numx = d.get("numx")
    texd = d.get("texd")
    wo = d.get("wo")
    bo = d.get("bo")
    gam = d.get("gam")
    bet = d.get("bet")
    gw = d.get("gw")
    selm = d.get("selm")
    wgu = d.get("wgu")
    wd = d.get("wd")
    idn = d.get("idn")
    yout = d.get("yout")
    s_scr = d["s_scr"]

    sb, ps = L.sb, L.ps
    x1b = sb("x1b", [128, D])
    nmx = sb("nmx", [128, 16 * 65])
    qfb = sb("qfb", [128, 8 * 128], BF16)
    texsb = sb("texsb", [128, 260], BF16)
    dnbuf = sb("dnbuf", [128, 16])
    rcbuf = sb("rcbuf", [128, 16])
    obuf = sb("obuf", [128, D], BF16)
    oTch = sb("oTch", [128, 8 * 128], BF16)
    xn1tok = sb("xn1tok", [128, D])
    xnbf = sb("xnbf", [128, D], BF16)
    ybuf = sb("ybuf", [128, D])
    ytmp = sb("ytmp", [128, D])
    sstage = sb("sstage", [128, D], BF16)
    bo_b = sb("bo_b", [128, D], BF16)
    gam_b = sb("gam_b", [128, D], BF16)
    bet_b = sb("bet_b", [128, D], BF16)
    ones1 = sb("ones1", [1, 128])
    ones1b = sb("ones1b", [1, 128], BF16)
    idsb = sb("idsb", [128, 128], BF16)
    lsq = sb("lsq", [128, D])
    rowbuf = lsq[0:1, :]
    lsm = sb("lsm", [128, 32])
    consts = sb("consts", [128, 4])
    xn1T = sb("xn1T", [128, 8 * TOK], BF16)
    acc_scr = d["acc_scr"]
    gwsb = sb("gwsb", [128, 8 * 8], BF16)
    lgbuf = sb("lgbuf", [128, 16 * 8])
    ebuf = sb("ebuf", [128, 16 * 8])
    smb = sb("smb", [128, 16])
    m1b = sb("m1b", [128, 16])
    m2b = sb("m2b", [128, 16])
    rnb = sb("rnb", [128, 16])
    mskb = sb("mskb", [128, 8])
    pmb = sb("pmb", [128, 16 * 8])
    p2b = sb("p2b", [128, 16 * 8])
    cwbuf = sb("cwbuf", [128, 16 * 8])
    cwbf = sb("cwbf", [128, 16 * 8], BF16)
    cwT = sb("cwT", [8, TOK], BF16)
    selsb = sb("selsb", [8, 8 * 128], BF16)
    cwb = sb("cwb", [128, 4 * 512], BF16)
    wgbuf = sb("wgbuf", [128, 2 * 8 * 512], BF16)
    gbuf16 = sb("gbuf16", [128, 8 * 128], BF16)
    wdsb = sb("wdsb", [128, 22 * D], BF16)
    wosb = wdsb[:, 0:8 * D]
    tgbuf = sb("tgbuf", [128, 2 * 1024], BF16)
    hbuf = sb("hbuf", [128, 22 * 1024], BF16)
    tbm = sb("tbm", [128, 2 * 1024], BF16)

    pq = [ps("pq0", [128, 1024]), ps("pq1", [128, 1024])]
    pb = [pq[0][:, 0:512], pq[0][:, 512:1024],
          pq[1][:, 0:512], pq[1][:, 512:1024]]
    pnb01 = ps("pn01", [128, 1024])
    pn = [pnb01[:, 0:65], pnb01[:, 512:577]]
    pnb = [pnb01[:, 0:512], pnb01[:, 512:1024]]
    ptb = ps("pt0", [128, 1024], BF16)
    pt0 = ptb[:, 0:512]

    dx = L.sem("dx")
    gx = L.sem("gx")
    st = L.sem("st")
    sv = L.sem("sv")
    ss = L.sem("ss")
    so = L.sem("so")
    sg = L.sem("sg")
    L.sv, L.ss = sv, ss

    eps5 = consts[:, 0:1]
    eps6 = consts[:, 1:2]

    def ln_sc(i):
        return {"sq": lsq, "s": lsm[:, 6 * i:6 * i + 6]}

    p.add("vector", lambda e: e.memset(eps5, 1e-5), inc=sv)
    p.add("vector", lambda e: e.memset(eps6, 1e-6), inc=sv)
    p.add("vector", lambda e: e.memset(ones1[:], 1.0), inc=sv)
    w_ones = p.now(sv)
    p.add("sync", lambda e: e.dma_start(out=idsb[:], in_=idn),
          inc=dx, inc_by=16)
    w_id = p.now(dx)

    cc_wait = d.get("cc_wait")
    tex_all = d.get("tex_all")
    if tex_all is not None:
        for half in range(2):
            def tdma(e, half=half):
                return e.dma_start(
                    out=texsb[half * 64:(half + 1) * 64, :].rearrange(
                        "p (g n) -> p g n", g=4),
                    in_=tex_all[0:256, :].rearrange(
                        "(g p) n -> p g n", p=64))
            p.add("gpsimd", tdma,
                  waits=(cc_wait,) if (cc_wait and half == 0) else (),
                  inc=gx, inc_by=16)
        w_tex0 = p.now(gx)
        p.add("sync", lambda e: e.dma_start(out=lsm[:, 30:31],
                                            in_=d["oddmask"]),
              inc=dx, inc_by=16)
        w_om = p.now(dx)
        p.add("vector", lambda e: e.tensor_scalar_mul(
            out=texsb[:], in0=texsb[:], scalar1=lsm[:, 30:31]),
            waits=((gx, w_tex0), (dx, w_om)), inc=sv)
        tex_ready = ("sv", p.now(sv))
    else:
        p.add("gpsimd", lambda e: e.dma_start(out=texsb[:], in_=texd),
              inc=gx, inc_by=16)
        tex_ready = None
    p.add("sync", lambda e: e.dma_start(out=selsb[:], in_=selm),
          inc=dx, inc_by=16)
    p.add("gpsimd", lambda e: e.dma_start(
        out=gwsb[:].rearrange("p (c n) -> p c n", c=8),
        in_=gw.rearrange("(c p) n -> p c n", p=128)), inc=gx, inc_by=16)
    p.add("gpsimd", lambda e: e.dma_start(
        out=wosb[:].rearrange("p (c n) -> p c n", c=8),
        in_=wo.rearrange("(c p) n -> p c n", p=128)), inc=gx, inc_by=16)
    w_pre = p.now(gx)
    p.add("vector", lambda e: e.memset(ones1b[:], 1.0), inc=sv)
    for i, (src, dst) in enumerate(((bo, bo_b), (gam, gam_b),
                                    (bet, bet_b))):
        def rdma(e, src=src):
            return e.dma_start(out=rowbuf[:], in_=src)
        p.add("sync", rdma, waits=L.guard("rowbuf"), inc=dx, inc_by=16)
        w_r = p.now(dx)
        for h2 in range(2):
            def bcm(e, h2=h2):
                return e.matmul(
                    out=pb[h2][:], lhsT=ones1[:],
                    rhs=rowbuf[0:1, h2 * 512:(h2 + 1) * 512],
                    start=True, stop=True)
            p.add("tensor", bcm,
                  waits=((sv, w_ones), (dx, w_r)) + L.guard(("pb", h2)),
                  inc=st)
            w_b = p.now(st)
            def bcp(e, dst=dst, h2=h2):
                return e.tensor_copy(out=dst[:, h2 * 512:(h2 + 1) * 512],
                                     in_=pb[h2][:])
            p.add("vector", bcp, waits=((st, w_b),), inc=sv)
            L.set_guard(("pb", h2), sv, p.now(sv))
        L.set_guard("rowbuf", st, p.now(st))
    bcast_rdy = p.now(sv)
    bcast_st = p.now(st)

    for j in range(16):
        def d1(e, j=j):
            return e.dma_start(out=x1b[:],
                               in_=x1i[j * 128:(j + 1) * 128, :])
        p.add("sync", d1, waits=L.guard("x1b"), inc=dx, inc_by=16)
        def d2(e, j=j):
            return e.dma_start(out=nmx[:],
                               in_=numx[j * 128:(j + 1) * 128, :])
        p.add("sync", d2, waits=L.guard("nmx"), inc=dx, inc_by=16)
        w_d12 = p.now(dx)
        def d3(e, j=j):
            return e.dma_start(
                out=qfb[:].rearrange("p (c n) -> p c n", c=8),
                in_=qft[:, j * 128:(j + 1) * 128].rearrange(
                    "(c p) n -> p c n", p=128))
        p.add("gpsimd", d3, waits=L.guard("qfb"), inc=gx, inc_by=16)
        w_d3 = p.now(gx)

        w_ln = emit_ln_tiles(L, [(x1b[:], xn1tok[:], ln_sc(0))],
                             waits=((dx, w_d12), (st, bcast_st)) +
                             L.guard("xn1tok"),
                             eps_ap=eps5)
        p.add("vector", lambda e: e.tensor_copy(out=xnbf[:], in_=xn1tok[:]),
              waits=((sv, w_ln),), inc=sv)
        w_xbf = p.now(sv)
        for ci in range(8):
            def tfn(e, ci=ci, j=j):
                return e.transpose(
                    out=pt0[:, 0:128],
                    in_=xnbf[:, ci * 128:(ci + 1) * 128],
                    identity=idsb[:])
            p.add("tensor", tfn,
                  waits=((sv, w_xbf), (dx, w_id)) + L.guard("ptD"),
                  inc=st)
            w_t = p.now(st)
            def cfn(e, ci=ci, j=j):
                return e.tensor_copy(
                    out=xn1T[:, ci * TOK + j * 128:ci * TOK + (j + 1) * 128],
                    in_=pt0[:, 0:128])
            p.add("vector", cfn, waits=((st, w_t),), inc=sv)
            L.set_guard("ptD", sv, p.now(sv))

        for h in range(16):
            g = h // 4
            hb = (h % 2) * 64
            qlh = qfb[hb:hb + 64, (h // 2) * 128:(h // 2 + 1) * 128]
            nb = pn[h % 2]
            def cmm(e, qlh=qlh, g=g, hb=hb, nb=nb):
                return e.matmul(
                    out=nb[:], lhsT=qlh,
                    rhs=texsb[hb:hb + 64, g * 65:(g + 1) * 65],
                    start=True, stop=True)
            cw_ = ((gx, max(w_d3, w_pre)),) + L.guard(("pn", h % 2))
            if tex_ready is not None:
                cw_ = cw_ + ((sv, tex_ready[1]),)
            p.add("tensor", cmm, waits=cw_, inc=st)
            w_c = p.now(st)
            dst = nmx[:, h * 65:(h + 1) * 65]
            def nadd(e, dst=dst, nb=nb):
                return e.tensor_add(out=dst, in0=dst, in1=nb[:])
            p.add("vector", nadd,
                  waits=((st, w_c), (dx, w_d12)), inc=sv)
            L.set_guard(("pn", h % 2), sv, p.now(sv))
        w_nm = p.now(sv)
        L.set_guard("qfb", st, p.now(st))
        p.add("vector", lambda e: e.tensor_scalar_add(
            out=dnbuf[:], in0=nmx[:].rearrange(
                "p (h n) -> p h n", h=16)[:, :, 64],
            scalar1=1e-6), waits=((sv, w_nm),), inc=sv)
        w_dn = p.now(sv)
        p.add("vector", lambda e: e.reciprocal(out=rcbuf[:], in_=dnbuf[:]),
              waits=((sv, w_dn),), inc=sv)
        w_rc = p.now(sv)
        for h in range(16):
            def odiv(e, h=h):
                return e.tensor_scalar_mul(
                    out=obuf[:, h * 64:(h + 1) * 64],
                    in0=nmx[:, h * 65:h * 65 + 64],
                    scalar1=rcbuf[:, h:h + 1])
            p.add("vector", odiv,
                  waits=((sv, w_rc),) + L.guard("obuf"), inc=sv)
        w_o = p.now(sv)
        L.set_guard("nmx", sv, w_o)

        for ci in range(8):
            def tfn(e, ci=ci):
                return e.transpose(
                    out=pt0[:, 0:128],
                    in_=obuf[:, ci * 128:(ci + 1) * 128],
                    identity=idsb[:])
            p.add("tensor", tfn,
                  waits=((sv, w_o),) + L.guard("ptD"), inc=st)
            w_t = p.now(st)
            def cfn(e, ci=ci):
                return e.tensor_copy(
                    out=oTch[:, ci * 128:(ci + 1) * 128],
                    in_=pt0[:, 0:128])
            p.add("vector", cfn,
                  waits=((st, w_t),) + L.guard("oTch"), inc=sv)
            L.set_guard("ptD", sv, p.now(sv))
        w_oT = p.now(sv)
        L.set_guard("obuf", sv, w_oT)

        for f2 in range(2):
            bk_ = 2 + f2
            for ci in range(8):
                w = ()
                if ci == 0:
                    w = ((sv, w_oT), (gx, w_pre)) + L.guard(("pb", bk_))
                def mfn(e, f2=f2, ci=ci, bk_=bk_):
                    return e.matmul(
                        out=pb[bk_][:],
                        lhsT=oTch[:, ci * 128:(ci + 1) * 128],
                        rhs=wosb[:, ci * D + f2 * 512:ci * D + (f2 + 1) * 512],
                        start=(ci == 0), stop=(ci == 7))
                p.add("tensor", mfn, waits=w, inc=st if ci == 7 else None)
            w_mm = p.now(st)
            cs = slice(f2 * 512, (f2 + 1) * 512)
            def v1(e, cs=cs, bk_=bk_):
                return e.tensor_add(out=ytmp[:, cs], in0=pb[bk_][:],
                                    in1=xn1tok[:, cs])
            p.add("vector", v1,
                  waits=((st, w_mm), (sv, bcast_rdy)) + L.guard("ytmp"),
                  inc=sv)
            w_v1 = p.now(sv)
            L.set_guard(("pb", bk_), sv, w_v1)
            def v2(e, cs=cs):
                return e.tensor_add(out=ybuf[:, cs], in0=ytmp[:, cs],
                                    in1=bo_b[:, cs])
            p.add("vector", v2, waits=((sv, w_v1),) + L.guard("ybuf"),
                  inc=sv)
        w_y = p.now(sv)
        L.set_guard("ytmp", sv, w_y)
        L.set_guard("xn1tok", sv, w_y)
        w_s = emit_ln_tiles(L, [(ybuf[:], sstage[:], ln_sc(1))],
                            waits=((sv, w_y),) + L.guard("sstage"),
                            eps_ap=eps5,
                            gamma_beta=(ytmp[:], gam_b[:], bet_b[:]))
        L.set_guard("ybuf", sv, w_s)
        L.set_guard("ytmp", sv, w_s)
        def sdma(e, j=j):
            return e.dma_start(out=s_scr[j * 128:(j + 1) * 128, :],
                               in_=sstage[:])
        p.add("sync", sdma, waits=((sv, w_s),), inc=so, inc_by=16)
        L.set_guard("sstage", so, p.now(so))
        L.set_guard("x1b", sv, w_ln)
    xn1T_rdy = p.now(sv)

    for j in range(16):
        for ci in range(8):
            w = ()
            if ci == 0:
                w = ((sv, xn1T_rdy), (gx, w_pre)) + L.guard(("pn", 0))
            def lmm(e, j=j, ci=ci):
                return e.matmul(
                    out=pnb[0][:, 0:8],
                    lhsT=xn1T[:, ci * TOK + j * 128:ci * TOK + (j + 1) * 128],
                    rhs=gwsb[:, ci * 8:(ci + 1) * 8],
                    start=(ci == 0), stop=(ci == 7))
            p.add("tensor", lmm, waits=w, inc=st if ci == 7 else None)
        w_mm = p.now(st)
        def lev(e, j=j):
            return e.tensor_copy(out=lgbuf[:, j * 8:(j + 1) * 8],
                                 in_=pnb[0][:, 0:8])
        p.add("vector", lev, waits=((st, w_mm),), inc=sv)
        L.set_guard(("pn", 0), sv, p.now(sv))
    w_lg = p.now(sv)
    p.add("scalar", lambda e: e.activation(out=ebuf[:], in_=lgbuf[:],
                                           func=AF.Exp),
          waits=((sv, w_lg),), inc=ss)
    w_e = p.now(ss)
    p.add("vector", lambda e: e.tensor_reduce(
        out=smb[:], in_=ebuf[:].rearrange("p (j n) -> p j n", j=16),
        axis=AX.X, op=ALU.add), waits=((ss, w_e),), inc=sv)
    w_sm = p.now(sv)
    p.add("vector", lambda e: e.reciprocal(out=rnb[:], in_=smb[:]),
          waits=((sv, w_sm),), inc=sv)
    w_rn = p.now(sv)
    for j in range(16):
        p.add("vector", lambda e, j=j: e.tensor_scalar_mul(
            out=ebuf[:, j * 8:(j + 1) * 8],
            in0=ebuf[:, j * 8:(j + 1) * 8], scalar1=rnb[:, j:j + 1]),
            waits=((sv, w_rn),) if j == 0 else (), inc=sv)
    w_pr = p.now(sv)
    p.add("vector", lambda e: e.tensor_reduce(
        out=m1b[:], in_=ebuf[:].rearrange("p (j n) -> p j n", j=16),
        axis=AX.X, op=ALU.max), waits=((sv, w_pr),), inc=sv)
    w_m1 = p.now(sv)
    for j in range(16):
        p.add("vector", lambda e, j=j: e.tensor_single_scalar(
            out=mskb[:], in_=ebuf[:, j * 8:(j + 1) * 8],
            scalar=m1b[:, j:j + 1], op=ALU.is_equal),
            waits=((sv, w_m1),) + L.guard("mskb"), inc=sv)
        w_mk = p.now(sv)
        p.add("vector", lambda e, j=j: e.tensor_mul(
            out=pmb[:, j * 8:(j + 1) * 8],
            in0=ebuf[:, j * 8:(j + 1) * 8], in1=mskb[:]),
            waits=((sv, w_mk),), inc=sv)
        w_pm = p.now(sv)
        L.set_guard("mskb", sv, w_pm)
        p.add("vector", lambda e, j=j: e.tensor_sub(
            out=p2b[:, j * 8:(j + 1) * 8],
            in0=ebuf[:, j * 8:(j + 1) * 8],
            in1=pmb[:, j * 8:(j + 1) * 8]),
            waits=((sv, w_pm),), inc=sv)
    w_p2 = p.now(sv)
    p.add("vector", lambda e: e.tensor_reduce(
        out=m2b[:], in_=p2b[:].rearrange("p (j n) -> p j n", j=16),
        axis=AX.X, op=ALU.max), waits=((sv, w_p2),), inc=sv)
    w_m2 = p.now(sv)
    for j in range(16):
        p.add("vector", lambda e, j=j: e.tensor_single_scalar(
            out=mskb[:], in_=p2b[:, j * 8:(j + 1) * 8],
            scalar=m2b[:, j:j + 1], op=ALU.is_equal),
            waits=((sv, w_m2),) + L.guard("mskb"), inc=sv)
        w_mk = p.now(sv)
        p.add("vector", lambda e, j=j: e.tensor_mul(
            out=p2b[:, j * 8:(j + 1) * 8],
            in0=p2b[:, j * 8:(j + 1) * 8], in1=mskb[:]),
            waits=((sv, w_mk),), inc=sv)
        L.set_guard("mskb", sv, p.now(sv))
    w_pm2 = p.now(sv)
    p.add("vector", lambda e: e.tensor_add(out=m1b[:], in0=m1b[:],
                                           in1=m2b[:]),
          waits=((sv, w_pm2),), inc=sv)
    w_dsum = p.now(sv)
    p.add("vector", lambda e: e.reciprocal(out=rnb[:], in_=m1b[:]),
          waits=((sv, w_dsum),), inc=sv)
    w_rn2 = p.now(sv)
    p.add("vector", lambda e: e.tensor_add(out=cwbuf[:], in0=pmb[:],
                                           in1=p2b[:]),
          waits=((sv, w_rn2),), inc=sv)
    w_cu = p.now(sv)
    for j in range(16):
        p.add("vector", lambda e, j=j: e.tensor_scalar_mul(
            out=cwbuf[:, j * 8:(j + 1) * 8],
            in0=cwbuf[:, j * 8:(j + 1) * 8], scalar1=rnb[:, j:j + 1]),
            waits=((sv, w_cu),) if j == 0 else (), inc=sv)
    w_cw0 = p.now(sv)
    p.add("vector", lambda e: e.tensor_copy(out=cwbf[:], in_=cwbuf[:]),
          waits=((sv, w_cw0),), inc=sv)
    w_cw = p.now(sv)
    for j in range(16):
        def tfn(e, j=j):
            return e.transpose(out=ptb[0:8, 0:128],
                               in_=cwbf[:, j * 8:(j + 1) * 8],
                               identity=idsb[:])
        p.add("tensor", tfn, waits=((sv, w_cw),) + L.guard("ptD"), inc=st)
        w_t = p.now(st)
        def cfn(e, j=j):
            return e.tensor_copy(out=cwT[:, j * 128:(j + 1) * 128],
                                 in_=ptb[0:8, 0:128])
        p.add("vector", cfn, waits=((st, w_t),), inc=sv)
        L.set_guard("ptD", sv, p.now(sv))
    cwT_rdy = p.now(sv)

    def tail_j(j, acc_wait):
        bx, bs, bg, bo, bt, by = (x1b, sstage, gbuf16, obuf, ytmp, ybuf)
        gk = 0
        def d1(e, j=j, bx=bx):
            return e.dma_start(out=bx[:],
                               in_=x1i[j * 128:(j + 1) * 128, :])
        p.add("sync", d1, waits=L.guard("x1b"), inc=dx, inc_by=16)
        def d2(e, j=j, bs=bs):
            return e.dma_start(out=bs[:],
                               in_=s_scr[j * 128:(j + 1) * 128, :])
        p.add("sync", d2, waits=L.guard("sstage"), inc=dx, inc_by=16)
        def d3(e, j=j, bg=bg):
            return e.dma_start(
                out=bg[:].rearrange("p (c n) -> p c n", c=8),
                in_=acc_scr[:, j * 128:(j + 1) * 128].rearrange(
                    "(c p) n -> p c n", p=128))
        p.add("sync", d3, waits=(acc_wait,) + L.guard("gbuf16"),
              inc=gx, inc_by=16)
        w_d = p.now(dx)
        w_g0 = p.now(gx)
        def gsil(e, bg=bg):
            return e.activation(out=bg[:], in_=bg[:], func=AF.Silu)
        p.add("scalar", gsil, waits=((gx, w_g0),), inc=ss)
        w_gs = p.now(ss)
        for ci in range(8):
            def tfn(e, ci=ci, bg=bg):
                return e.transpose(
                    out=pt0[:, 0:128],
                    in_=bg[:, ci * 128:(ci + 1) * 128],
                    identity=idsb[:])
            p.add("tensor", tfn,
                  waits=((ss, w_gs),) + L.guard("ptD"), inc=st)
            w_t = p.now(st)
            def cfn(e, ci=ci, bo=bo):
                return e.tensor_copy(out=bo[:, ci * 128:(ci + 1) * 128],
                                     in_=pt0[:, 0:128])
            p.add("vector", cfn,
                  waits=((st, w_t),) +
                  (L.guard("obuf") if ci == 0 else ()),
                  inc=sv)
            L.set_guard("ptD", sv, p.now(sv))
        w_g = p.now(sv)
        L.set_guard("gbuf16", st, p.now(st))
        p.add("vector", lambda e, bt=bt, bs=bs, bo=bo: e.tensor_mul(
            out=bt[:], in0=bs[:], in1=bo[:]),
              waits=((sv, w_g), (dx, w_d)) + L.guard("ytmp"), inc=sv)
        w_t1 = p.now(sv)
        L.set_guard("obuf", sv, w_t1)
        L.set_guard("sstage", sv, w_t1)
        p.add("vector", lambda e, by=by, bx=bx, bt=bt: e.tensor_add(
            out=by[:], in0=bx[:], in1=bt[:]),
              waits=((sv, w_t1),) + L.guard("ybuf"), inc=sv)
        w_t2 = p.now(sv)
        L.set_guard("ytmp", sv, w_t2)
        L.set_guard("x1b", sv, w_t2)
        def ydma(e, j=j, by=by):
            return e.dma_start(out=yout[j * 128:(j + 1) * 128, :],
                               in_=by[:])
        p.add("sync", ydma, waits=((sv, w_t2),), inc=so, inc_by=16)
        L.set_guard("ybuf", so, p.now(so))

    NCH = 11
    aw_acc = {}
    for e9 in range(9):
        if e9 > 0:
            for tc in range(4):
                def bcm(e, e9=e9, tc=tc):
                    return e.matmul(
                        out=pb[3][:],
                        lhsT=selsb[:, (e9 - 1) * 128:e9 * 128],
                        rhs=cwT[:, tc * 512:(tc + 1) * 512],
                        start=True, stop=True)
                p.add("tensor", bcm,
                      waits=((sv, cwT_rdy),) + L.guard(("pb", 3)), inc=st)
                w_b = p.now(st)
                def bcp(e, tc=tc):
                    return e.tensor_copy(
                        out=cwb[:, tc * 512:(tc + 1) * 512], in_=pb[3][:])
                p.add("vector", bcp,
                      waits=((st, w_b),) + L.guard("cwb"), inc=sv)
                L.set_guard(("pb", 3), sv, p.now(sv))
        cwb_rdy = p.now(sv)
        wdw0 = L.guard("wdsb")
        if e9 == 0:
            wdw0 = wdw0 + ((st, p.now(st)),)
        w_wd = None
        for tp in range(2):
            for ch in range(NCH):
                par = ch % 2
                wgv = wgbuf[:, par * 8 * 512:(par + 1) * 8 * 512]
                def wgdma(e, e9=e9, ch=ch, wgv=wgv):
                    return e.dma_start(
                        out=wgv.rearrange("p (c n) -> p c n", c=8),
                        in_=wgu[e9 * D:(e9 + 1) * D,
                                ch * 512:(ch + 1) * 512].rearrange(
                            "(c p) n -> p c n", p=128))
                p.add("gpsimd", wgdma,
                      waits=L.guard(("wgbuf", par)), inc=gx, inc_by=16)
                w_wg = p.now(gx)
                if tp == 0:
                    def wdp(e, e9=e9, k=ch):
                        return e.dma_start(
                            out=wdsb[:, k * 2 * D:(k + 1) * 2 * D].rearrange(
                                "p (c n) -> p c n", c=2),
                            in_=wd[e9 * 2816 + k * 256:
                                   e9 * 2816 + (k + 1) * 256, :].rearrange(
                                "(c p) n -> p c n", p=128))
                    p.add("gpsimd", wdp, waits=wdw0 if ch == 0 else (),
                          inc=gx, inc_by=16)
                    if ch == NCH - 1:
                        w_wd = p.now(gx)
                for f1 in range(4):
                    pdst = pq[f1 % 2]
                    for ci in range(8):
                        w = ()
                        if ci == 0:
                            w = ((gx, w_wg), (sv, xn1T_rdy)) + \
                                L.guard(("pq", f1 % 2))
                        for th in range(2):
                            def gmm(e, f1=f1, ci=ci, th=th, tp=tp,
                                    wgv=wgv, pdst=pdst):
                                return e.matmul(
                                    out=pdst[:, th * 512:(th + 1) * 512],
                                    lhsT=wgv[:, ci * 512 + f1 * 128:
                                             ci * 512 + (f1 + 1) * 128],
                                    rhs=xn1T[:, ci * TOK + tp * 1024 +
                                             th * 512:
                                             ci * TOK + tp * 1024 +
                                             (th + 1) * 512],
                                    start=(ci == 0), stop=(ci == 7))
                            p.add("tensor", gmm, waits=w if th == 0 else (),
                                  inc=st if (ci == 7 and th == 1) else None)
                            w = ()
                    w_mm = p.now(st)
                    if f1 < 2:
                        def sev(e, f1=f1, pdst=pdst):
                            return e.activation(
                                out=tgbuf[:, f1 * 1024:(f1 + 1) * 1024],
                                in_=pdst[:], func=AF.Silu)
                        p.add("scalar", sev,
                              waits=((st, w_mm),) +
                              (L.guard("tgbuf") if f1 == 0 else ()),
                              inc=ss)
                        L.set_guard(("pq", f1 % 2), ss, p.now(ss))
                    else:
                        hidx = 2 * ch + (f1 - 2)
                        def hev(e, f1=f1, hidx=hidx, pdst=pdst):
                            return e.tensor_mul(
                                out=hbuf[:, hidx * 1024:(hidx + 1) * 1024],
                                in0=tgbuf[:, (f1 - 2) * 1024:
                                          (f1 - 1) * 1024],
                                in1=pdst[:])
                        p.add("vector", hev,
                              waits=((st, w_mm), (ss, p.now(ss))) +
                              L.guard("hbuf"), inc=sv)
                        L.set_guard(("pq", f1 % 2), sv, p.now(sv))
                L.set_guard(("wgbuf", par), st, p.now(st))
                L.set_guard("tgbuf", sv, p.now(sv))
                if e9 == 8 and tp == 1 and 1 <= ch <= 8:
                    tail_j(ch - 1, aw_acc[0])
            w_h = p.now(sv)
            for oi in range(8):
                for c22 in range(22):
                    w = ()
                    if c22 == 0:
                        w = ((sv, w_h), (gx, w_wd)) + L.guard("pnb")
                    for th in range(2):
                        def dmm(e, oi=oi, c22=c22, th=th):
                            return e.matmul(
                                out=pnb[th][:],
                                lhsT=wdsb[:, c22 * D + oi * 128:
                                          c22 * D + (oi + 1) * 128],
                                rhs=hbuf[:, c22 * 1024 + th * 512:
                                         c22 * 1024 + (th + 1) * 512],
                                start=(c22 == 0), stop=(c22 == 21))
                        p.add("tensor", dmm, waits=w if th == 0 else (),
                              inc=st if (c22 == 21 and th == 1) else None)
                        w = ()
                w_mm = p.now(st)
                tpar = oi % 2
                tbv = tbm[:, tpar * 1024:(tpar + 1) * 1024]
                if e9 == 0:
                    def tev(e, tbv=tbv):
                        return e.tensor_copy(out=tbv, in_=pnb01[:])
                    p.add("vector", tev,
                          waits=((st, w_mm),) + L.guard(("tbm", tpar)),
                          inc=sv)
                else:
                    def tev(e, tbv=tbv, tp=tp):
                        return e.tensor_mul(
                            out=tbv, in0=pnb01[:],
                            in1=cwb[:, tp * 1024:(tp + 1) * 1024])
                    p.add("vector", tev,
                          waits=((st, w_mm), (sv, cwb_rdy)) +
                          L.guard(("tbm", tpar)), inc=sv)
                w_tb = p.now(sv)
                L.set_guard("pnb", sv, w_tb)
                def adma(e, oi=oi, tp=tp, e9=e9, tbv=tbv):
                    return e.dma_start(
                        out=acc_scr[oi * 128:(oi + 1) * 128,
                                    tp * 1024:(tp + 1) * 1024],
                        in_=tbv,
                        accum_op=(ALU.bypass if e9 == 0 else ALU.add))
                p.add("gpsimd", adma,
                      waits=((sv, w_tb),) + L.guard(("accr", oi, tp)),
                      inc=gx, inc_by=16)
                L.set_guard(("accr", oi, tp), gx, p.now(gx))
                L.set_guard(("tbm", tpar), gx, p.now(gx))
            if e9 == 8:
                aw_acc[tp] = (gx, p.now(gx))
                if tp == 1:
                    for j in range(8, 16):
                        tail_j(j, aw_acc[1])
        if e9 > 0:
            L.set_guard("cwb", gx, p.now(gx))
        L.set_guard("wdsb", st, p.now(st))

    p.add("sync", lambda e: e.nop(), waits=((so, p.now(so)),))


def build_launch_D():
    L = Launch()
    d = {}
    d["x1i"] = L.inp("x1i", [TOK, D])
    d["qft"] = L.inp("qft", [D, TOK])
    d["numx"] = L.inp("numx", [TOK, 16 * 65])
    d["texd"] = L.inp("texd", [128, 260])
    d["wo"] = L.inp("wo", [D, D])
    d["bo"] = L.inp("bo", [1, D])
    d["gam"] = L.inp("gam", [1, D])
    d["bet"] = L.inp("bet", [1, D])
    d["gw"] = L.inp("gw", [D, 8])
    d["selm"] = L.inp("selm", [8, 8 * 128], BF16)
    d["wgu"] = L.inp("wgu", [9 * D, 5632])
    d["wd"] = L.inp("wd", [9 * 2816, D])
    d["idn"] = L.inp("idn", [128, 128], BF16)
    d["yout"] = L.out("yout", [TOK, D])
    d["s_scr"] = L.scr("s_scr", [TOK, D], BF16)
    d["acc_scr"] = L.scr("acc_scr", [D, TOK], BF16)
    with L.es:
        phase_D(L, d)
        L.p.emit()
    return L.nc





def phase_B(L, d):
    nc, p = L.nc, L.p
    s0p0 = d["s0p0"]
    s0p1 = d["s0p1"]
    x_full = d["x_full"]
    bp = d["bp"]
    oddmask = d["oddmask"]
    wk = d["wk"]; bk = d["bk"]; wv = d["wv"]; bv = d["bv"]
    wc = d["wc"]; we = d["we"]
    w0t = d["w0t"]
    xnt_scr = d["xnt"]
    idn = d["idn"]
    x1own_scr = d["x1i"]
    base_scr = d["base_scr"]

    sb, ps = L.sb, L.ps
    pbf = sb("pbf", [128, 4 * D])
    pbs = sb("pbs", [128, 4 * D])
    s0bf = sb("s0bf", [128, 4 * D], BF16)
    s0bs = sb("s0bs", [128, 4 * D], BF16)
    gbf = sb("gbf", [128, 4 * D])
    gbs = sb("gbs", [128, 4 * D])
    sgnT = sb("sgnT", [128, 8 * D], BF16)
    xbf = sb("xbf", [128, 4 * D])
    xbs = sb("xbs", [128, 4 * D])
    bpb = sb("bpb", [128, D])
    bpsb = sb("bpsb", [1, D])
    msk = sb("msk", [128, 1])
    xn_bf = sb("xn_bf", [128, 4 * D], BF16)
    xnT = sb("xnT", [128, 8 * 512], BF16)
    xnTA = sb("xnTA", [128, 8 * 512], BF16)
    xnTB = sb("xnTB", [128, 8 * 512], BF16)
    wksb = sb("wksb", [128, 8 * 256], BF16)
    wvsb = sb("wvsb", [128, 8 * 256], BF16)
    wcsb = sb("wcsb", [128, 2 * 64], BF16)
    wesb = sb("wesb", [64, 256], BF16)
    bksb = sb("bksb", [128, 2])
    bvsb = sb("bvsb", [128, 2])
    idsb = sb("idsb", [128, 128], BF16)
    qtmp = sb("qtmp", [128, 512])
    t1 = sb("t1", [128, 512])
    t2 = sb("t2", [128, 512])
    t3 = sb("t3", [128, 512])
    kraw = sb("kraw", [128, 2 * 512], BF16)
    vraw = sb("vraw", [128, 2 * 512], BF16)
    ckT = sb("ckT", [64, 512], BF16)
    cvT = sb("cvT", [64, 512], BF16)
    kfT = sb("kfT", [128, 2 * 512], BF16)
    kftok = sb("kftok", [128, 4 * 256], BF16)
    v2e = sb("v2e", [128, 4 * 260], BF16)
    base32 = sb("base32", [64, 260])
    lsq = sb("lsq", [128, D])
    lsm = sb("lsm", [128, 32])
    consts = sb("consts", [128, 4])

    pb = [ps(f"pb{i}", [128, 512]) for i in range(4)]
    pSb = ps("pS", [128, 512])
    pS = pSb[0:64, 0:260]
    ptb = ps("pt0", [128, 2048], BF16)
    pt0 = ptb[:, 0:512]
    ptP = [ptb[:, 0:512], ptb[:, 1024:1536]]

    dx = L.sem("dx")
    gx = L.sem("gx")
    st = L.sem("st")
    sv = L.sem("sv")
    ss = L.sem("ss")
    so = L.sem("so")
    L.sv, L.ss = sv, ss
    eps5 = consts[:, 0:1]

    def ln_sc(i):
        return {"sq": lsq, "s": lsm[:, 6 * i:6 * i + 6]}

    p.add("vector", lambda e: e.memset(eps5, 1e-5), inc=sv)
    p.add("vector", lambda e: e.memset(v2e[:], 1.0), inc=sv)
    p.add("vector", lambda e: e.memset(base32[:], 0.0), inc=sv)
    p.add("vector", lambda e: e.memset(qtmp[0:1, 0:128], 1.0), inc=sv)
    w_ones = p.now(sv)
    p.add("sync", lambda e: e.dma_start(out=idsb[:], in_=idn), inc=dx, inc_by=16)
    p.add("sync", lambda e: e.dma_start(out=bpsb[:], in_=bp), inc=dx, inc_by=16)
    p.add("sync", lambda e: e.dma_start(out=msk[:], in_=oddmask), inc=dx, inc_by=16)
    for bsrc, bdst in ((bk, bksb), (bv, bvsb)):
        p.add("sync", lambda e, s=bsrc, dd=bdst: e.dma_start(out=dd[:], in_=s),
              inc=dx, inc_by=16)
    w_small = p.now(dx)
    p.add("gpsimd", lambda e: e.dma_start(
        out=wksb[:].rearrange("p (c n) -> p c n", c=8),
        in_=wk.rearrange("(c p) n -> p c n", p=128)), inc=gx, inc_by=16)
    p.add("gpsimd", lambda e: e.dma_start(
        out=wvsb[:].rearrange("p (c n) -> p c n", c=8),
        in_=wv.rearrange("(c p) n -> p c n", p=128)), inc=gx, inc_by=16)
    p.add("gpsimd", lambda e: e.dma_start(
        out=wcsb[:].rearrange("p (c n) -> p c n", c=2),
        in_=wc.rearrange("(c p) n -> p c n", p=128)), inc=gx, inc_by=16)
    p.add("gpsimd", lambda e: e.dma_start(out=wesb[:], in_=we), inc=gx, inc_by=16)
    p.add("gpsimd", lambda e: e.dma_start(
        out=sgnT[:].rearrange("p (c n) -> p c n", c=8),
        in_=w0t.rearrange("(c p) n -> p c n", p=128)), inc=gx, inc_by=16)
    w_wdma = p.now(gx)
    p.add("scalar", lambda e: e.activation(out=sgnT[:], in_=sgnT[:],
                                           func=AF.Sign),
          waits=((gx, w_wdma),), inc=ss)
    w_sgn = p.now(ss)
    for h2 in range(2):
        def bcm(e, h2=h2):
            return e.matmul(out=pb[h2][:], lhsT=qtmp[0:1, 0:128],
                            rhs=bpsb[0:1, h2 * 512:(h2 + 1) * 512],
                            start=True, stop=True)
        p.add("tensor", bcm, waits=((sv, w_ones), (dx, w_small)), inc=st)
        w_b = p.now(st)
        p.add("vector", lambda e, h2=h2: e.tensor_copy(
            out=bpb[:, h2 * 512:(h2 + 1) * 512], in_=pb[h2][:]),
            waits=((st, w_b),), inc=sv)
    bpb_rdy = p.now(sv)

    def ln_to_xnT(buf, dmawaits):
        tiles = [(buf[:, t * D:(t + 1) * D], xn_bf[:, t * D:(t + 1) * D],
                  ln_sc(t)) for t in range(4)]
        w_apply = emit_ln_tiles(L, tiles, waits=dmawaits, eps_ap=eps5)
        for ci in range(8):
            for t in range(4):
                def tfn(e, ci=ci, t=t):
                    return e.transpose(
                        out=pt0[:, t * 128:(t + 1) * 128],
                        in_=xn_bf[:, t * D + ci * 128:t * D + ci * 128 + 128],
                        identity=idsb[:])
                p.add("tensor", tfn,
                      waits=(((sv, w_apply), (dx, w_small)) + L.guard("ptB")
                             if t == 0 else ()),
                      inc=st if t == 3 else None)
            w_t = p.now(st)
            def cfn(e, ci=ci):
                return e.tensor_copy(out=xnT[:, ci * 512:(ci + 1) * 512],
                                     in_=pt0[:])
            p.add("vector", cfn, waits=((st, w_t),), inc=sv)
            L.set_guard("ptB", sv, p.now(sv))
        return w_apply, p.now(sv)

    def g0_block(gb, xn_wait, gkey, xsrc):
        for tt in range(4):
            for f2 in range(2):
                bk_ = (tt * 2 + f2) % 2
                for ci in range(8):
                    w = ()
                    if ci == 0:
                        w = (xn_wait, (ss, w_sgn)) + \
                            L.guard(("pb", bk_))
                    def mfn(e, tt=tt, f2=f2, ci=ci, bk_=bk_, xsrc=xsrc):
                        return e.matmul(
                            out=pb[bk_][:],
                            lhsT=xsrc[:, ci * 512 + tt * 128:
                                      ci * 512 + (tt + 1) * 128],
                            rhs=sgnT[:, ci * D + f2 * 512:
                                     ci * D + (f2 + 1) * 512],
                            start=(ci == 0), stop=(ci == 7))
                    p.add("tensor", mfn, waits=w, inc=st if ci == 7 else None)
                w_mm = p.now(st)
                dst = gb[:, tt * D + f2 * 512:tt * D + f2 * 512 + 512]
                def ev(e, dst=dst, bk_=bk_):
                    return e.activation(out=dst, in_=pb[bk_][:], func=AF.Silu)
                p.add("scalar", ev, waits=((st, w_mm),) + L.guard(gkey),
                      inc=ss)
                L.set_guard(("pb", bk_), ss, p.now(ss))
        return p.now(ss)

    for c in range(4):
        rf = slice(c * 512, (c + 1) * 512)
        rs = slice(2048 + c * 512, 2048 + (c + 1) * 512)
        def xldf(e, c=c):
            return e.dma_start(out=xnTA[:], in_=xnt_scr[c])
        p.add("gpsimd", xldf, waits=L.guard("xnT_rdA"), inc=gx, inc_by=16)
        w_ldf = p.now(gx)
        def xlds(e, c=c):
            return e.dma_start(out=xnTB[:], in_=xnt_scr[c + 4])
        p.add("gpsimd", xlds, waits=L.guard("xnT_rdB"), inc=gx, inc_by=16)
        w_lds = p.now(gx)
        for (dstb, r0) in ((xbf, rf), (xbs, rs)):
            def dxm(e, dstb=dstb, r0=r0):
                return e.dma_start(
                    out=dstb[:].rearrange("p (t n) -> p t n", t=4),
                    in_=x_full[r0, :].rearrange("(t p) n -> p t n", p=128))
            p.add("sync", dxm, waits=L.guard("pb_bufs"), inc=dx, inc_by=16)
        w_dx = p.now(dx)
        for (dstb, r0) in ((s0bf, rf), (s0bs, rs)):
            def dp1(e, dstb=dstb, r0=r0):
                return e.dma_start(
                    out=dstb[:].rearrange("p (t n) -> p t n", t=4),
                    in_=s0p0[r0, :].rearrange("(t p) n -> p t n", p=128))
            p.add("gpsimd", dp1, waits=L.guard("pb_bufs"), inc=gx, inc_by=16)
        w_d = p.now(gx)
        gf_done = g0_block(gbf, (gx, w_ldf), "gbf", xnTA)
        L.set_guard("xnT_rdA", st, p.now(st))
        gs_done = g0_block(gbs, (gx, w_lds), "gbs", xnTB)
        L.set_guard("xnT_rdB", st, p.now(st))
        for (buf, srcb) in ((pbf, s0bf), (pbs, s0bs)):
            for t in range(4):
                cs = slice(t * D, (t + 1) * D)
                p.add("vector", lambda e, buf=buf, srcb=srcb, cs=cs:
                      e.tensor_add(out=buf[:, cs], in0=srcb[:, cs],
                                   in1=bpb[:]),
                      waits=((gx, w_d), (sv, bpb_rdy)), inc=sv)
        w_v1 = p.now(sv)
        for (buf, gb, gw_) in ((pbf, gbf, gf_done), (pbs, gbs, gs_done)):
            for t in range(4):
                cs = slice(t * D, (t + 1) * D)
                p.add("vector", lambda e, buf=buf, gb=gb, cs=cs: e.tensor_mul(
                    out=buf[:, cs], in0=buf[:, cs], in1=gb[:, cs]),
                    waits=((sv, w_v1), (ss, gw_)), inc=sv)
        w_v2 = p.now(sv)
        L.set_guard("gbf", sv, w_v2)
        L.set_guard("gbs", sv, w_v2)
        for (buf, xb) in ((pbf, xbf), (pbs, xbs)):
            for t in range(4):
                cs = slice(t * D, (t + 1) * D)
                p.add("vector", lambda e, buf=buf, xb=xb, cs=cs: e.tensor_add(
                    out=xb[:, cs], in0=xb[:, cs], in1=buf[:, cs]),
                    waits=((sv, w_v2), (dx, w_dx)), inc=sv)
        w_x1 = p.now(sv)
        for t in range(4):
            cs = slice(t * D, (t + 1) * D)
            p.add("vector", lambda e, cs=cs: e.tensor_sub(
                out=pbf[:, cs], in0=xbs[:, cs], in1=xbf[:, cs]),
                waits=((sv, w_x1),), inc=sv)
        w_s1 = p.now(sv)
        for t in range(4):
            cs = slice(t * D, (t + 1) * D)
            p.add("vector", lambda e, cs=cs: e.scalar_tensor_tensor(
                out=pbs[:, cs], in0=pbf[:, cs], scalar=msk[:, 0:1],
                in1=xbf[:, cs], op0=ALU.mult, op1=ALU.add),
                waits=((sv, w_s1), (dx, w_small)), inc=sv)
        w_own = p.now(sv)
        def o3(e, c=c):
            return e.dma_start(
                out=x1own_scr[c * 512:(c + 1) * 512, :].rearrange(
                    "(t p) n -> p t n", p=128),
                in_=pbs[:].rearrange("p (t n) -> p t n", t=4))
        p.add("sync", o3, waits=((sv, w_own),), inc=so, inc_by=16)
        w_so = p.now(so)

        tiles = [(xbf[:, t * D:(t + 1) * D], xn_bf[:, t * D:(t + 1) * D],
                  ln_sc(t)) for t in range(4)]
        w_apply = emit_ln_tiles(L, tiles, waits=((sv, w_x1),), eps_ap=eps5)
        L.set_guard("pb_bufs", so, w_so)
        for ci in range(8):
            bkp = ci % 2
            for t in range(4):
                def tfn(e, ci=ci, t=t, bkp=bkp):
                    return e.transpose(
                        out=ptP[bkp][:, t * 128:(t + 1) * 128],
                        in_=xn_bf[:, t * D + ci * 128:t * D + ci * 128 + 128],
                        identity=idsb[:])
                p.add("tensor", tfn,
                      waits=(((sv, w_apply), (dx, w_small)) +
                             L.guard(("ptB", bkp))
                             if t == 0 else ()),
                      inc=st if t == 3 else None)
            w_t = p.now(st)
            def cfn(e, ci=ci, bkp=bkp):
                return e.tensor_copy(out=xnT[:, ci * 512:(ci + 1) * 512],
                                     in_=ptP[bkp][:])
            p.add("vector", cfn, waits=((st, w_t),), inc=sv)
            L.set_guard(("ptB", bkp), sv, p.now(sv))
        xnT_done = p.now(sv)
        for (wsb, bsb, dstraw) in ((wksb, bksb, kraw), (wvsb, bvsb, vraw)):
            for f1 in range(2):
                bk_ = 2 + f1
                for ci in range(8):
                    w = ()
                    if ci == 0:
                        w = ((sv, xnT_done), (gx, w_wdma)) +                             L.guard(("pb", bk_))
                    def mfn(e, f1=f1, ci=ci, bk_=bk_, wsb=wsb):
                        return e.matmul(
                            out=pb[bk_][:],
                            lhsT=wsb[:, ci * 256 + f1 * 128:
                                     ci * 256 + (f1 + 1) * 128],
                            rhs=xnT[:, ci * 512:(ci + 1) * 512],
                            start=(ci == 0), stop=(ci == 7))
                    p.add("tensor", mfn, waits=w, inc=st if ci == 7 else None)
                w_mm = p.now(st)
                def efn(e, f1=f1, bk_=bk_, bsb=bsb, dstraw=dstraw):
                    return e.activation(
                        out=dstraw[:, f1 * 512:(f1 + 1) * 512],
                        in_=pb[bk_][:], func=AF.Identity,
                        bias=bsb[:, f1:f1 + 1], scale=1.0)
                p.add("scalar", efn, waits=((st, w_mm), (dx, w_small)), inc=ss)
                L.set_guard(("pb", bk_), ss, p.now(ss))
        kv_done = p.now(ss)
        L.set_guard("xnT_rd", st, p.now(st))
        for (rawb, cdst) in ((kraw, ckT), (vraw, cvT)):
            for ci in range(2):
                def mfn(e, ci=ci, rawb=rawb):
                    return e.matmul(
                        out=pb[0][0:64, :],
                        lhsT=wcsb[:, ci * 64:(ci + 1) * 64],
                        rhs=rawb[:, ci * 512:(ci + 1) * 512],
                        start=(ci == 0), stop=(ci == 1))
                p.add("tensor", mfn,
                      waits=((ss, kv_done),) + L.guard(("pb", 0))
                      if ci == 0 else (), inc=st if ci == 1 else None)
            w_mm = p.now(st)
            def efn(e, cdst=cdst):
                return e.tensor_copy(out=cdst[:], in_=pb[0][0:64, :])
            p.add("vector", efn, waits=((st, w_mm),), inc=sv)
            L.set_guard(("pb", 0), sv, p.now(sv))
        c_done = p.now(sv)
        for f1 in range(2):
            def mfn(e, f1=f1):
                return e.matmul(out=pb[1][:],
                                lhsT=wesb[:, f1 * 128:(f1 + 1) * 128],
                                rhs=ckT[:], start=True, stop=True)
            p.add("tensor", mfn,
                  waits=((sv, c_done), (gx, w_wdma)) + L.guard(("pb", 1)),
                  inc=st)
            w_mm = p.now(st)
            def efn(e):
                return e.copy(out=qtmp[:], in_=pb[1][:])
            p.add("scalar", efn, waits=((st, w_mm),) + L.guard("qtmp"), inc=ss)
            w_e = p.now(ss)
            L.set_guard(("pb", 1), ss, w_e)
            p.add("vector", lambda e: e.tensor_scalar_min(
                out=t1[:], in0=qtmp[:], scalar1=0.0),
                waits=((ss, w_e),) + L.guard("t1"), inc=sv)
            p.add("vector", lambda e: e.tensor_scalar_max(
                out=t3[:], in0=qtmp[:], scalar1=0.0), inc=sv)
            w_mn = p.now(sv)
            L.set_guard("qtmp", sv, w_mn)
            p.add("scalar", lambda e: e.activation(
                out=t2[:], in_=t1[:], func=AF.Exp),
                waits=((sv, w_mn),) + L.guard("t2"), inc=ss)
            w_ex = p.now(ss)
            L.set_guard("t1", ss, w_ex)
            dst = kfT[:, f1 * 512:(f1 + 1) * 512]
            p.add("vector", lambda e, dst=dst: e.tensor_add(
                out=dst, in0=t2[:], in1=t3[:]),
                waits=((ss, w_ex), (sv, w_mn)), inc=sv)
            L.set_guard("t2", sv, p.now(sv))
            L.set_guard("t3", sv, p.now(sv))
        kf_done = p.now(sv)
        for t in range(4):
            def mfn(e, t=t):
                return e.matmul(out=pb[2][:, 0:256],
                                lhsT=cvT[:, t * 128:(t + 1) * 128],
                                rhs=wesb[:], start=True, stop=True)
            p.add("tensor", mfn,
                  waits=((sv, c_done), (gx, w_wdma)) + L.guard(("pb", 2)),
                  inc=st)
            w_mm = p.now(st)
            for g in range(4):
                dst = v2e[:, t * 260 + g * 65:t * 260 + g * 65 + 64]
                def efn(e, dst=dst, g=g):
                    return e.tensor_copy(out=dst,
                                         in_=pb[2][:, g * 64:(g + 1) * 64])
                p.add("vector", efn, waits=((st, w_mm),),
                      inc=sv if g == 3 else None)
            L.set_guard(("pb", 2), sv, p.now(sv))
        v2_done = p.now(sv)
        for ci in range(2):
            for t in range(4):
                def tfn(e, ci=ci, t=t):
                    return e.transpose(
                        out=pt0[:, 0:128],
                        in_=kfT[:, ci * 512 + t * 128:ci * 512 + (t + 1) * 128],
                        identity=idsb[:])
                p.add("tensor", tfn,
                      waits=((sv, kf_done),) + L.guard("ptB"), inc=st)
                w_t = p.now(st)
                dst = kftok[:, t * 256 + ci * 128:t * 256 + (ci + 1) * 128]
                def cfn(e, dst=dst):
                    return e.tensor_copy(out=dst, in_=pt0[:, 0:128])
                p.add("vector", cfn, waits=((st, w_t),), inc=sv)
                L.set_guard("ptB", sv, p.now(sv))
        kftok_done = p.now(sv)
        for t in range(4):
            for g in range(4):
                def smm(e, t=t, g=g):
                    return e.matmul(
                        out=pS[:, g * 65:(g + 1) * 65],
                        lhsT=kftok[:, t * 256 + g * 64:t * 256 + (g + 1) * 64],
                        rhs=v2e[:, t * 260 + g * 65:t * 260 + (g + 1) * 65],
                        start=(g == 0), stop=(g == 3))
                p.add("tensor", smm,
                      waits=((sv, max(kftok_done, v2_done)),) +
                      (L.guard("pS") if g == 0 else ()),
                      inc=st if g == 3 else None)
            w_s = p.now(st)
            p.add("vector", lambda e: e.tensor_add(
                out=base32[:], in0=base32[:], in1=pS[:]),
                waits=((st, w_s),), inc=sv)
            L.set_guard("pS", sv, p.now(sv))
        L.set_guard("pb_bufs", sv, p.now(sv))

    p.add("vector", lambda e: e.tensor_scalar_mul(
        out=base32[:], in0=base32[:], scalar1=msk[0:64, 0:1]),
        waits=((sv, p.now(sv)),), inc=sv)
    w_bm = p.now(sv)
    def bdma(e):
        return e.dma_start(out=base_scr, in_=base32[:])
    p.add("sync", bdma, waits=((sv, w_bm),), inc=so, inc_by=16)
    p.add("sync", lambda e: e.nop(), waits=((so, p.now(so)),))



PAIRS = [[0, 1], [2, 3], [4, 5], [6, 7]]


def build_fused():
    L = Launch()
    p = L.p
    d = {}
    d["x_full"] = L.inp("x_full", [S, D])
    d["w1p"] = L.inp("w1p", [D, 2 * 1536])
    d["b1p"] = L.inp("b1p", [128, 24])
    d["w0t"] = L.inp("w0t", [D, D])
    d["wpmy"] = L.inp("wpmy", [D, D])
    d["idn"] = L.inp("idn", [128, 128], BF16)
    d["dmask"] = L.inp("dmask", [4 * 128, 1024], BF16)
    d["bp"] = L.inp("bp", [1, D])
    d["wq"] = L.inp("wq", [D, D])
    d["bq"] = L.inp("bq", [128, 8])
    d["wk"] = L.inp("wk", [D, 256])
    d["bk"] = L.inp("bk", [128, 2])
    d["wv"] = L.inp("wv", [D, 256])
    d["bv"] = L.inp("bv", [128, 2])
    d["wc"] = L.inp("wc", [256, 64])
    d["we"] = L.inp("we", [64, 256])
    d["trimask"] = L.inp("trimask", [128, 128], BF16)
    d["duplo"] = L.inp("duplo", [128, 128], BF16)
    d["duphi"] = L.inp("duphi", [128, 128], BF16)
    d["wo"] = L.inp("wo", [D, D])
    d["bo"] = L.inp("bo", [1, D])
    d["gam"] = L.inp("gam", [1, D])
    d["bet"] = L.inp("bet", [1, D])
    d["gw"] = L.inp("gw", [D, 8])
    d["selm"] = L.inp("selm", [8, 8 * 128], BF16)
    d["wgu"] = L.inp("wgu", [9 * D, 5632])
    d["wd"] = L.inp("wd", [9 * 2816, D])
    d["oddmask"] = L.inp("oddmask", [128, 1])
    d["texd"] = L.inp("texd", [128, 260])
    d["yout"] = L.out("yout", [TOK, D])
    s0p0 = L.scr("s0p0_scr", [S, D], BF16)
    s0p1 = L.scr("s0p1_scr", [S, D], BF16)
    d["xnt"] = L.scr("xnt_scr", [8, 128, 8 * 512], BF16)
    d["g0"] = L.scr("g0_scr", [S, D])
    d["x1i"] = L.scr("x1_own_scr", [TOK, D])
    d["x1_own_src"] = d["x1i"]
    d["base_scr"] = L.scr("base_scr", [64, 260])
    d["qfo"] = L.scr("qf_scr", [D, TOK], BF16)
    d["qft"] = d["qfo"]
    d["numx"] = L.scr("numx_scr", [TOK, 16 * 65])
    d["tex"] = L.scr("tex_scr", [256, 65])
    d["s_scr"] = L.scr("s_scr", [TOK, D], BF16)
    d["acc_scr"] = L.scr("acc_scr", [D, TOK], BF16)

    with L.es:
        sbar = p.sem("sbar")
        for g in range(2):
            dg = dict(d)
            dg["w1p"] = d["w1p"][:, g * 1536:(g + 1) * 1536]
            dg["b1p"] = d["b1p"][:, g * 12:(g + 1) * 12]
            dg["wpmy"] = d["wpmy"][g * 512:(g + 1) * 512, :]
            dg["s0p"] = s0p0
            dg["s0p_accum"] = (g == 1)
            dg["x_own"] = d["x_full"]
            dg["skip_tail"] = True
            if g == 0:
                dg["save_xnt"] = d["xnt"]
            else:
                dg["load_xnt"] = d["xnt"]
            L.sfx = f"_a{g}"
            with contextlib.ExitStack() as sc:
                L.scope = sc
                phase_A(L, dg)
            L.guards.clear()
            p.barrier(sbar)
        dB = dict(d)
        dB["s0p0"] = s0p0
        dB["s0p1"] = s0p1
        L.sfx = "_b"
        with contextlib.ExitStack() as sc:
            L.scope = sc
            phase_B(L, dB)
        L.guards.clear()
        p.barrier(sbar)
        L.sfx = "_c"
        with contextlib.ExitStack() as sc:
            L.scope = sc
            phase_C(L, d)
        L.guards.clear()
        p.barrier(sbar)
        L.sfx = "_d"
        with contextlib.ExitStack() as sc:
            L.scope = sc
            phase_D(L, d)
        p.emit()
    return L.nc


def _fused_inputs(inputs):
    x = np.asarray(inputs["L0_attn_x"])
    W1 = np.asarray(inputs["L0_attn_W1"])
    b1 = np.asarray(inputs["L0_attn_b1"])
    Wp = np.asarray(inputs["L0_attn_Wp"])
    W0 = np.asarray(inputs["L0_ffn_weight"])
    perm = np.concatenate([_perm_w1_cols(0), _perm_w1_cols(1)])
    w1p = np.ascontiguousarray(W1[:, perm])
    b1p = np.ascontiguousarray(b1[perm].reshape(24, 128).T)
    bp = np.asarray(inputs["L0_attn_bp"]).reshape(1, D)
    tri = _tri_mask()
    lo, hi = _dup_mats()
    dm = _diag_masks()
    Wo = np.asarray(inputs["L1_attn_Wo"])
    bo = np.asarray(inputs["L1_attn_bo"]).reshape(1, D)
    gam = np.asarray(inputs["L1_attn_gamma"]).reshape(1, D)
    bet = np.asarray(inputs["L1_attn_beta"]).reshape(1, D)
    gw = np.asarray(inputs["L1_ffn_gate_w"])
    wperm = _wgu_perm()
    wgu_all = np.concatenate(
        [np.asarray(inputs["L1_ffn_base_wgu"])[None]] +
        [np.asarray(inputs["L1_ffn_exp_wgu"])], axis=0)[:, :, wperm]
    wgu_all = np.ascontiguousarray(wgu_all.reshape(9 * D, 5632))
    wd_all = np.concatenate(
        [np.asarray(inputs["L1_ffn_base_wd"])[None]] +
        [np.asarray(inputs["L1_ffn_exp_wd"])], axis=0)
    wd_all = np.ascontiguousarray(wd_all.reshape(9 * 2816, D))
    sel = _sel_mat()
    zeros_tex = np.zeros((128, 260), np.float32)
    in_maps = []
    for core in range(NCORE):
        b, h = core // 2, core % 2
        in_maps.append({
            "x_full": np.ascontiguousarray(x[b]),
            "w1p": w1p, "b1p": b1p,
            "w0t": np.ascontiguousarray(W0.T),
            "wpmy": Wp,
            "idn": _IDN, "dmask": dm,
            "bp": np.ascontiguousarray(bp),
            "wq": np.asarray(inputs["L1_attn_Wq"]),
            "bq": np.ascontiguousarray(
                np.asarray(inputs["L1_attn_bq"]).reshape(8, 128).T),
            "wk": np.asarray(inputs["L1_attn_Wk"]),
            "bk": np.ascontiguousarray(
                np.asarray(inputs["L1_attn_bk"]).reshape(2, 128).T),
            "wv": np.asarray(inputs["L1_attn_Wv"]),
            "bv": np.ascontiguousarray(
                np.asarray(inputs["L1_attn_bv"]).reshape(2, 128).T),
            "wc": np.asarray(inputs["L1_attn_Wc"]),
            "we": np.asarray(inputs["L1_attn_We"]),
            "trimask": tri, "duplo": lo, "duphi": hi,
            "wo": Wo, "bo": np.ascontiguousarray(bo),
            "gam": np.ascontiguousarray(gam),
            "bet": np.ascontiguousarray(bet),
            "gw": gw, "selm": sel,
            "wgu": wgu_all, "wd": wd_all,
            "oddmask": np.full((128, 1), float(h), np.float32),
            "texd": zeros_tex,
        })
    return in_maps



def _perm_w1_cols(half):
    heads = range(half * 8, half * 8 + 8)
    cols = []
    for part in range(3):
        for hh in heads:
            base = hh * 192 + part * 64
            cols.extend(range(base, base + 64))
    return np.array(cols)


def _diag_masks():
    j = np.arange(4)[:, None, None]
    k = np.arange(128)[None, :, None]
    q = np.arange(512)[None, None, :]
    m = (128 * j + k <= q).astype(np.float32)
    m = np.concatenate([m, m], axis=-1)
    return np.ascontiguousarray(m.reshape(4 * 128, 1024).astype(BF))


_IDN = np.eye(128, dtype=np.float32).astype(BF)


def _launch_A_inputs(inputs):
    x = np.asarray(inputs["L0_attn_x"])
    W1 = np.asarray(inputs["L0_attn_W1"])
    b1 = np.asarray(inputs["L0_attn_b1"])
    Wp = np.asarray(inputs["L0_attn_Wp"])
    W0 = np.asarray(inputs["L0_ffn_weight"])
    dm = _diag_masks()
    in_maps = []
    for core in range(NCORE):
        b, h = core // 2, core % 2
        perm = _perm_w1_cols(h)
        w1p = np.ascontiguousarray(W1[:, perm])
        b1p = np.ascontiguousarray(b1[perm].reshape(12, 128).T)
        wpmy = np.ascontiguousarray(Wp[h * 512:(h + 1) * 512, :])
        in_maps.append({
            "x_full": np.ascontiguousarray(x[b]),
            "x_own": np.ascontiguousarray(x[b, h * TOK:(h + 1) * TOK]),
            "w1p": w1p, "b1p": b1p,
            "w0t": np.ascontiguousarray(W0.T), "wpmy": wpmy,
            "idn": _IDN, "dmask": dm,
        })
    return in_maps


def _tri_mask():
    j = np.arange(128)[:, None]
    i = np.arange(128)[None, :]
    return np.ascontiguousarray((j <= i).astype(np.float32).astype(BF))


def _dup_mats():
    c = np.arange(128)[:, None]
    f = np.arange(128)[None, :]
    lo = ((c < 64) & (f % 64 == c)).astype(np.float32).astype(BF)
    hi = ((c >= 64) & (f % 64 == c - 64)).astype(np.float32).astype(BF)
    return np.ascontiguousarray(lo), np.ascontiguousarray(hi)


def _launch_C_inputs(inputs, resA):
    x = np.asarray(inputs["L0_attn_x"])
    bp = np.asarray(inputs["L0_attn_bp"]).reshape(1, D)
    Wq = np.asarray(inputs["L1_attn_Wq"])
    bq = np.asarray(inputs["L1_attn_bq"]).reshape(8, 128).T
    Wk = np.asarray(inputs["L1_attn_Wk"])
    bk = np.asarray(inputs["L1_attn_bk"]).reshape(2, 128).T
    Wv = np.asarray(inputs["L1_attn_Wv"])
    bv = np.asarray(inputs["L1_attn_bv"]).reshape(2, 128).T
    Wc = np.asarray(inputs["L1_attn_Wc"])
    We = np.asarray(inputs["L1_attn_We"])
    tri = _tri_mask()
    lo, hi = _dup_mats()
    in_maps = []
    for core in range(NCORE):
        b, h = core // 2, core % 2
        rows = slice(h * TOK, (h + 1) * TOK)
        partner = 2 * b + (1 - h)
        in_maps.append({
            "p_self": np.ascontiguousarray(np.asarray(resA[core]["s0p"])[rows]),
            "p_part": np.ascontiguousarray(np.asarray(resA[partner]["s0p"])[rows]),
            "g0": np.asarray(resA[core]["g0"]),
            "x_own": np.ascontiguousarray(x[b, rows]),
            "bp": np.ascontiguousarray(bp),
            "wq": Wq, "bq": np.ascontiguousarray(bq),
            "wk": Wk, "bk": np.ascontiguousarray(bk),
            "wv": Wv, "bv": np.ascontiguousarray(bv),
            "wc": Wc, "we": We,
            "idn": _IDN, "trimask": tri, "duplo": lo, "duphi": hi,
        })
    return in_maps


def _sel_mat():
    c = np.arange(8)[:, None]
    ep = np.arange(8 * 128)[None, :] // 128
    return np.ascontiguousarray((c == ep).astype(np.float32).astype(BF))


def _wgu_perm():
    idx = []
    for ch in range(11):
        idx.extend(range(ch * 256, (ch + 1) * 256))
        idx.extend(range(2816 + ch * 256, 2816 + (ch + 1) * 256))
    return np.array(idx)


def _launch_D_inputs(inputs, resC):
    Wo = np.asarray(inputs["L1_attn_Wo"])
    bo = np.asarray(inputs["L1_attn_bo"]).reshape(1, D)
    gam = np.asarray(inputs["L1_attn_gamma"]).reshape(1, D)
    bet = np.asarray(inputs["L1_attn_beta"]).reshape(1, D)
    gw = np.asarray(inputs["L1_ffn_gate_w"])
    perm = _wgu_perm()
    wgu_all = np.concatenate(
        [np.asarray(inputs["L1_ffn_base_wgu"])[None]] +
        [np.asarray(inputs["L1_ffn_exp_wgu"])], axis=0)[:, :, perm]
    wgu_all = np.ascontiguousarray(wgu_all.reshape(9 * D, 5632))
    wd_all = np.concatenate(
        [np.asarray(inputs["L1_ffn_base_wd"])[None]] +
        [np.asarray(inputs["L1_ffn_exp_wd"])], axis=0)
    wd_all = np.ascontiguousarray(wd_all.reshape(9 * 2816, D))
    sel = _sel_mat()
    in_maps = []
    for core in range(NCORE):
        b, h = core // 2, core % 2
        partner = 2 * b + (1 - h)
        if h == 0:
            texd = np.zeros((128, 260), np.float32)
        else:
            t = np.asarray(resC[partner]["tex"]).reshape(4, 64, 65)
            t = np.ascontiguousarray(t.transpose(1, 0, 2).reshape(64, 260))
            texd = np.concatenate([t, t], axis=0)
        in_maps.append({
            "x1i": np.asarray(resC[core]["x1o"]),
            "qft": np.asarray(resC[core]["qfo"]),
            "numx": np.asarray(resC[core]["numx"]),
            "texd": np.ascontiguousarray(texd),
            "wo": Wo, "bo": np.ascontiguousarray(bo),
            "gam": np.ascontiguousarray(gam),
            "bet": np.ascontiguousarray(bet),
            "gw": gw, "selm": sel,
            "wgu": wgu_all, "wd": wd_all,
            "idn": _IDN,
        })
    return in_maps


_NC_CACHE = {}


def _get_nc(name, builder):
    if name not in _NC_CACHE:
        _NC_CACHE[name] = builder()
    return _NC_CACHE[name]


def kernel(**inputs):
    res = run_bass_kernel_spmd(_get_nc("F", build_fused),
                               _fused_inputs(inputs),
                               core_ids=list(range(NCORE))).results
    out = np.zeros((NSEQ, S, D), np.float32)
    for core in range(NCORE):
        b, h = core // 2, core % 2
        out[b, h * TOK:(h + 1) * TOK] = np.asarray(res[core]["yout"])
    return out
